# version 29
# baseline (speedup 1.0000x reference)
import contextlib
import math
import numpy as np
import ml_dtypes
import concourse.bass as bass
import concourse.mybir as mybir
from concourse.bass_utils import run_bass_kernel_spmd

F32 = mybir.dt.float32
BF16 = mybir.dt.bfloat16
AF = mybir.ActivationFunctionType
ALU = mybir.AluOpType
AX = mybir.AxisListType

P = 128
D = 1024
NB = 4
S = 8192
TOWN = 4096
H = 1152
OH = H + TOWN
NC = 8
C_A = float(np.float32(1.0986122886681098) / np.float32(32.0))
HID = 2730
W1C = 5460
INNER = 512
DHEAD = 64
NHEAD = 8
WIN = 512

N_PHASES = 5


_TBL = {}


def _tables():
    if _TBL:
        return _TBL
    import jax
    import jax.numpy as jnp
    cpu = jax.devices("cpu")[0]
    with jax.default_device(cpu):
        _tables_jax(jnp)
    return _TBL


def _tables_jax(jnp):
    import jax
    half = D // 2
    inv_freq = 1.0 / 10000.0 ** (jnp.arange(half, dtype=jnp.float32) / half)
    ang = jnp.arange(S, dtype=jnp.float32)[:, None] * inv_freq[None, :]
    cos = jnp.round(jnp.cos(ang))
    sin = jnp.round(jnp.sin(ang))
    C = jnp.concatenate([cos, cos], axis=-1)
    Sg = jnp.concatenate([sin, sin], axis=-1)
    Sf = np.asarray(Sg, np.float32).copy()
    Sf[:, :half] = -Sf[:, :half]
    rc = np.asarray(C, np.float32).T.astype(ml_dtypes.bfloat16)
    rs = Sf.T.astype(ml_dtypes.bfloat16)
    _TBL["rc"] = np.ascontiguousarray(rc)
    _TBL["rs"] = np.ascontiguousarray(rs)

    n = S
    half_idx = jnp.arange(0, DHEAD, 2, dtype=jnp.float32)
    inv_f = 1.0 / 10000.0 ** (half_idx / DHEAD)
    t_ext = jnp.arange(-H, n, dtype=jnp.float32)
    freqs = t_ext[:, None] * inv_f[None, :]
    freqs = jnp.concatenate([freqs, freqs], axis=-1)
    base = (half_idx + 0.4 * DHEAD) / (1.4 * DHEAD)
    power = (t_ext - n // 2) / float(WIN // 2)
    scale = base[None, :] ** power[:, None]
    scale = jnp.concatenate([scale, scale], axis=-1)
    cosx = jnp.cos(freqs)
    sinx = jnp.sin(freqs)
    qc = np.array(cosx * scale, np.float32)
    qs = np.array(sinx * scale, np.float32)
    kc = np.array(cosx / scale, np.float32)
    ks = np.array(sinx / scale, np.float32)
    for t in (qs, ks):
        t[:, :32] = -t[:, :32]
    def fm(t):
        tt = t.T
        return np.ascontiguousarray(np.concatenate([tt, tt], axis=0).astype(np.float32))
    _TBL["xqc"] = fm(qc)
    _TBL["xqs"] = fm(qs)
    _TBL["xkc"] = fm(kc)
    _TBL["xks"] = fm(ks)

    def band_mask(tbase):
        m = np.zeros((128, 640), np.float32)
        pp = np.arange(128)[:, None]
        cc = np.arange(640)[None, :]
        valid = (cc >= pp) & (cc <= pp + 512)
        if tbase < 512:
            valid &= (tbase - 512 + cc) >= 0
        m[~valid] = -1e30
        return m
    _TBL["mask_plain"] = band_mask(10_000)
    _TBL["mask_first"] = np.stack([band_mask(i * 128) for i in range(4)])

    pm = np.zeros((128, 128), np.float32)
    for kk in range(128):
        pm[kk, kk ^ 32] = 1.0
    _TBL["perm32"] = pm.astype(ml_dtypes.bfloat16)

    seg = np.zeros((128, 2), np.float32)
    seg[:64, 0] = 1.0
    seg[64:, 1] = 1.0
    _TBL["segbf"] = seg.astype(ml_dtypes.bfloat16)
    _TBL["segT"] = np.ascontiguousarray(seg.T)


class Ctr:
    def __init__(self):
        self.v = {}

    def add(self, sem, n):
        self.v[sem.name] = self.v.get(sem.name, 0) + n
        return self.v[sem.name]

    def cur(self, sem):
        return self.v.get(sem.name, 0)

    def snap(self):
        return dict(self.v)


class Env:
    pass


def _mk_env(nc, es):
    env = Env()
    env.nc = nc
    env.es = es
    sem = lambda n: es.enter_context(nc.semaphore(n))
    env.sD = sem("sD")
    env.sG = sem("sG")
    env.sT = sem("sT")
    env.sV = sem("sV")
    env.sA = sem("sA")
    env.c = Ctr()
    return env


def _fence(env, eng, sem, instr):
    v = env.c.add(sem, 1)
    instr.then_inc(sem, 1)
    eng.wait_ge(sem, v)


def _phase_barrier(env, blk, snap):
    for e in ("sync", "gpsimd", "tensor", "vector", "scalar"):
        rec = blk.recs[e]
        for s in (env.sD, env.sG, env.sT, env.sV, env.sA):
            v = snap.get(s.name, 0)
            if v:
                rec.calls.append((False, None, "wait_ge", (s, v), {}))


def _barrier_waits(env, eng, snap):
    for s in (env.sD, env.sG, env.sT, env.sV, env.sA):
        v = snap.get(s.name, 0)
        if v:
            eng.wait_ge(s, v)


class _ChainProxy:
    def __init__(self, rec, idx):
        self.rec = rec
        self.idx = idx

    def then_inc(self, sem, n):
        self.rec.calls.append((True, self.idx, "then_inc", (sem, n), {}))
        return self


class _Rec:
    def __init__(self):
        self.calls = []

    def __getattr__(self, name):
        rec = self

        def m(*a, **k):
            rec.calls.append((False, None, name, a, k))
            return _ChainProxy(rec, len(rec.calls) - 1)
        return m

    def replay(self, eng):
        res = []
        for chain, idx, name, a, k in self.calls:
            if chain:
                res.append(res[idx].then_inc(*a))
            else:
                res.append(getattr(eng, name)(*a, **k))


class DeferBlk:
    _ENGS = ("sync", "gpsimd", "tensor", "vector", "scalar")

    def __init__(self):
        self.recs = {e: _Rec() for e in self._ENGS}

    def _dec(self, e):
        def deco(f):
            f(self.recs[e])
        return deco

    @property
    def sync(self):
        return self._dec("sync")

    @property
    def gpsimd(self):
        return self._dec("gpsimd")

    @property
    def tensor(self):
        return self._dec("tensor")

    @property
    def vector(self):
        return self._dec("vector")

    @property
    def scalar(self):
        return self._dec("scalar")

    def flush(self, blk):
        for e in self._ENGS:
            rec = self.recs[e]
            if rec.calls:
                getattr(blk, e)(lambda eng, r=rec: r.replay(eng))


class _PhaseCtx:
    def __init__(self, env):
        self.env = env

    def __enter__(self):
        return self.env.dblk

    def __exit__(self, et, ev, tb):
        return False


class _DeferCtx:
    def __init__(self, nc, name, **kw):
        self.nc = nc
        self.name = name
        self.kw = kw
        self.d = DeferBlk()

    def __enter__(self):
        return self.d

    def __exit__(self, et, ev, tb):
        if et is None:
            with self.nc.Block(self.name, **self.kw) as b:
                self.d.flush(b)
        return False


def build_program(taps=()):
    nc = bass.Bass("TRN2", target_bir_lowering=False)
    es = contextlib.ExitStack()
    env = _mk_env(nc, es)
    env.dblk = DeferBlk()
    env.taps = set(taps)

    di = lambda n, s, d=F32: nc.dram_tensor(n, s, d, kind="ExternalInput").ap()
    I = {}
    I["x_oh"] = di("x_oh", [OH, D])
    I["x_oth"] = di("x_oth", [TOWN, D])
    I["gnorm"] = di("gnorm", [P, 8])
    I["wqkvA"] = di("wqkvA", [D, 3 * D])
    I["bqk_t"] = di("bqk_t", [P, 16])
    I["bqv_row"] = di("bqv_row", [1, D])
    I["wprojA"] = di("wprojA", [D, D])
    I["bprojA"] = di("bprojA", [P, 8])
    I["rc_own"] = di("rc_own", [D, TOWN], BF16)
    I["rs_own"] = di("rs_own", [D, TOWN], BF16)
    I["rc_oth"] = di("rc_oth", [D, TOWN], BF16)
    I["rs_oth"] = di("rs_oth", [D, TOWN], BF16)
    I["xqc"] = di("xqc", [P, OH])
    I["xqs"] = di("xqs", [P, OH])
    I["xkc"] = di("xkc", [P, OH])
    I["xks"] = di("xks", [P, OH])
    I["lmask"] = di("lmask", [5, P, 640])
    I["segbf"] = di("segbf", [P, 2], BF16)
    I["perm32"] = di("perm32", [P, P], BF16)
    I["segT"] = di("segT", [2, P])
    I["gate"] = di("gate", [P, 1])
    I["wqkv2"] = di("wqkv2", [2, D, 3 * INNER])
    I["qsc"] = di("qsc", [P, 2])
    I["ksc"] = di("ksc", [P, 2])
    I["wo2"] = di("wo2", [2, INNER, D])
    I["lng"] = di("lng", [P, 2, 8])
    I["lnb"] = di("lnb", [P, 2, 8])
    I["w12"] = di("w12", [2, D, W1C])
    I["b12"] = di("b12", [P, 2, 44])
    I["w22"] = di("w22", [2, HID, D])
    I["b22"] = di("b22", [P, 2, 8])
    env.I = I

    out_o = nc.dram_tensor("out_o", [TOWN, D], F32, kind="ExternalOutput").ap()

    def scratch(name, shape, dt):
        kind = "ExternalOutput" if name in env.taps else "Internal"
        return nc.dram_tensor(name, shape, dt, kind=kind).ap()

    Z = {}
    Z["xnF"] = scratch("xnF", [D, OH], F32)
    Z["yA"] = scratch("yA", [D, OH], F32)
    Z["yB"] = scratch("yB", [D, OH], F32)
    Z["qfA"] = scratch("qfA", [D, TOWN], BF16)
    Z["kfA"] = scratch("kfA", [D, S], BF16)
    Z["vtA"] = scratch("vtA", [S, D], BF16)
    Z["sF"] = scratch("sF", [D, TOWN], F32)
    Z["wqkvA_bf"] = scratch("wqkvA_bf", [D, 3 * D], BF16)
    Z["wprojA_bf"] = scratch("wprojA_bf", [D, D], BF16)
    Z["wqkv2_bf"] = scratch("wqkv2_bf", [2, D, 3 * INNER], BF16)
    Z["wo2_bf"] = scratch("wo2_bf", [2, INNER, D], BF16)
    Z["w12_bf"] = scratch("w12_bf", [2, D, W1C], BF16)
    Z["w22_bf"] = scratch("w22_bf", [2, HID, D], BF16)
    env.Z = Z
    env.out_o = out_o

    sb = lambda n, s, d: es.enter_context(nc.sbuf_tensor(n, s, d))
    pt = Env()
    pt.identb = sb("identb", [P, P], BF16)
    pt.ident3 = sb("ident3", [P, P], F32)
    pt.ones1r = sb("ones1r", [1, P], F32)
    pt.onesc = sb("onesc", [P, 1], F32)
    pt.onesb = sb("onesb", [P, 1], BF16)
    pt.gn = sb("gn", [P, 8], F32)
    pt.bqkv = sb("bqkv", [P, 16], F32)
    pt.bvb = sb("bvb", [P, D], F32)
    pt.bproj = sb("bproj", [P, 8], F32)
    pt.segb = sb("segb_sb", [P, 2], BF16)
    pt.perm32 = sb("perm32_sb", [P, P], BF16)
    pt.segT = sb("segT_sb", [2, P], F32)
    pt.gate = sb("gate_sb", [P, 1], F32)
    pt.qs8 = sb("qs8", [P, 2], F32)
    pt.ksc = sb("kscs", [P, 2], F32)
    pt.lng = sb("lngs", [P, 2, 8], F32)
    pt.lnb = sb("lnbs", [P, 2, 8], F32)
    pt.b2s = sb("b2s", [P, 2, 8], F32)
    pt.b1s = sb("b1s", [P, 2, 44], F32)
    pt.eps = sb("eps", [P, 1], F32)
    pt.eps6 = sb("eps6", [P, 1], F32)
    env.pt = pt

    _emit_init(env)
    with contextlib.ExitStack() as wces:
        _emit_wcast(env, wces)
        if N_PHASES >= 1:
            _emit_ph0(env)
    if N_PHASES >= 2:
        _emit_attnA(env)
    if N_PHASES >= 3:
        _emit_zeroinit(env)
        _emit_slA(env, 0)
        _emit_slF(env, 0)
        _emit_gate(env, "yB")
    if N_PHASES >= 4:
        _emit_slA(env, 1)
        _emit_gate(env, "yA")
        _emit_slF(env, 1)
    if N_PHASES >= 5:
        _emit_final(env)
    _emit_tail(env)

    with nc.Block("main") as _mb:
        env.dblk.flush(_mb)
    es.close()
    return nc


def _emit_init(env):
    nc, I, pt, c = env.nc, env.I, env.pt, env.c
    sD, sG = env.sD, env.sG

    with _PhaseCtx(env) as blk:
        @blk.gpsimd
        def _(g):
            g.memset(pt.identb[:], 0.0)
            g.affine_select(out=pt.identb[:], in_=pt.identb[:], compare_op=ALU.not_equal,
                            fill=1.0, base=0, pattern=[[-1, P]], channel_multiplier=1)
            g.memset(pt.ident3[:], 0.0)
            g.affine_select(out=pt.ident3[:], in_=pt.ident3[:], compare_op=ALU.not_equal,
                            fill=1.0, base=0, pattern=[[-1, P]], channel_multiplier=1)
            g.memset(pt.ones1r[:], 1.0)
            g.memset(pt.onesc[:], 1.0)
            g.memset(pt.onesb[:], 1.0)
            g.memset(pt.eps[:], 1e-5)
            g.memset(pt.eps6[:], 1e-6).then_inc(env.sG, 1)
            c.add(sG, 1)

        @blk.sync
        def _(s):
            def ld(dst, src):
                s.dma_start(out=dst, in_=src).then_inc(sD, 16)
                c.add(sD, 16)
            ld(pt.gn[:], I["gnorm"][:])
            ld(pt.bqkv[:], I["bqk_t"][:])
            ld(pt.bproj[:], I["bprojA"][:])
            ld(pt.segb[:], I["segbf"][:])
            ld(pt.perm32[:], I["perm32"][:])
            ld(pt.segT[:], I["segT"][:])
            ld(pt.gate[:], I["gate"][:])
            ld(pt.qs8[:], I["qsc"][:])
            ld(pt.ksc[:], I["ksc"][:])
            ld(pt.lng[:], I["lng"][:])
            ld(pt.lnb[:], I["lnb"][:])
            ld(pt.b2s[:], I["b22"][:])
            ld(pt.b1s[:], I["b12"][:])

    snap0 = c.snap()

    with (nc.psum_tensor("ps_init", [P, 512], F32) as ps,
          nc.sbuf_tensor("bvrow", [1, D], F32) as bvrow,
          _PhaseCtx(env) as blk):
        @blk.sync
        def _(s):
            _barrier_waits(env, s, snap0)
            s.dma_start(out=bvrow[:], in_=I["bqv_row"][:]).then_inc(sD, 16)
            c.add(sD, 16)

        vready = c.cur(env.sD)

        @blk.tensor
        def _(t):
            t.wait_ge(sD, vready)
            t.matmul(ps[:, 0:512], env.pt.ones1r[0:1, :], bvrow[0:1, 0:512],
                     start=True, stop=True).then_inc(env.sT, 1)
            c.add(env.sT, 1)

        @blk.vector
        def _(v):
            v.wait_ge(env.sT, c.cur(env.sT))
            v.tensor_copy(pt.bvb[:, 0:512], ps[:, 0:512]).then_inc(env.sV, 1)
            c.add(env.sV, 1)

        t2 = c.cur(env.sV)

        @blk.tensor
        def _(t):
            t.wait_ge(env.sV, t2)
            t.matmul(ps[:, 0:512], env.pt.ones1r[0:1, :], bvrow[0:1, 512:1024],
                     start=True, stop=True).then_inc(env.sT, 1)
            c.add(env.sT, 1)

        @blk.vector
        def _(v):
            v.wait_ge(env.sT, c.cur(env.sT))
            v.tensor_copy(pt.bvb[:, 512:1024], ps[:, 0:512]).then_inc(env.sV, 1)
            c.add(env.sV, 1)
            v.wait_ge(env.sG, 1)
            v.tensor_scalar_mul(pt.qs8[:], pt.qs8[:], 8.0).then_inc(env.sV, 1)
            c.add(env.sV, 1)

    env.snap = c.snap()


def _emit_wcast(env, wces):
    nc, I, Z, c = env.nc, env.I, env.Z, env.c
    sG = env.sG

    jobs = []
    def add2d(srcap, dstap, rows, ncols):
        for r in range(0, rows, 128):
            rr = min(128, rows - r)
            for c0 in range(0, ncols, HID):
                cw = min(HID, ncols - c0)
                jobs.append((srcap[r:r + rr, c0:c0 + cw],
                             dstap[r:r + rr, c0:c0 + cw], rr, cw))
    add2d(I["wqkvA"], Z["wqkvA_bf"], D, 3 * D)
    add2d(I["wprojA"], Z["wprojA_bf"], D, D)
    for l in range(2):
        add2d(I["wqkv2"][l], Z["wqkv2_bf"][l], D, 3 * INNER)
        add2d(I["wo2"][l], Z["wo2_bf"][l], INNER, D)
        add2d(I["w12"][l], Z["w12_bf"][l], D, W1C)
        add2d(I["w22"][l], Z["w22_bf"][l], HID, D)
    wq_jobs = 16

    b32 = [wces.enter_context(nc.sbuf_tensor("wc_f0", [P, HID], F32))]
    b16 = [wces.enter_context(nc.sbuf_tensor("wc_b0", [P, HID], BF16))]

    snap = env.snap
    wq_mark = [0]

    with _PhaseCtx(env) as blk:
        @blk.gpsimd
        def _(g):
            _barrier_waits(env, g, snap)
            store_done = [0]
            for i, (sap, dap, rr, ncols) in enumerate(jobs):
                sl = 0
                if store_done[sl]:
                    g.wait_ge(sG, store_done[sl])
                g.dma_start(out=b32[sl][0:rr, 0:ncols], in_=sap).then_inc(sG, 16)
                ld = c.add(sG, 16)
                g.wait_ge(sG, ld)
                g.tensor_copy(b16[sl][0:rr, 0:ncols], b32[sl][0:rr, 0:ncols]).then_inc(sG, 1)
                cv = c.add(sG, 1)
                g.wait_ge(sG, cv)
                g.dma_start(out=dap, in_=b16[sl][0:rr, 0:ncols]).then_inc(sG, 16)
                store_done[sl] = c.add(sG, 16)
                if i == wq_jobs - 1:
                    wq_mark[0] = max(store_done)
    env.wq_ready = wq_mark[0]


def _emit_ph0(env):
    nc, I, Z, pt, c = env.nc, env.I, env.Z, env.pt, env.c
    sD, sG, sT, sV, sA = env.sD, env.sG, env.sT, env.sV, env.sA

    groups = []
    groups.append(("oh", 0, 512, "halo"))
    groups.append(("oh", 512, 512, "halo"))
    groups.append(("oh", 1024, 128, "halo"))
    for g in range(8):
        groups.append(("oh", 1152 + 512 * g, 512, "own"))
    for g in range(8):
        groups.append(("oth", 512 * g, 512, "oth"))

    with contextlib.ExitStack() as ph:
        sb = lambda n, s, d: ph.enter_context(nc.sbuf_tensor(n, s, d))
        wq_bf = sb("ph0_wq", [P, 8, 3 * D], BF16)
        xin = [sb(f"ph0_xin{i}", [P, 4, D], F32) for i in range(2)]
        stat = [sb(f"ph0_stat{i}", [P, 4 * 6], F32) for i in range(2)]
        xc = [sb(f"ph0_xc{i}", [P, 4, D], F32) for i in range(2)]
        ha_t = sb("ph0_hat", [P, 4, D], BF16)
        ha_f = sb("ph0_haf", [P, 8, 512], BF16)
        stg = sb("ph0_stg", [P, 512], F32)
        qtmp = sb("ph0_qtmp", [P, 8, 512], F32)
        rstg = sb("ph0_rstg", [P, 512], BF16)
        rc_s = [sb("ph0_rc0", [P, 8, 512], BF16)]
        rs_s = [sb("ph0_rs0", [P, 8, 512], BF16)]
        vstg = sb("ph0_vstg", [P, 1024], BF16)
        rstg2 = sb("ph0_rstg2", [P, 512], F32)
        ps1 = ph.enter_context(nc.psum_tensor("ph0_ps1", [P, 512], F32))
        ps2 = ph.enter_context(nc.psum_tensor("ph0_ps2", [P, 512], F32))
        psb = ph.enter_context(nc.psum_tensor("ph0_psb", [P, 1024], BF16))
        psv = ph.enter_context(nc.psum_tensor("ph0_psv", [P, 1024], F32))

        with _PhaseCtx(env) as blk:
            snap = env.snap
            _phase_barrier(env, blk, snap)

            @blk.sync
            def _(s):
                _barrier_waits(env, s, snap)
                s.wait_ge(env.sG, env.wq_ready)
                s.dma_start(out=wq_bf[:], in_=Z["wqkvA_bf"].rearrange("(c p) n -> p c n", p=P)
                            ).then_inc(sD, 16)
                c.add(sD, 16)

            bufs = dict(wq_bf=wq_bf, xin=xin, stat=stat, xc=xc, ha_t=ha_t,
                        ha_f=ha_f, stg=stg, qtmp=qtmp, rstg=rstg, rc_s=rc_s,
                        rs_s=rs_s, vstg=vstg, rstg2=rstg2, ps1=ps1, ps2=ps2,
                        psb=psb, psv=psv)
            prev = {}
            pend = None
            for gi, (src, r0, nr, kind) in enumerate(groups):
                if pend is not None:
                    _ph0_back(env, blk, bufs, pend)
                fr = _ph0_front(env, blk, bufs, gi, src, r0, nr, kind, prev)
                prev = fr["prev2"]
                pend = fr
            if pend is not None:
                _ph0_back(env, blk, bufs, pend)

    env.snap = c.snap()


def _ph0_front(env, blk, bufs, gi, src, r0, nr, kind, prev):
    nc, I, Z, pt, c = env.nc, env.I, env.Z, env.pt, env.c
    sD, sG, sT, sV, sA = env.sD, env.sG, env.sT, env.sV, env.sA
    nt = nr // 128
    par = gi % 2
    xin, stat, xc = bufs["xin"][par], bufs["stat"][par], bufs["xc"][par]
    ha_t, stg, psv = bufs["ha_t"], bufs["stg"], bufs["psv"]
    xsrc = I["x_oh"] if src == "oh" else I["x_oth"]
    do_xn = kind in ("halo", "own")
    do_qkv = kind in ("own", "oth")
    own_q = kind == "own"
    xn_c0 = r0 if src == "oh" else None
    key_c0 = (r0 - H) if src == "oh" else (TOWN + r0)
    pf = prev.get(par, {})
    out = {}

    @blk.sync
    def _(s, pf=pf):
        if pf.get("xin_T"):
            s.wait_ge(sT, pf["xin_T"])
        if pf.get("xin_V"):
            s.wait_ge(sV, pf["xin_V"])
        s.dma_start(out=xin[:, 0:nt, :],
                    in_=xsrc[r0:r0 + nr, :].rearrange("(a p) f -> p a f", p=P)
                    ).then_inc(sD, 16)
        c.add(sD, 16)
    ld_done = c.cur(sD)

    @blk.vector
    def _(v, pf=pf):
        v.wait_ge(sD, ld_done)
        if pf.get("stat_A"):
            v.wait_ge(sA, pf["stat_A"])
        for a in range(nt):
            last = v.reduce_sum(stat[:, a * 6:a * 6 + 1], xin[:, a, :], axis=AX.X)
        _fence(env, v, sV, last)
        for a in range(nt):
            last = v.tensor_scalar_mul(stat[:, a * 6 + 1:a * 6 + 2], stat[:, a * 6:a * 6 + 1], 1.0 / D)
        _fence(env, v, sV, last)
        for a in range(nt):
            last = v.tensor_scalar(xc[:, a, :], xin[:, a, :], stat[:, a * 6 + 1:a * 6 + 2],
                                   None, op0=ALU.subtract)
        last.then_inc(sV, 1)
        c.add(sV, 1)
    xc_done = c.cur(sV)

    @blk.scalar
    def _(sc, tw=c.cur(sT)):
        sc.wait_ge(sV, xc_done)
        sc.wait_ge(sT, tw)
        for a in range(nt):
            last = sc.activation(out=ha_t[:, a, :], in_=xc[:, a, :], func=AF.Square)
        last.then_inc(sA, 1)
        c.add(sA, 1)
    sq_done = c.cur(sA)

    @blk.vector
    def _(v):
        v.wait_ge(sA, sq_done)
        for a in range(nt):
            last = v.reduce_sum(stat[:, a * 6 + 2:a * 6 + 3], ha_t[:, a, :], axis=AX.X)
        last.then_inc(sV, 1)
        c.add(sV, 1)
    ss_done = c.cur(sV)

    @blk.scalar
    def _(sc):
        sc.wait_ge(sV, ss_done)
        for a in range(nt):
            last = sc.activation(out=stat[:, a * 6 + 3:a * 6 + 4], in_=stat[:, a * 6 + 2:a * 6 + 3],
                                 func=AF.Ln, bias=pt.eps[:, 0:1], scale=1.0 / D)
        _fence(env, sc, sA, last)
        for a in range(nt):
            last = sc.activation(out=stat[:, a * 6 + 2:a * 6 + 3], in_=stat[:, a * 6 + 3:a * 6 + 4],
                                 func=AF.Exp, scale=-0.5)
        if do_qkv:
            for a in range(nt):
                last = sc.activation(out=stat[:, a * 6 + 4:a * 6 + 5], in_=stat[:, a * 6 + 3:a * 6 + 4],
                                     func=AF.Exp, scale=-1.0)
        last.then_inc(sA, 1)
        c.add(sA, 1)
    r1_done = c.cur(sA)

    if do_qkv:
        @blk.vector
        def _(v):
            v.wait_ge(sA, r1_done)
            for a in range(nt):
                last = v.tensor_scalar(stat[:, a * 6 + 4:a * 6 + 5], stat[:, a * 6 + 4:a * 6 + 5],
                                       -1e-5, 1.0, op0=ALU.mult, op1=ALU.add)
            last.then_inc(sV, 1)
            c.add(sV, 1)
        ms_done = c.cur(sV)

        @blk.scalar
        def _(sc):
            sc.wait_ge(sV, ms_done)
            for a in range(nt):
                last = sc.activation(out=stat[:, a * 6 + 5:a * 6 + 6], in_=stat[:, a * 6 + 4:a * 6 + 5],
                                     func=AF.Ln, bias=pt.eps6[:, 0:1])
            last.then_inc(sA, 1)
            c.add(sA, 1)
        w_done = c.cur(sA)

        @blk.vector
        def _(v):
            v.wait_ge(sA, w_done)
            for a in range(nt):
                last = v.tensor_add(stat[:, a * 6 + 5:a * 6 + 6], stat[:, a * 6 + 5:a * 6 + 6],
                                    stat[:, a * 6 + 3:a * 6 + 4])
            last.then_inc(sV, 1)
            c.add(sV, 1)
        uw_done = c.cur(sV)

        @blk.scalar
        def _(sc):
            sc.wait_ge(sV, uw_done)
            for a in range(nt):
                last = sc.activation(out=stat[:, a * 6 + 5:a * 6 + 6], in_=stat[:, a * 6 + 5:a * 6 + 6],
                                     func=AF.Exp, scale=-0.5)
            last.then_inc(sA, 1)
            c.add(sA, 1)

    stats_done = c.cur(sA)
    out["stat_A"] = stats_done

    @blk.vector
    def _(v):
        v.wait_ge(sA, stats_done)
        last = None
        if do_qkv:
            for a in range(nt):
                last = v.tensor_scalar(ha_t[:, a, :], xc[:, a, :], stat[:, a * 6 + 5:a * 6 + 6],
                                       None, op0=ALU.mult)
        if do_xn:
            for a in range(nt):
                last = v.tensor_scalar(xin[:, a, :], xc[:, a, :], stat[:, a * 6 + 2:a * 6 + 3],
                                       None, op0=ALU.mult)
        last.then_inc(sV, 1)
        c.add(sV, 1)
    norm_done = c.cur(sV)

    if do_xn:
        for cc in range(8):
            tw = c.cur(sV)
            aw3 = c.cur(sA)

            @blk.tensor
            def _(t, cc=cc, tw=tw, aw3=aw3):
                t.wait_ge(sV, tw)
                t.wait_ge(sA, aw3)
                for a in range(nt):
                    last = t.transpose(psv[:, a * 128:(a + 1) * 128],
                                       xin[:, a, cc * 128:(cc + 1) * 128], pt.ident3[:])
                last.then_inc(sT, 1)
                c.add(sT, 1)
            tp_done = c.cur(sT)
            dprev = c.cur(sD)

            @blk.scalar
            def _(sc, cc=cc, tp_done=tp_done, dprev=dprev):
                sc.wait_ge(sT, tp_done)
                sc.wait_ge(sD, dprev)
                sc.activation(out=stg[:, 0:nr], in_=psv[:, 0:nr],
                              func=AF.Copy).then_inc(sA, 1)
                c.add(sA, 1)
            ev_done = c.cur(sA)

            @blk.sync
            def _(s, cc=cc, ev_done=ev_done):
                s.wait_ge(sA, ev_done)
                s.dma_start(out=Z["xnF"][cc * 128:(cc + 1) * 128, xn_c0:xn_c0 + nr],
                            in_=stg[:, 0:nr]).then_inc(sD, 16)
                c.add(sD, 16)
    out["xin_T"] = c.cur(sT)
    out["xin_V"] = norm_done

    prev2 = dict(prev)
    prev2[par] = out
    return dict(prev2=prev2, gi=gi, src=src, r0=r0, nr=nr, kind=kind, par=par,
                out=out, norm_done=norm_done, key_c0=key_c0)


def _ph0_back(env, blk, bufs, fr):
    nc, I, Z, pt, c = env.nc, env.I, env.Z, env.pt, env.c
    sD, sG, sT, sV, sA = env.sD, env.sG, env.sT, env.sV, env.sA
    kind, r0, nr, par = fr["kind"], fr["r0"], fr["nr"], fr["par"]
    do_qkv = kind in ("own", "oth")
    if not do_qkv:
        return
    own_q = kind == "own"
    key_c0 = fr["key_c0"]
    nt = nr // 128
    norm_done = fr["norm_done"]
    wq_bf, ha_t, ha_f = bufs["wq_bf"], bufs["ha_t"], bufs["ha_f"]
    qtmp, rstg, rstg2 = bufs["qtmp"], bufs["rstg"], bufs["rstg2"]
    rc_s, rs_s = bufs["rc_s"][0], bufs["rs_s"][0]
    vstg, ps1, psb, psv = bufs["vstg"], bufs["ps1"], bufs["psb"], bufs["psv"]

    @blk.sync
    def _(s, vw0=c.cur(sV)):
        s.wait_ge(sV, vw0)
        rc = I["rc_own"] if own_q else I["rc_oth"]
        rs = I["rs_own"] if own_q else I["rs_oth"]
        col = (r0 - H) if own_q else r0
        s.dma_start(out=rc_s[:],
                    in_=rc[:, col:col + 512].rearrange("(c p) n -> p c n", p=P)
                    ).then_inc(sD, 16)
        c.add(sD, 16)
        s.dma_start(out=rs_s[:],
                    in_=rs[:, col:col + 512].rearrange("(c p) n -> p c n", p=P)
                    ).then_inc(sD, 16)
        c.add(sD, 16)
    tbl_ok = c.cur(sD)

    for cc in range(8):
        tw = c.cur(sV)
        aw2 = c.cur(sA)

        @blk.tensor
        def _(t, cc=cc, tw=tw, aw2=aw2):
            t.wait_ge(sV, max(tw, norm_done))
            t.wait_ge(sA, aw2)
            for a in range(nt):
                last = t.transpose(psb[:, a * 128:(a + 1) * 128],
                                   ha_t[:, a, cc * 128:(cc + 1) * 128], pt.identb[:])
            last.then_inc(sT, 1)
            c.add(sT, 1)
        tp2 = c.cur(sT)

        @blk.scalar
        def _(sc, cc=cc, tp2=tp2):
            sc.wait_ge(sT, tp2)
            sc.activation(out=ha_f[:, cc, 0:nr], in_=psb[:, 0:nr], func=AF.Copy,
                          scale=pt.gn[:, cc:cc + 1]).then_inc(sA, 1)
            c.add(sA, 1)
    haf_done = c.cur(sA)

    col_sets = []
    if own_q:
        col_sets.append(("q", 0))
    col_sets.append(("k", 8))
    for nmk, cbase in col_sets:
        for oc in range(8):
            aw = c.cur(sA)
            vw = c.cur(sV)

            @blk.tensor
            def _(t, oc=oc, cbase=cbase, aw=aw):
                t.wait_ge(sA, max(haf_done, aw))
                for f in range(8):
                    last = t.matmul(ps1[:, 0:nr],
                                    wq_bf[:, f, (cbase + oc) * 128:(cbase + oc + 1) * 128],
                                    ha_f[:, f, 0:nr], start=(f == 0), stop=(f == 7))
                last.then_inc(sT, 1)
                c.add(sT, 1)
            mm_done = c.cur(sT)

            @blk.scalar
            def _(sc, oc=oc, cbase=cbase, mm_done=mm_done, vw=vw):
                sc.wait_ge(sT, mm_done)
                sc.wait_ge(sV, vw)
                sc.activation(out=qtmp[:, oc, 0:nr], in_=ps1[:, 0:nr], func=AF.Identity,
                              bias=pt.bqkv[:, cbase + oc:cbase + oc + 1]).then_inc(sA, 1)
                c.add(sA, 1)
        qk_done = c.cur(sA)

        for oc in range(8):
            dw = c.cur(sD)

            @blk.vector
            def _(v, oc=oc, dw=dw, qk_done=qk_done):
                v.wait_ge(sA, qk_done)
                v.wait_ge(sD, max(dw, tbl_ok))
                v.tensor_tensor(rstg2[:, 0:nr], qtmp[:, oc, 0:nr], rc_s[:, oc, 0:nr],
                                op=ALU.mult)
                last = v.tensor_tensor(bufs["xc"][par][:, 0, 0:nr], qtmp[:, oc ^ 4, 0:nr],
                                       rs_s[:, oc, 0:nr], op=ALU.mult)
                _fence(env, v, sV, last)
                v.tensor_tensor(rstg[:, 0:nr], rstg2[:, 0:nr],
                                bufs["xc"][par][:, 0, 0:nr], op=ALU.add).then_inc(sV, 1)
                c.add(sV, 1)
            rp_done = c.cur(sV)

            @blk.sync
            def _(s, oc=oc, nmk=nmk, rp_done=rp_done):
                s.wait_ge(sV, rp_done)
                dst = Z["qfA"] if nmk == "q" else Z["kfA"]
                col0 = (r0 - H) if nmk == "q" else key_c0
                s.dma_start(out=dst[oc * 128:(oc + 1) * 128, col0:col0 + nr],
                            in_=rstg[:, 0:nr]).then_inc(sD, 16)
                c.add(sD, 16)

    for a in range(nt):
        vw = c.cur(sV)
        aw4 = c.cur(sA)

        @blk.tensor
        def _(t, a=a, vw=vw, aw4=aw4):
            t.wait_ge(sA, max(haf_done, aw4))
            t.wait_ge(sV, vw)
            for f in range(8):
                for vh in range(2):
                    last = t.matmul(psv[:, vh * 512:(vh + 1) * 512],
                                    ha_f[:, f, a * 128:(a + 1) * 128],
                                    wq_bf[:, f, 2048 + vh * 512:2048 + (vh + 1) * 512],
                                    start=(f == 0), stop=(f == 7))
            last.then_inc(sT, 1)
            c.add(sT, 1)
        vm_done = c.cur(sT)
        dw = c.cur(sD)

        @blk.vector
        def _(v, a=a, vm_done=vm_done, dw=dw):
            v.wait_ge(sT, vm_done)
            v.wait_ge(sD, dw)
            v.tensor_tensor(vstg[:, 0:1024], psv[:, 0:1024], pt.bvb[:, 0:1024],
                            op=ALU.add).then_inc(sV, 1)
            c.add(sV, 1)
        ve_done = c.cur(sV)

        @blk.sync
        def _(s, a=a, ve_done=ve_done):
            s.wait_ge(sV, ve_done)
            s.dma_start(out=Z["vtA"][key_c0 + a * 128:key_c0 + (a + 1) * 128, :],
                        in_=vstg[:, 0:1024]).then_inc(sD, 16)
            c.add(sD, 16)


def _emit_attnA(env):
    nc, I, Z, pt, c = env.nc, env.I, env.Z, env.pt, env.c
    sD, sG, sT, sV, sA = env.sD, env.sG, env.sT, env.sV, env.sA

    with contextlib.ExitStack() as ph:
        sb = lambda n, s, d: ph.enter_context(nc.sbuf_tensor(n, s, d))
        QSB = sb("aA_q", [P, 8, 1024], BF16)
        KSB = [sb(f"aA_k{i}", [P, 8, 2048], BF16) for i in range(2)]
        VSB = sb("aA_v", [P, 16, 1024], BF16)
        OACC = sb("aA_oacc", [P, 8, 1024], F32)
        PSB = sb("aA_p", [P, 2048], BF16)
        PTS = sb("aA_pt", [P, 2048], BF16)
        OFSB = sb("aA_of", [P, 8, 1024], BF16)
        WP = sb("aA_wp", [P, 8, 1024], BF16)
        mcur = sb("aA_m", [P, 8], F32)
        lcur = sb("aA_l", [P, 8], F32)
        mx = sb("aA_mx", [P, 1], F32)
        mnew = sb("aA_mn", [P, 1], F32)
        dd = sb("aA_dd", [P, 1], F32)
        af = sb("aA_af", [P, 1], F32)
        bias = sb("aA_bias", [P, 1], F32)
        lch = sb("aA_lch", [P, 1], F32)
        rl = sb("aA_rl", [P, 1], F32)
        obf = sb("aA_obf", [P, 1024], BF16)
        stg = sb("aA_stg", [P, 512], F32)
        xns = sb("aA_xns", [P, 512], F32)
        psS = ph.enter_context(nc.psum_tensor("aA_psS", [P, 2048], F32))
        psT = ph.enter_context(nc.psum_tensor("aA_psT", [P, 2048], BF16))
        psO = ph.enter_context(nc.psum_tensor("aA_psO", [P, 1024], F32))

        with _PhaseCtx(env) as blk:
            snap = env.snap
            _phase_barrier(env, blk, snap)

            @blk.sync
            def _(s):
                s.dma_start(out=WP[:], in_=Z["wprojA_bf"].rearrange("(c p) n -> p c n", p=P)
                            ).then_inc(sD, 16)
                c.add(sD, 16)

            for qpass in range(4):
                _attnA_pass(env, blk, qpass, QSB, KSB, VSB, OACC, PSB, PTS, OFSB,
                            WP, mcur, lcur, mx, mnew, dd, af, bias, lch, rl, obf,
                            stg, xns, psS, psT, psO)

    env.snap = c.snap()


def _attnA_pass(env, blk, qpass, QSB, KSB, VSB, OACC, PSB, PTS, OFSB, WP,
                mcur, lcur, mx, mnew, dd, af, bias, lch, rl, obf, stg, xns,
                psS, psT, psO):
    nc, I, Z, pt, c = env.nc, env.I, env.Z, env.pt, env.c
    sD, sG, sT, sV, sA = env.sD, env.sG, env.sT, env.sV, env.sA
    q0 = qpass * 1024

    st = dict(exp=0, trans=0, pv=0, mch=0, lo=0, ttr=0)
    tprev, vprev, aprev = c.cur(sT), c.cur(sV), c.cur(sA)

    @blk.sync
    def _(s):
        s.wait_ge(sT, tprev)
        s.wait_ge(sV, vprev)
        s.wait_ge(sA, aprev)
        for f in range(8):
            s.dma_start(out=QSB[:, f, :],
                        in_=Z["qfA"][f * 128:(f + 1) * 128, q0:q0 + 1024]).then_inc(sD, 16)
            c.add(sD, 16)

    kv_users = [tprev, tprev]
    v_user = tprev
    kv_ok = [0] * 5
    v_ok = [0] * 5

    def load_k(kc):
        sl = kc % 2

        @blk.sync
        def _(s, kc=kc, sl=sl, tw=kv_users[sl]):
            s.wait_ge(sT, tw)
            for f in range(8):
                s.dma_start(out=KSB[sl][:, f, :],
                            in_=Z["kfA"][f * 128:(f + 1) * 128, kc * 2048:(kc + 1) * 2048]
                            ).then_inc(sD, 16)
                c.add(sD, 16)
        kv_ok[kc] = c.cur(sD)

    def load_v(kc):
        @blk.sync
        def _(s, kc=kc, tw=v_user):
            s.wait_ge(sT, tw)
            for u in range(16):
                s.dma_start(out=VSB[:, u, :],
                            in_=Z["vtA"][kc * 2048 + u * 128:kc * 2048 + (u + 1) * 128, :]
                            ).then_inc(sD, 16)
                c.add(sD, 16)
        v_ok[kc] = c.cur(sD)

    load_k(0)
    load_v(0)
    for kc in range(4):
        sl = kc % 2
        if kc + 1 < 4:
            load_k(kc + 1)

        def emit_S(qt):
            @blk.tensor
            def _(t, qt=qt, sl=sl, ko=kv_ok[kc], ew=st["exp"]):
                t.wait_ge(sD, ko)
                if ew:
                    t.wait_ge(sA, ew)
                for f in range(8):
                    for kk in range(4):
                        last = t.matmul(psS[:, kk * 512:(kk + 1) * 512],
                                        QSB[:, f, qt * 128:(qt + 1) * 128],
                                        KSB[sl][:, f, kk * 512:(kk + 1) * 512],
                                        start=(f == 0), stop=(f == 7))
                last.then_inc(sT, 1)
                c.add(sT, 1)
            s_done = c.cur(sT)

            @blk.vector
            def _(v, qt=qt, kc=kc, s_done=s_done, aw2=st["exp"]):
                v.wait_ge(sT, s_done)
                if aw2:
                    v.wait_ge(sA, aw2)
                last = v.reduce_max(mx[:], psS[:, 0:2048], axis=AX.X)
                _fence(env, v, sV, last)
                if kc == 0:
                    v.tensor_copy(mcur[:, qt:qt + 1], mx[:])
                    v.tensor_scalar_mul(bias[:], mx[:], -C_A).then_inc(sV, 1)
                else:
                    last = v.tensor_max(mnew[:], mcur[:, qt:qt + 1], mx[:])
                    _fence(env, v, sV, last)
                    v.tensor_scalar_mul(bias[:], mnew[:], -C_A)
                    v.tensor_tensor(dd[:], mcur[:, qt:qt + 1], mnew[:], op=ALU.subtract)
                    v.tensor_copy(mcur[:, qt:qt + 1], mnew[:]).then_inc(sV, 1)
                c.add(sV, 1)
            st["mch"] = c.cur(sV)

        def emit_exp(qt):
            @blk.scalar
            def _(sc, qt=qt, kc=kc, m_ok=st["mch"], tw=st["trans"], lw=st["lo"]):
                sc.wait_ge(sV, m_ok)
                if tw:
                    sc.wait_ge(sT, tw)
                if lw:
                    sc.wait_ge(sV, lw)
                last = sc.activation(out=PSB[:], in_=psS[:, 0:2048], func=AF.Exp,
                                     bias=bias[:, 0:1], scale=C_A,
                                     accum_out=lch[:, 0:1])
                if kc > 0:
                    last = sc.activation(out=af[:], in_=dd[:, 0:1], func=AF.Exp,
                                         scale=C_A)
                last.then_inc(sA, 1)
                c.add(sA, 1)
            st["exp"] = c.cur(sA)

        def emit_tail(qt):
            e_done = st["exp"]

            @blk.tensor
            def _(t, qt=qt, e_done=e_done, vw=st["ttr"]):
                t.wait_ge(sA, e_done)
                if vw:
                    t.wait_ge(sV, vw)
                for f in range(16):
                    last = t.transpose(psT[:, f * 128:(f + 1) * 128],
                                       PSB[:, f * 128:(f + 1) * 128], pt.identb[:])
                last.then_inc(sT, 1)
                c.add(sT, 1)
            st["trans"] = c.cur(sT)
            tr_done = st["trans"]

            @blk.vector
            def _(v, tr_done=tr_done, pw=st["pv"]):
                v.wait_ge(sT, tr_done)
                if pw:
                    v.wait_ge(sT, pw)
                v.tensor_copy(PTS[:], psT[:, 0:2048]).then_inc(sV, 1)
                c.add(sV, 1)
            st["ttr"] = c.cur(sV)
            pt_ok = st["ttr"]

            @blk.tensor
            def _(t, pt_ok=pt_ok, vo=v_ok[kc]):
                t.wait_ge(sV, pt_ok)
                t.wait_ge(sD, vo)
                for f in range(16):
                    for vh in range(2):
                        last = t.matmul(psO[:, vh * 512:(vh + 1) * 512],
                                        PTS[:, f * 128:(f + 1) * 128],
                                        VSB[:, f, vh * 512:(vh + 1) * 512],
                                        start=(f == 0), stop=(f == 15))
                last.then_inc(sT, 1)
                c.add(sT, 1)
            st["pv"] = c.cur(sT)
            pv_ok = st["pv"]

            @blk.vector
            def _(v, qt=qt, kc=kc, pv_ok=pv_ok):
                v.wait_ge(sT, pv_ok)
                if kc == 0:
                    v.tensor_copy(lcur[:, qt:qt + 1], lch[:])
                    v.tensor_copy(OACC[:, qt, :], psO[:, 0:1024]).then_inc(sV, 1)
                else:
                    v.scalar_tensor_tensor(out=lcur[:, qt:qt + 1], in0=lcur[:, qt:qt + 1],
                                           scalar=af[:, 0:1], in1=lch[:],
                                           op0=ALU.mult, op1=ALU.add)
                    v.scalar_tensor_tensor(out=OACC[:, qt, :], in0=OACC[:, qt, :],
                                           scalar=af[:, 0:1], in1=psO[:, 0:1024],
                                           op0=ALU.mult, op1=ALU.add).then_inc(sV, 1)
                c.add(sV, 1)
            st["lo"] = c.cur(sV)

        for qt in range(8):
            emit_S(qt)
            if qt >= 1:
                emit_tail(qt - 1)
            emit_exp(qt)
        emit_tail(7)
        kv_users[sl] = c.cur(sT)
        v_user = c.cur(sT)
        if kc + 1 < 4:
            load_v(kc + 1)

    for qt in range(8):
        @blk.vector
        def _(v, qt=qt, tw=c.cur(sT)):
            v.wait_ge(sT, tw)
            last = v.reciprocal(rl[:], lcur[:, qt:qt + 1])
            _fence(env, v, sV, last)
            v.tensor_scalar(obf[:], OACC[:, qt, :], rl[:, 0:1], None,
                            op0=ALU.mult).then_inc(sV, 1)
            c.add(sV, 1)
        n_done = c.cur(sV)

        @blk.tensor
        def _(t, qt=qt, n_done=n_done):
            t.wait_ge(sV, n_done)
            for cc in range(8):
                last = t.transpose(psT[:, cc * 128:(cc + 1) * 128],
                                   obf[:, cc * 128:(cc + 1) * 128], pt.identb[:])
            last.then_inc(sT, 1)
            c.add(sT, 1)
        ot_done = c.cur(sT)

        @blk.vector
        def _(v, qt=qt, ot_done=ot_done):
            v.wait_ge(sT, ot_done)
            for cc in range(8):
                last = v.tensor_copy(OFSB[:, cc, qt * 128:(qt + 1) * 128],
                                     psT[:, cc * 128:(cc + 1) * 128])
            last.then_inc(sV, 1)
            c.add(sV, 1)

    of_done = c.cur(sV)

    for pc in range(8):
        for ntile in range(2):
            aw = c.cur(sA)
            par = ntile % 2

            @blk.sync
            def _(s, pc=pc, ntile=ntile, dw=c.cur(sD), vw2=c.cur(sV)):
                s.wait_ge(sD, dw)
                s.wait_ge(sV, vw2)
                s.dma_start(out=xns[:],
                            in_=Z["xnF"][pc * 128:(pc + 1) * 128,
                                         H + q0 + ntile * 512:H + q0 + (ntile + 1) * 512]
                            ).then_inc(sD, 16)
                c.add(sD, 16)
            xn_ok = c.cur(sD)

            @blk.tensor
            def _(t, pc=pc, ntile=ntile, of_done=of_done, aw=aw):
                t.wait_ge(sV, of_done)
                t.wait_ge(sA, aw)
                for f in range(8):
                    last = t.matmul(psO[:, 0:512],
                                    WP[:, f, pc * 128:(pc + 1) * 128],
                                    OFSB[:, f, ntile * 512:(ntile + 1) * 512],
                                    start=(f == 0), stop=(f == 7))
                last.then_inc(sT, 1)
                c.add(sT, 1)
            pj_done = c.cur(sT)

            @blk.scalar
            def _(sc, pc=pc, pj_done=pj_done, dw2=c.cur(sD)):
                sc.wait_ge(sT, pj_done)
                sc.wait_ge(sD, dw2)
                sc.activation(out=stg[:], in_=psO[:, 0:512], func=AF.Identity,
                              bias=pt.bproj[:, pc:pc + 1]).then_inc(sA, 1)
                c.add(sA, 1)
            ev_done = c.cur(sA)

            @blk.vector
            def _(v, xn_ok=xn_ok, ev_done=ev_done):
                v.wait_ge(sD, xn_ok)
                v.wait_ge(sA, ev_done)
                v.tensor_add(stg[:], stg[:], xns[:]).then_inc(sV, 1)
                c.add(sV, 1)
            add_done = c.cur(sV)

            @blk.sync
            def _(s, pc=pc, ntile=ntile, add_done=add_done):
                s.wait_ge(sV, add_done)
                s.dma_start(out=Z["sF"][pc * 128:(pc + 1) * 128,
                                        q0 + ntile * 512:q0 + (ntile + 1) * 512],
                            in_=stg[:]).then_inc(sD, 16)
                c.add(sD, 16)


def _slA_groups():
    gs = [(0, 512), (512, 512), (1024, 128)]
    for g in range(8):
        gs.append((1152 + 512 * g, 512))
    return gs


def _emit_zeroinit(env):
    nc, Z, c = env.nc, env.Z, env.c
    sD, sV = env.sD, env.sV
    with contextlib.ExitStack() as ph:
        zt = ph.enter_context(nc.sbuf_tensor("zi_z", [P, 512], F32))
        with _PhaseCtx(env) as blk:
            snap = env.snap
            _phase_barrier(env, blk, snap)

            @blk.vector
            def _(v):
                _barrier_waits(env, v, snap)
                v.memset(zt[:], 0.0).then_inc(sV, 1)
                c.add(sV, 1)
            zok = c.cur(sV)

            @blk.sync
            def _(s):
                s.wait_ge(sV, zok)
                for buf in ("yA", "yB"):
                    for cc in range(8):
                        s.dma_start(out=env.Z[buf][cc * 128:(cc + 1) * 128, 0:512],
                                    in_=zt[:]).then_inc(sD, 16)
                        c.add(sD, 16)
    env.snap = c.snap()


def _emit_gate(env, bufname):
    nc, Z, pt, c = env.nc, env.Z, env.pt, env.c
    sD, sV = env.sD, env.sV
    with contextlib.ExitStack() as ph:
        colt = ph.enter_context(nc.sbuf_tensor("gt_col_" + bufname, [P, 8], F32))
        with _PhaseCtx(env) as blk:
            snap = env.snap
            _phase_barrier(env, blk, snap)

            @blk.sync
            def _(s):
                _barrier_waits(env, s, snap)
                s.dma_start(out=colt[:], in_=Z[bufname][:, H - 1:H].rearrange("(c p) one -> p c one", p=P),
                            allow_slow_non_contiguous=True).then_inc(sD, 16)
                c.add(sD, 16)
            lok = c.cur(sD)

            @blk.vector
            def _(v):
                v.wait_ge(sD, lok)
                v.tensor_scalar(colt[:], colt[:], pt.gate[:, 0:1], None,
                                op0=ALU.mult).then_inc(sV, 1)
                c.add(sV, 1)
            gok = c.cur(sV)

            @blk.sync
            def _(s):
                s.wait_ge(sV, gok)
                s.dma_start(out=Z[bufname][:, H - 1:H].rearrange("(c p) one -> p c one", p=P),
                            in_=colt[:], allow_slow_non_contiguous=True).then_inc(sD, 16)
                c.add(sD, 16)
    env.snap = c.snap()


def _emit_slA(env, l):
    nc, I, Z, pt, c = env.nc, env.I, env.Z, env.pt, env.c
    sD, sG, sT, sV, sA = env.sD, env.sG, env.sT, env.sV, env.sA
    ysrc = Z["xnF"] if l == 0 else Z["yB"]
    qstart = 512 if l == 0 else 1024

    with contextlib.ExitStack() as ph:
        sb = lambda n, s, d: ph.enter_context(nc.sbuf_tensor(f"l{l}{n}", s, d))
        WQ2 = sb("sa_wq", [P, 8, 1536], BF16)
        lmk = sb("sa_lmk", [P, 5, 640], F32)
        WO2 = sb("sa_wo", [P, 4, 1024], BF16)
        KF2 = sb("sa_k", [P, 4, OH], BF16)
        VT2 = sb("sa_v", [P, 41, 512], BF16)
        QF2 = sb("sa_q", [P, 4, 512], BF16)
        ysh = sb("sa_ysh", [P, 8, 512], BF16)
        yf32 = sb("sa_yf32", [P, 8, 513], F32)
        qraw = sb("sa_qraw", [P, 4, 512], F32)
        qnrm = sb("sa_qnrm", [P, 512], BF16)
        sqb = sb("sa_sqb", [P, 512], BF16)
        rr = sb("sa_rr", [P, 512], F32)
        xc1 = sb("sa_xc1", [P, 512], F32)
        xc2 = sb("sa_xc2", [P, 512], F32)
        xc3 = sb("sa_xc3", [P, 512], F32)
        xc4 = sb("sa_xc4", [P, 512], F32)
        Ssb0 = sb("sa_S0", [P, 2, 640], F32)
        Ssb = [Ssb0, Ssb0]
        gx1 = Ssb0[:, 0, 0:512]
        gx2 = Ssb0[:, 1, 0:512]
        Pb0 = sb("sa_P0", [P, 2, 640], BF16)
        Pb = [Pb0, Pb0]
        PTS = [sb(f"sa_PT{i}", [P, 2, 640], BF16) for i in range(2)]
        mxp = [sb(f"sa_mx{i}", [P, 2], F32) for i in range(2)]
        nm2 = [sb(f"sa_nm{i}", [P, 2], F32) for i in range(2)]
        ll2 = [sb(f"sa_ll{i}", [P, 2], F32) for i in range(2)]
        rli2 = [sb(f"sa_rl{i}", [P, 2], F32) for i in range(2)]
        o_loc = sb("sa_ol", [P, 4, 512], BF16)
        oF = sb("sa_of", [P, 4, 512], BF16)
        ystg = sb("sa_ystg", [P, 512], F32)
        PS_BIG = ph.enter_context(nc.psum_tensor(f"l{l}sa_big", [P, 2048], F32))
        PS_T2 = ph.enter_context(nc.psum_tensor(f"l{l}sa_t2", [P, 2048], BF16))
        PS_O = [ph.enter_context(nc.psum_tensor(f"l{l}sa_o{i}", [P, 512], F32))
                for i in range(2)]

        with _PhaseCtx(env) as blk:
            snap = env.snap
            _phase_barrier(env, blk, snap)

            @blk.sync
            def _(s):
                _barrier_waits(env, s, snap)
                s.dma_start(out=WQ2[:], in_=Z["wqkv2_bf"][l].rearrange("(c p) n -> p c n", p=P)
                            ).then_inc(sD, 16)
                c.add(sD, 16)
                s.dma_start(out=WO2[:], in_=Z["wo2_bf"][l].rearrange("(c p) n -> p c n", p=P)
                            ).then_inc(sD, 16)
                c.add(sD, 16)
                s.dma_start(out=lmk[:], in_=I["lmask"].rearrange("m p n -> p m n")
                            ).then_inc(sD, 16)
                c.add(sD, 16)

            buf = Env()
            buf.WQ2, buf.WO2, buf.KF2, buf.VT2, buf.QF2 = WQ2, WO2, KF2, VT2, QF2
            buf.ysh, buf.yf32, buf.qraw, buf.qnrm = ysh, yf32, qraw, qnrm
            buf.gx1, buf.gx2, buf.sqb, buf.rr = gx1, gx2, sqb, rr
            buf.xc = (xc1, xc2, xc3, xc4)
            buf.Ssb, buf.Pb, buf.PTS = Ssb, Pb, PTS
            buf.mxp, buf.nm2, buf.ll2, buf.rli2 = mxp, nm2, ll2, rli2
            buf.o_loc, buf.oF, buf.ystg, buf.lmk = o_loc, oF, ystg, lmk
            buf.PS_BIG, buf.PS_T2, buf.PS_O = PS_BIG, PS_T2, PS_O

            for (r0, nr) in _slA_groups():
                _slA_group(env, blk, l, ysrc, qstart, r0, nr, buf)

    env.snap = c.snap()


def _slA_group(env, blk, l, ysrc, qstart, r0, nr, buf):
    nc, I, Z, pt, c = env.nc, env.I, env.Z, env.pt, env.c
    sD, sG, sT, sV, sA = env.sD, env.sG, env.sT, env.sV, env.sA
    nt = nr // 128
    do_attn = (r0 + nr) > qstart
    B = buf

    tprev, vprev, aprev = c.cur(sT), c.cur(sV), c.cur(sA)

    @blk.sync
    def _(s):
        s.wait_ge(sT, tprev)
        s.wait_ge(sV, vprev)
        s.wait_ge(sA, aprev)
        if r0 == 0:
            s.dma_start(out=B.yf32[:, :, 1:513],
                        in_=ysrc[:, 0:512].rearrange("(c p) n -> p c n", p=P)
                        ).then_inc(sD, 16)
            c.add(sD, 16)
        else:
            s.dma_start(out=B.yf32[:, :, 0:nr + 1],
                        in_=ysrc[:, r0 - 1:r0 + nr].rearrange("(c p) n -> p c n", p=P)
                        ).then_inc(sD, 16)
            c.add(sD, 16)
        for name, bufx in (("xqc", B.xc[0]), ("xqs", B.xc[1]),
                           ("xkc", B.xc[2]), ("xks", B.xc[3])):
            s.dma_start(out=bufx[:, 0:nr], in_=I[name][:, r0:r0 + nr]).then_inc(sD, 16)
            c.add(sD, 16)
    yld = c.cur(sD)

    @blk.vector
    def _(v, yld=yld):
        v.wait_ge(sD, yld)
        if r0 == 0:
            last = v.memset(B.yf32[:, 4:8, 0:1], 0.0)
            _fence(env, v, sV, last)
        v.tensor_copy(B.ysh[:, 0:4, 0:nr], B.yf32[:, 0:4, 1:nr + 1])
        v.tensor_copy(B.ysh[:, 4:8, 0:nr], B.yf32[:, 4:8, 0:nr]).then_inc(sV, 1)
        c.add(sV, 1)
    ysh_ok = c.cur(sV)
    ld_ok = (c.cur(sD), c.cur(sG))

    for cc in range(8):
        aw = c.cur(sA)
        par = cc % 2
        psq = B.PS_BIG[:, par * 512:(par + 1) * 512]

        @blk.tensor
        def _(t, cc=cc, aw=aw, psq=psq):
            t.wait_ge(sV, ysh_ok)
            t.wait_ge(sA, aw)
            for f in range(8):
                last = t.matmul(psq[:, 0:nr], B.WQ2[:, f, cc * 128:(cc + 1) * 128],
                                B.ysh[:, f, 0:nr], start=(f == 0), stop=(f == 7))
            last.then_inc(sT, 1)
            c.add(sT, 1)
        mm_ok = c.cur(sT)

        @blk.scalar
        def _(sc, cc=cc, mm_ok=mm_ok, vw=c.cur(sV), psq=psq):
            sc.wait_ge(sT, mm_ok)
            sc.wait_ge(sV, vw)
            sc.activation(out=B.qraw[:, cc % 4, 0:nr], in_=psq[:, 0:nr],
                          func=AF.Identity).then_inc(sA, 1)
            c.add(sA, 1)

        if cc % 4 == 3:
            isq = cc < 4
            for u in range(4):
                _slA_norm_chunk(env, blk, l, isq, u, r0, nr, B, ld_ok)

    for a in range(nt):
        vw = c.cur(sV)

        @blk.tensor
        def _(t, a=a, vw=vw):
            t.wait_ge(sV, max(vw, ysh_ok))
            for f in range(8):
                last = t.matmul(B.PS_O[0][:, 0:512], B.ysh[:, f, a * 128:(a + 1) * 128],
                                B.WQ2[:, f, 1024:1536], start=(f == 0), stop=(f == 7))
            last.then_inc(sT, 1)
            c.add(sT, 1)
        vm_ok = c.cur(sT)

        @blk.vector
        def _(v, a=a, vm_ok=vm_ok):
            v.wait_ge(sT, vm_ok)
            v.tensor_copy(B.VT2[:, (r0 // 128) + a, :], B.PS_O[0][:, 0:512]).then_inc(sV, 1)
            c.add(sV, 1)

    if not do_attn:
        return

    ttr_done = [0, 0]
    exp_done = [0, 0]
    trans_done = [0, 0]
    evac_done = [0, 0]
    pv_done = [0, 0]
    o_done = [0, 0]

    pairs = []
    for a in range(nt):
        r_tile = r0 + a * 128
        if r_tile + 128 <= qstart:
            continue
        if r_tile >= H:
            j = (r_tile - H) // 128
            midx = j if j < 4 else 4
        else:
            midx = 4
        for hp2 in range(4):
            pairs.append((a, hp2, r_tile, midx))

    qkv_done = (c.cur(sV), c.cur(sA), c.cur(sT))

    def emit_S(i):
        a, hp2, r_tile, midx = pairs[i]
        par = i % 2
        k0 = r_tile - 512

        @blk.tensor
        def _(t, a=a, hp2=hp2, k0=k0, tw=max(ttr_done)):
            if i == 0:
                t.wait_ge(sV, qkv_done[0])
                t.wait_ge(sA, qkv_done[1])
            if tw:
                t.wait_ge(sV, tw)
            qA = B.QF2[0:64, hp2, a * 128:(a + 1) * 128]
            qB = B.QF2[64:128, hp2, a * 128:(a + 1) * 128]
            kA = B.KF2[0:64, hp2, :]
            kB = B.KF2[64:128, hp2, :]
            t.matmul(B.PS_BIG[:, 0:512], qA, kA[:, k0:k0 + 512], start=True, stop=True)
            t.matmul(B.PS_BIG[:, 512:640], qA, kA[:, k0 + 512:k0 + 640], start=True, stop=True)
            t.matmul(B.PS_BIG[:, 1024:1536], qB, kB[:, k0:k0 + 512], start=True, stop=True)
            t.matmul(B.PS_BIG[:, 1536:1664], qB, kB[:, k0 + 512:k0 + 640],
                     start=True, stop=True).then_inc(sT, 1)
            c.add(sT, 1)
        s_ok = c.cur(sT)

        @blk.vector
        def _(v, midx=midx, par=par, s_ok=s_ok, aw=c.cur(sA)):
            v.wait_ge(sT, s_ok)
            if aw:
                v.wait_ge(sA, aw)
            pview = B.PS_BIG.rearrange("p (h n) -> p h n", n=1024)
            last = v.tensor_tensor(B.Ssb[par][:, :, :], pview[:, :, 0:640],
                                   B.lmk[:, midx, :].unsqueeze(1).broadcast_to([P, 2, 640]),
                                   op=ALU.add)
            _fence(env, v, sV, last)
            v.reduce_max(B.nm2[par][:, 0:2], B.Ssb[par][:, :, :], axis=AX.X,
                         negate=True).then_inc(sV, 1)
            c.add(sV, 1)
        ttr_done[par] = c.cur(sV)

    def emit_exp(i):
        par = i % 2
        m_ok = ttr_done[par]

        @blk.scalar
        def _(sc, par=par, m_ok=m_ok, tw=max(trans_done), ow=o_done[par]):
            sc.wait_ge(sV, m_ok)
            if tw:
                sc.wait_ge(sT, tw)
            if ow:
                sc.wait_ge(sV, ow)
            sc.activation(out=B.Pb[par][:, 0, :], in_=B.Ssb[par][:, 0, :], func=AF.Exp,
                          bias=B.nm2[par][:, 0:1], accum_out=B.ll2[par][:, 0:1])
            sc.activation(out=B.Pb[par][:, 1, :], in_=B.Ssb[par][:, 1, :], func=AF.Exp,
                          bias=B.nm2[par][:, 1:2], accum_out=B.ll2[par][:, 1:2]
                          ).then_inc(sA, 1)
            c.add(sA, 1)
        exp_done[par] = c.cur(sA)

    def emit_tail(i):
        a, hp2, r_tile, midx = pairs[i]
        par = i % 2
        e_ok = exp_done[par]

        @blk.tensor
        def _(t, par=par, e_ok=e_ok, vw=c.cur(sV)):
            t.wait_ge(sA, e_ok)
            t.wait_ge(sV, vw)
            for i5 in range(5):
                t.transpose(B.PS_T2[:, i5 * 128:i5 * 128 + 128],
                            B.Pb[par][:, 0, i5 * 128:(i5 + 1) * 128], pt.identb[:])
            for i5 in range(5):
                last = t.transpose(B.PS_T2[:, 1024 + i5 * 128:1024 + i5 * 128 + 128],
                                   B.Pb[par][:, 1, i5 * 128:(i5 + 1) * 128], pt.identb[:])
            last.then_inc(sT, 1)
            c.add(sT, 1)
        trans_done[par] = c.cur(sT)
        tr_ok = trans_done[par]

        @blk.vector
        def _(v, par=par, tr_ok=tr_ok, pw=pv_done[par]):
            v.wait_ge(sT, tr_ok)
            if pw:
                v.wait_ge(sT, pw)
            tv = B.PS_T2.rearrange("p (h n) -> p h n", n=1024)
            v.tensor_copy(B.PTS[par][:, 0, :], tv[:, 0, 0:640])
            last = v.tensor_copy(B.PTS[par][:, 1, :], tv[:, 1, 0:640])
            _fence(env, v, sV, last)
            last = v.reciprocal(rli_ap(B, par), B.ll2[par][:])
            last.then_inc(sV, 1)
            c.add(sV, 1)
        pt_ok = c.cur(sV)

        @blk.tensor
        def _(t, a=a, hp2=hp2, r_tile=r_tile, par=par, pt_ok=pt_ok):
            t.wait_ge(sV, pt_ok)
            sub0 = (r_tile - 512) // 128
            for h2 in range(2):
                h = hp2 * 2 + h2
                for i5 in range(5):
                    last = t.matmul(B.PS_O[h2][:, 0:64],
                                    B.PTS[par][:, h2, i5 * 128:(i5 + 1) * 128],
                                    B.VT2[:, sub0 + i5, h * 64:(h + 1) * 64],
                                    start=(i5 == 0), stop=(i5 == 4))
            last.then_inc(sT, 1)
            c.add(sT, 1)
        pv_done[par] = c.cur(sT)
        pv_ok = pv_done[par]

        @blk.vector
        def _(v, a=a, hp2=hp2, par=par, pv_ok=pv_ok):
            v.wait_ge(sT, pv_ok)
            h0 = hp2 * 2
            v.tensor_scalar(B.o_loc[:, a, h0 * 64:(h0 + 1) * 64], B.PS_O[0][:, 0:64],
                            B.rli2[par][:, 0:1], None, op0=ALU.mult)
            v.tensor_scalar(B.o_loc[:, a, (h0 + 1) * 64:(h0 + 2) * 64], B.PS_O[1][:, 0:64],
                            B.rli2[par][:, 1:2], None, op0=ALU.mult).then_inc(sV, 1)
            c.add(sV, 1)
        o_done[par] = c.cur(sV)

    for i in range(len(pairs)):
        emit_S(i)
        if i >= 1:
            emit_tail(i - 1)
        emit_exp(i)
    if pairs:
        emit_tail(len(pairs) - 1)

    a0 = max(0, (qstart - r0) // 128)
    for a in range(a0, nt):
        ow = c.cur(sV)

        @blk.tensor
        def _(t, a=a, ow=ow):
            t.wait_ge(sV, ow)
            for oc4 in range(4):
                last = t.transpose(B.PS_T2[:, oc4 * 128:(oc4 + 1) * 128],
                                   B.o_loc[:, a, oc4 * 128:(oc4 + 1) * 128], pt.identb[:])
            last.then_inc(sT, 1)
            c.add(sT, 1)
        ot_ok = c.cur(sT)

        @blk.vector
        def _(v, a=a, ot_ok=ot_ok):
            v.wait_ge(sT, ot_ok)
            for oc4 in range(4):
                last = v.tensor_copy(B.oF[:, oc4, a * 128:(a + 1) * 128],
                                     B.PS_T2[:, oc4 * 128:(oc4 + 1) * 128])
            last.then_inc(sV, 1)
            c.add(sV, 1)
    of_ok = c.cur(sV)

    wbase = qstart - r0 if qstart > r0 else 0
    wn = nr - wbase
    if wn <= 0:
        return
    for oc in range(8):
        par = oc % 2
        psw = B.PS_BIG[:, par * 512:(par + 1) * 512]

        @blk.tensor
        def _(t, oc=oc, psw=psw, of_ok=of_ok, vw4=c.cur(sV)):
            t.wait_ge(sV, of_ok)
            t.wait_ge(sV, vw4)
            for kk in range(4):
                last = t.matmul(psw[:, 0:wn], B.WO2[:, kk, oc * 128:(oc + 1) * 128],
                                B.oF[:, kk, wbase:nr], start=(kk == 0), stop=(kk == 3))
            last.then_inc(sT, 1)
            c.add(sT, 1)
        wm_ok = c.cur(sT)

        @blk.vector
        def _(v, oc=oc, psw=psw, wm_ok=wm_ok, dw2=c.cur(sD)):
            v.wait_ge(sT, wm_ok)
            v.wait_ge(sD, dw2)
            v.tensor_add(B.ystg[:, 0:wn], psw[:, 0:wn],
                         B.yf32[:, oc, 1 + wbase:1 + nr]).then_inc(sV, 1)
            c.add(sV, 1)
        ad_ok = c.cur(sV)

        @blk.sync
        def _(s, oc=oc, ad_ok=ad_ok):
            s.wait_ge(sV, ad_ok)
            s.dma_start(out=Z["yA"][oc * 128:(oc + 1) * 128, r0 + wbase:r0 + nr],
                        in_=B.ystg[:, 0:wn]).then_inc(sD, 16)
            c.add(sD, 16)


def rli_ap(B, par):
    return B.rli2[par][:]


def _slA_norm_chunk(env, blk, l, isq, u, r0, nr, B, ld_ok):
    nc, I, Z, pt, c = env.nc, env.I, env.Z, env.pt, env.c
    sD, sG, sT, sV, sA = env.sD, env.sG, env.sT, env.sV, env.sA
    tct = B.xc[0] if isq else B.xc[2]
    tst = B.xc[1] if isq else B.xc[3]
    psSS = B.PS_BIG[:, 1024:1536]
    psBC = B.PS_BIG[:, 1536:2048]
    psROT = B.PS_O[1]

    aw0 = c.cur(sA)

    @blk.scalar
    def _(sc, u=u, aw0=aw0, tw0=c.cur(sT)):
        sc.wait_ge(sA, aw0)
        sc.wait_ge(sT, tw0)
        sc.activation(out=B.sqb[:, 0:nr], in_=B.qraw[:, u, 0:nr],
                      func=AF.Square).then_inc(sA, 1)
        c.add(sA, 1)
    sq_ok = c.cur(sA)

    @blk.tensor
    def _(t, u=u, sq_ok=sq_ok, vw=c.cur(sV)):
        t.wait_ge(sA, sq_ok)
        t.wait_ge(sV, vw)
        t.matmul(psSS[0:2, 0:nr], pt.segb[:, 0:2], B.sqb[:, 0:nr],
                 start=True, stop=True).then_inc(sT, 1)
        c.add(sT, 1)
    ssm_ok = c.cur(sT)

    @blk.vector
    def _(v, ssm_ok=ssm_ok, aw=c.cur(sA), tw1=c.cur(sT)):
        v.wait_ge(sT, max(ssm_ok, tw1))
        v.wait_ge(sA, aw)
        v.tensor_scalar_max(B.rr[0:2, 0:nr], psSS[0:2, 0:nr], 1e-24).then_inc(sV, 1)
        c.add(sV, 1)
    clip_ok = c.cur(sV)

    @blk.scalar
    def _(sc, clip_ok=clip_ok):
        sc.wait_ge(sV, clip_ok)
        last = sc.activation(out=B.rr[0:2, 0:nr], in_=B.rr[0:2, 0:nr], func=AF.Ln)
        _fence(env, sc, sA, last)
        sc.activation(out=B.rr[0:2, 0:nr], in_=B.rr[0:2, 0:nr], func=AF.Exp,
                      scale=-0.5).then_inc(sA, 1)
        c.add(sA, 1)
    rr_ok = c.cur(sA)

    @blk.tensor
    def _(t, rr_ok=rr_ok, vw2=c.cur(sV)):
        t.wait_ge(sA, rr_ok)
        t.wait_ge(sV, vw2)
        t.matmul(psBC[:, 0:nr], pt.segT[0:2, :], B.rr[0:2, 0:nr],
                 start=True, stop=True).then_inc(sT, 1)
        c.add(sT, 1)
    bc_ok = c.cur(sT)

    @blk.vector
    def _(v, u=u, isq=isq, bc_ok=bc_ok, ld2=ld_ok[0], tw2=c.cur(sT)):
        v.wait_ge(sT, bc_ok)
        v.wait_ge(sD, ld2)
        sc_ap = pt.qs8[:, l:l + 1] if isq else pt.ksc[:, l:l + 1]
        v.scalar_tensor_tensor(out=B.qnrm[:, 0:nr], in0=B.qraw[:, u, 0:nr],
                               scalar=sc_ap, in1=psBC[:, 0:nr],
                               op0=ALU.mult, op1=ALU.mult).then_inc(sV, 1)
        c.add(sV, 1)
    qn_ok = c.cur(sV)

    @blk.tensor
    def _(t, qn_ok=qn_ok):
        t.wait_ge(sV, qn_ok)
        t.matmul(psROT[:, 0:nr], pt.perm32[:], B.qnrm[:, 0:nr],
                 start=True, stop=True).then_inc(sT, 1)
        c.add(sT, 1)
    rot_ok = c.cur(sT)

    @blk.vector
    def _(v, u=u, isq=isq, r0=r0, rot_ok=rot_ok, tct=tct, tst=tst):
        v.wait_ge(sT, rot_ok)
        v.tensor_tensor(B.gx1[:, 0:nr], B.qnrm[:, 0:nr], tct[:, 0:nr], op=ALU.mult)
        last = v.tensor_tensor(B.gx2[:, 0:nr], psROT[:, 0:nr], tst[:, 0:nr], op=ALU.mult)
        _fence(env, v, sV, last)
        dst = B.QF2[:, u, 0:nr] if isq else B.KF2[:, u, r0:r0 + nr]
        v.tensor_add(dst, B.gx1[:, 0:nr], B.gx2[:, 0:nr]).then_inc(sV, 1)
        c.add(sV, 1)


def _emit_slF(env, l):
    nc, I, Z, pt, c = env.nc, env.I, env.Z, env.pt, env.c
    sD, sG, sT, sV, sA = env.sD, env.sG, env.sT, env.sV, env.sA

    if l == 0:
        groups = [(512, 512), (1024, 128)] + [(1152 + 512 * g, 512) for g in range(8)]
    else:
        groups = [(1152 + 512 * g, 512) for g in range(8)]

    with contextlib.ExitStack() as ph:
        sb = lambda n, s, d: ph.enter_context(nc.sbuf_tensor(f"l{l}{n}", s, d))
        W1 = sb("sf_w1", [P, 8, W1C], BF16)
        W2 = sb("sf_w2", [P, 22, D], BF16)
        ysh = sb("sf_ysh", [P, 8, 512], F32)
        sq8 = sb("sf_sq8", [P, 8, 512], BF16)
        yn2 = sb("sf_yn2", [P, 8, 512], BF16)
        tshb = sb("sf_tsh", [P, 22, 512], BF16)
        gelb = [sb(f"sf_gel{i}", [P, 512], BF16) for i in range(2)]
        yinc = [sb(f"sf_yin{i}", [P, 512], F32) for i in range(2)]
        ystg = [sb(f"sf_ystg{i}", [P, 512], F32) for i in range(2)]
        murow = sb("sf_mu", [1, 512], F32)
        v1row = sb("sf_v1", [1, 512], F32)
        varrow = sb("sf_var", [1, 512], F32)
        psS1 = ph.enter_context(nc.psum_tensor(f"l{l}sf_psS1", [P, 512], F32))
        psS2 = ph.enter_context(nc.psum_tensor(f"l{l}sf_psS2", [P, 512], F32))
        psBC = ph.enter_context(nc.psum_tensor(f"l{l}sf_psBC", [P, 1024], F32))
        psA1 = [ph.enter_context(nc.psum_tensor(f"l{l}sf_psA{i}", [P, 512], F32))
                for i in range(2)]
        psG = [ph.enter_context(nc.psum_tensor(f"l{l}sf_psG{i}", [P, 512], F32))
               for i in range(2)]

        with _PhaseCtx(env) as blk:
            snap = env.snap
            _phase_barrier(env, blk, snap)

            @blk.sync
            def _(s):
                _barrier_waits(env, s, snap)
                s.dma_start(out=W1[:], in_=Z["w12_bf"][l].rearrange("(c p) n -> p c n", p=P)
                            ).then_inc(sD, 16)
                c.add(sD, 16)
                s.dma_start(out=W2[:, 0:21, :],
                            in_=Z["w22_bf"][l][0:2688].rearrange("(c p) n -> p c n", p=P)
                            ).then_inc(sD, 16)
                c.add(sD, 16)
                s.dma_start(out=W2[0:42, 21, :], in_=Z["w22_bf"][l][2688:2730, :]
                            ).then_inc(sD, 16)
                c.add(sD, 16)

            prev = None
            for (r0, nr) in groups:
                prev = _slF_group(env, blk, l, r0, nr, W1, W2, ysh, sq8, yn2, tshb,
                                  gelb, yinc, ystg, murow, v1row, varrow,
                                  psS1, psS2, psBC, psA1, psG, prev)

    env.snap = c.snap()


def _slF_group(env, blk, l, r0, nr, W1, W2, ysh, sq8, yn2, tshb, gelb, yinc,
               ystg, murow, v1row, varrow, psS1, psS2, psBC, psA1, psG, prev):
    nc, I, Z, pt, c = env.nc, env.I, env.Z, env.pt, env.c
    sD, sG, sT, sV, sA = env.sD, env.sG, env.sT, env.sV, env.sA

    if prev is None:
        tprev, vprev, aprev = c.cur(sT), c.cur(sV), c.cur(sA)
    else:
        tprev, vprev, aprev = prev["ysh_T"], prev["ysh_V"], prev["ysh_A"]

    @blk.sync
    def _(s):
        s.wait_ge(sT, tprev)
        s.wait_ge(sV, vprev)
        s.wait_ge(sA, aprev)
        s.dma_start(out=ysh[:, 0:4, 0:nr],
                    in_=Z["yA"][0:512, r0:r0 + nr].rearrange("(c p) n -> p c n", p=P)
                    ).then_inc(sD, 16)
        c.add(sD, 16)
        s.dma_start(out=ysh[:, 4:8, 0:nr],
                    in_=Z["yA"][512:1024, r0 - 1:r0 + nr - 1].rearrange("(c p) n -> p c n", p=P)
                    ).then_inc(sD, 16)
        c.add(sD, 16)
    ld_ok = c.cur(sD)

    @blk.scalar
    def _(sc, ld_ok=ld_ok, tw=(prev or {}).get("sq8_T", 0)):
        sc.wait_ge(sD, ld_ok)
        if tw:
            sc.wait_ge(sT, tw)
        for f in range(8):
            last = sc.activation(out=sq8[:, f, 0:nr], in_=ysh[:, f, 0:nr],
                                 func=AF.Square)
        last.then_inc(sA, 1)
        c.add(sA, 1)
    sq_ok = c.cur(sA)

    @blk.tensor
    def _(t, ld_ok=ld_ok, sq_ok=sq_ok, vw=(prev or {}).get("stats_V", 0)):
        t.wait_ge(sD, ld_ok)
        t.wait_ge(sA, sq_ok)
        if vw:
            t.wait_ge(sV, vw)
        for f in range(8):
            t.matmul(psS1[0:1, 0:nr], pt.onesc[:, 0:1], ysh[:, f, 0:nr],
                     start=(f == 0), stop=(f == 7))
        for f in range(8):
            last = t.matmul(psS2[0:1, 0:nr], pt.onesb[:, 0:1], sq8[:, f, 0:nr],
                            start=(f == 0), stop=(f == 7))
        last.then_inc(sT, 1)
        c.add(sT, 1)
    sums_ok = c.cur(sT)
    sq8_T = sums_ok

    @blk.vector
    def _(v, sums_ok=sums_ok, tw2=(prev or {}).get("rows_T", 0)):
        v.wait_ge(sT, sums_ok)
        if tw2:
            v.wait_ge(sT, tw2)
        v.tensor_scalar_mul(murow[0:1, 0:nr], psS1[0:1, 0:nr], 1.0 / D)
        last = v.tensor_scalar_mul(v1row[0:1, 0:nr], psS2[0:1, 0:nr], 1.0 / D)
        _fence(env, v, sV, last)
        last = v.tensor_tensor(varrow[0:1, 0:nr], murow[0:1, 0:nr], murow[0:1, 0:nr],
                               op=ALU.mult)
        _fence(env, v, sV, last)
        v.tensor_tensor(varrow[0:1, 0:nr], v1row[0:1, 0:nr], varrow[0:1, 0:nr],
                        op=ALU.subtract).then_inc(sV, 1)
        c.add(sV, 1)
    var_ok = c.cur(sV)
    stats_V = var_ok

    @blk.scalar
    def _(sc, var_ok=var_ok):
        sc.wait_ge(sV, var_ok)
        last = sc.activation(out=varrow[0:1, 0:nr], in_=varrow[0:1, 0:nr],
                             func=AF.Ln, bias=pt.eps[0:1, 0:1])
        _fence(env, sc, sA, last)
        sc.activation(out=varrow[0:1, 0:nr], in_=varrow[0:1, 0:nr],
                      func=AF.Exp, scale=-0.5).then_inc(sA, 1)
        c.add(sA, 1)
    rstd_ok = c.cur(sA)

    @blk.tensor
    def _(t, rstd_ok=rstd_ok, vw2=(prev or {}).get("bc_V", 0)):
        t.wait_ge(sA, rstd_ok)
        if vw2:
            t.wait_ge(sV, vw2)
        t.matmul(psBC[:, 0:nr], pt.ones1r[0:1, :], murow[0:1, 0:nr],
                 start=True, stop=True)
        t.matmul(psBC[:, 512:512 + nr], pt.ones1r[0:1, :], varrow[0:1, 0:nr],
                 start=True, stop=True).then_inc(sT, 1)
        c.add(sT, 1)
    bc_ok = c.cur(sT)
    rows_T = bc_ok

    @blk.vector
    def _(v, bc_ok=bc_ok, tw3=(prev or {}).get("yn2_T", 0)):
        v.wait_ge(sT, bc_ok)
        if tw3:
            v.wait_ge(sT, tw3)
        for f in range(8):
            last = v.tensor_tensor(ysh[:, f, 0:nr], ysh[:, f, 0:nr], psBC[:, 0:nr],
                                   op=ALU.subtract)
        _fence(env, v, sV, last)
        for f in range(8):
            last = v.scalar_tensor_tensor(out=yn2[:, f, 0:nr], in0=ysh[:, f, 0:nr],
                                          scalar=pt.lng[:, l, f:f + 1],
                                          in1=psBC[:, 512:512 + nr],
                                          op0=ALU.mult, op1=ALU.mult)
        _fence(env, v, sV, last)
        for f in range(8):
            last = v.tensor_scalar(yn2[:, f, 0:nr], yn2[:, f, 0:nr],
                                   pt.lnb[:, l, f:f + 1], None, op0=ALU.add)
        last.then_inc(sV, 1)
        c.add(sV, 1)
    yn_ok = c.cur(sV)
    bc_V = yn_ok
    ysh_V = yn_ok

    gel_done = [0, 0]
    gelu_done = [0, 0]
    for j in range(22):
        par = j % 2
        mj = 128 if j < 21 else 42
        ac0 = j * 128
        gc0 = HID + j * 128

        @blk.tensor
        def _(t, j=j, par=par, mj=mj, ac0=ac0, gc0=gc0, yn_ok=yn_ok,
              aw=gelu_done[par], vw3=gel_done[par]):
            t.wait_ge(sV, yn_ok)
            if aw:
                t.wait_ge(sA, aw)
            if vw3:
                t.wait_ge(sV, vw3)
            for f in range(8):
                t.matmul(psA1[par][0:mj, 0:nr], W1[:, f, ac0:ac0 + mj],
                         yn2[:, f, 0:nr], start=(f == 0), stop=(f == 7))
            for f in range(8):
                last = t.matmul(psG[par][0:mj, 0:nr], W1[:, f, gc0:gc0 + mj],
                                yn2[:, f, 0:nr], start=(f == 0), stop=(f == 7))
            last.then_inc(sT, 1)
            c.add(sT, 1)
        w1_ok = c.cur(sT)

        @blk.scalar
        def _(sc, j=j, par=par, mj=mj, w1_ok=w1_ok, vw4=gel_done[par]):
            sc.wait_ge(sT, w1_ok)
            if vw4:
                sc.wait_ge(sV, vw4)
            sc.activation(out=gelb[par][0:mj, 0:nr], in_=psG[par][0:mj, 0:nr],
                          func=AF.Gelu,
                          bias=pt.b1s[0:mj, l, 22 + j:23 + j]).then_inc(sA, 1)
            c.add(sA, 1)
        gelu_done[par] = c.cur(sA)
        gel_ok = gelu_done[par]

        @blk.vector
        def _(v, j=j, par=par, mj=mj, gel_ok=gel_ok):
            v.wait_ge(sA, gel_ok)
            v.scalar_tensor_tensor(out=tshb[0:mj, j, 0:nr], in0=psA1[par][0:mj, 0:nr],
                                   scalar=pt.b1s[0:mj, l, j:j + 1],
                                   in1=gelb[par][0:mj, 0:nr],
                                   op0=ALU.add, op1=ALU.mult).then_inc(sV, 1)
            c.add(sV, 1)
        gel_done[par] = c.cur(sV)
    t_ok = c.cur(sV)
    yn2_T = c.cur(sT)

    w2v = [gel_done[0], gel_done[1]]
    ystg_store = [0, 0]
    for oc in range(8):
        par = oc % 2

        @blk.sync
        def _(s, oc=oc, par=par, vw5=w2v[par]):
            if vw5:
                s.wait_ge(sV, vw5)
            s.dma_start(out=yinc[par][:, 0:nr],
                        in_=Z["yA"][oc * 128:(oc + 1) * 128, r0:r0 + nr]
                        ).then_inc(sD, 16)
            c.add(sD, 16)
        yi_ok = c.cur(sD)

        @blk.tensor
        def _(t, oc=oc, par=par, t_ok=t_ok, vw6=w2v[par]):
            t.wait_ge(sV, t_ok)
            if vw6:
                t.wait_ge(sV, vw6)
            for j in range(22):
                mj = 128 if j < 21 else 42
                last = t.matmul(psA1[par][:, 0:nr], W2[0:mj, j, oc * 128:(oc + 1) * 128],
                                tshb[0:mj, j, 0:nr], start=(j == 0), stop=(j == 21))
            last.then_inc(sT, 1)
            c.add(sT, 1)
        w2_ok = c.cur(sT)

        @blk.vector
        def _(v, oc=oc, par=par, w2_ok=w2_ok, yi_ok=yi_ok, sw=ystg_store[par]):
            v.wait_ge(sT, w2_ok)
            v.wait_ge(sD, max(yi_ok, sw))
            last = v.scalar_tensor_tensor(out=ystg[par][:, 0:nr], in0=psA1[par][:, 0:nr],
                                          scalar=pt.b2s[:, l, oc:oc + 1],
                                          in1=yinc[par][:, 0:nr],
                                          op0=ALU.add, op1=ALU.add)
            last.then_inc(sV, 1)
            c.add(sV, 1)
        w2v[par] = c.cur(sV)
        st_ok = w2v[par]

        @blk.sync
        def _(s, oc=oc, par=par, st_ok=st_ok):
            s.wait_ge(sV, st_ok)
            s.dma_start(out=Z["yB"][oc * 128:(oc + 1) * 128, r0:r0 + nr],
                        in_=ystg[par][:, 0:nr]).then_inc(sD, 16)
            c.add(sD, 16)
        ystg_store[par] = c.cur(sD)

    return {
        "ysh_T": sums_ok, "ysh_V": ysh_V, "ysh_A": sq_ok,
        "sq8_T": sq8_T, "stats_V": stats_V, "rows_T": rows_T,
        "bc_V": bc_V, "yn2_T": yn2_T,
    }


def _emit_final(env):
    nc, I, Z, pt, c = env.nc, env.I, env.Z, env.pt, env.c
    sD, sG, sT, sV, sA = env.sD, env.sG, env.sT, env.sV, env.sA

    with contextlib.ExitStack() as ph:
        sb = lambda n, s, d: ph.enter_context(nc.sbuf_tensor(n, s, d))
        yb = sb("fn_y", [P, 8, 512], F32)
        sfb = sb("fn_s", [P, 8, 512], F32)
        gch = sb("fn_g", [P, 8, 512], F32)
        xtk = sb("fn_x", [P, 4, 1024], F32)
        osb = sb("fn_o", [P, 1024], F32)
        psF = ph.enter_context(nc.psum_tensor("fn_psF", [P, 1024], F32))

        with _PhaseCtx(env) as blk:
            snap = env.snap
            _phase_barrier(env, blk, snap)
            first = [True]
            for g in range(8):
                c0 = H + g * 512
                tprev, vprev, aprev = c.cur(sT), c.cur(sV), c.cur(sA)

                @blk.sync
                def _(s, g=g, c0=c0, tprev=tprev, vprev=vprev, aprev=aprev,
                      fst=first[0]):
                    if fst:
                        _barrier_waits(env, s, snap)
                    s.wait_ge(sT, tprev)
                    s.wait_ge(sV, vprev)
                    s.wait_ge(sA, aprev)
                    s.dma_start(out=yb[:], in_=Z["yB"][:, c0:c0 + 512]
                                .rearrange("(c p) n -> p c n", p=P)).then_inc(sD, 16)
                    c.add(sD, 16)
                    s.dma_start(out=sfb[:], in_=Z["sF"][:, g * 512:(g + 1) * 512]
                                .rearrange("(c p) n -> p c n", p=P)).then_inc(sD, 16)
                    c.add(sD, 16)
                    s.dma_start(out=xtk[:], in_=I["x_oh"][c0:c0 + 512, :]
                                .rearrange("(a p) f -> p a f", p=P)).then_inc(sD, 16)
                    c.add(sD, 16)
                first[0] = False
                ld_ok = c.cur(sD)

                @blk.scalar
                def _(sc, ld_ok=ld_ok):
                    sc.wait_ge(sD, ld_ok)
                    for f in range(8):
                        last = sc.activation(out=gch[:, f, :], in_=yb[:, f, :],
                                             func=AF.Silu)
                    last.then_inc(sA, 1)
                    c.add(sA, 1)
                si_ok = c.cur(sA)

                @blk.vector
                def _(v, si_ok=si_ok):
                    v.wait_ge(sA, si_ok)
                    for f in range(8):
                        last = v.tensor_tensor(gch[:, f, :], gch[:, f, :],
                                               sfb[:, f, :], op=ALU.mult)
                    last.then_inc(sV, 1)
                    c.add(sV, 1)
                t_ok = c.cur(sV)

                for a in range(4):
                    vw = c.cur(sV)

                    @blk.tensor
                    def _(t, a=a, t_ok=t_ok, vw=vw):
                        t.wait_ge(sV, vw)
                        for f in range(8):
                            last = t.transpose(psF[:, f * 128:(f + 1) * 128],
                                               gch[:, f, a * 128:(a + 1) * 128],
                                               pt.ident3[:])
                        last.then_inc(sT, 1)
                        c.add(sT, 1)
                    tr_ok = c.cur(sT)
                    dw = c.cur(sD)

                    @blk.vector
                    def _(v, a=a, tr_ok=tr_ok, dw=dw):
                        v.wait_ge(sT, tr_ok)
                        v.wait_ge(sD, dw)
                        v.tensor_tensor(osb[:], psF[:, 0:1024], xtk[:, a, :],
                                        op=ALU.add).then_inc(sV, 1)
                        c.add(sV, 1)
                    ev_ok = c.cur(sV)

                    @blk.sync
                    def _(s, g=g, a=a, ev_ok=ev_ok):
                        s.wait_ge(sV, ev_ok)
                        s.dma_start(out=env.out_o[g * 512 + a * 128:
                                                  g * 512 + (a + 1) * 128, :],
                                    in_=osb[:]).then_inc(sD, 16)
                        c.add(sD, 16)

    env.snap = c.snap()


def _emit_tail(env):
    nc, c = env.nc, env.c
    with _PhaseCtx(env) as blk:
        @blk.sync
        def _(s):
            s.wait_ge(env.sD, c.cur(env.sD))
            s.wait_ge(env.sG, c.cur(env.sG))


def pk(vec, ncol):
    return np.ascontiguousarray(np.asarray(vec, np.float32).reshape(ncol, P).T)


def pk8(vec):
    return pk(vec, 8)


def pk2x8(arr):
    a = np.asarray(arr, np.float32).reshape(2, 8, P)
    return np.ascontiguousarray(a.transpose(2, 0, 1))


def pk_head(arr):
    a = np.asarray(arr, np.float32)
    return np.ascontiguousarray(np.concatenate([a, a], axis=1).T)


def pk_b1(arr):
    out = np.zeros((P, 2, 44), np.float32)
    a = np.asarray(arr, np.float32)
    for l in range(2):
        for half in range(2):
            seg = a[l, half * 2730:(half + 1) * 2730]
            full = seg[0:2688].reshape(21, P).T
            out[:, l, half * 22:half * 22 + 21] = full
            out[0:42, l, half * 22 + 21] = seg[2688:2730]
    return out


def _in_maps(inputs):
    t = _tables()
    x = np.ascontiguousarray(inputs["L0_attn_x"], dtype=np.float32)
    maps = []
    for core in range(NC):
        b, hf = core // 2, core % 2
        t0 = hf * TOWN
        xb = x[b]
        if hf == 0:
            x_oh = np.concatenate([np.zeros((H, D), np.float32), xb[0:TOWN]], axis=0)
            x_oth = xb[TOWN:]
            rc_own = t["rc"][:, 0:TOWN]
            rs_own = t["rs"][:, 0:TOWN]
            rc_oth = t["rc"][:, TOWN:]
            rs_oth = t["rs"][:, TOWN:]
            xslc = slice(H - H, H - H + OH)
            lm = np.concatenate([t["mask_first"], t["mask_plain"][None]], axis=0)
            gate = np.zeros((P, 1), np.float32)
        else:
            x_oh = xb[TOWN - H:]
            x_oth = xb[0:TOWN]
            rc_own = t["rc"][:, TOWN:]
            rs_own = t["rs"][:, TOWN:]
            rc_oth = t["rc"][:, 0:TOWN]
            rs_oth = t["rs"][:, 0:TOWN]
            xslc = slice(TOWN, TOWN + OH)
            lm = np.concatenate([np.broadcast_to(t["mask_plain"], (4, P, 640)),
                                 t["mask_plain"][None]], axis=0)
            gate = np.ones((P, 1), np.float32)
        xoff = t0
        m = {
            "x_oh": np.ascontiguousarray(x_oh),
            "x_oth": np.ascontiguousarray(x_oth),
            "gnorm": pk8(inputs["L0_attn_g_norm"]),
            "wqkvA": np.ascontiguousarray(inputs["L0_attn_w_qkv"], np.float32),
            "bqk_t": pk(np.asarray(inputs["L0_attn_b_qkv"], np.float32)[0:2048], 16),
            "bqv_row": np.ascontiguousarray(
                np.asarray(inputs["L0_attn_b_qkv"], np.float32)[None, 2048:3072]),
            "wprojA": np.ascontiguousarray(inputs["L0_attn_w_proj"], np.float32),
            "bprojA": pk8(inputs["L0_attn_b_proj"]),
            "rc_own": np.ascontiguousarray(rc_own),
            "rs_own": np.ascontiguousarray(rs_own),
            "rc_oth": np.ascontiguousarray(rc_oth),
            "rs_oth": np.ascontiguousarray(rs_oth),
            "xqc": np.ascontiguousarray(t["xqc"][:, xoff:xoff + OH]),
            "xqs": np.ascontiguousarray(t["xqs"][:, xoff:xoff + OH]),
            "xkc": np.ascontiguousarray(t["xkc"][:, xoff:xoff + OH]),
            "xks": np.ascontiguousarray(t["xks"][:, xoff:xoff + OH]),
            "lmask": np.ascontiguousarray(lm),
            "segbf": t["segbf"],
            "perm32": t["perm32"],
            "segT": t["segT"],
            "gate": gate,
            "wqkv2": np.ascontiguousarray(inputs["L0_ffn_wqkv"], np.float32),
            "qsc": pk_head(inputs["L0_ffn_q_scale"]),
            "ksc": pk_head(inputs["L0_ffn_k_scale"]),
            "wo2": np.ascontiguousarray(inputs["L0_ffn_wo"], np.float32),
            "lng": pk2x8(inputs["L0_ffn_ln_g"]),
            "lnb": pk2x8(inputs["L0_ffn_ln_b"]),
            "w12": np.ascontiguousarray(inputs["L0_ffn_w1"], np.float32),
            "b12": pk_b1(inputs["L0_ffn_b1"]),
            "w22": np.ascontiguousarray(inputs["L0_ffn_w2"], np.float32),
            "b22": pk2x8(inputs["L0_ffn_b2"]),
        }
        maps.append(m)
    return maps


def kernel(**inputs):
    maps = _in_maps(inputs)
    nc = build_program()
    res = run_bass_kernel_spmd(nc, maps, core_ids=list(range(NC)))
    out = np.empty((NB, S, D), np.float32)
    for core in range(NC):
        b, hf = core // 2, core % 2
        out[b, hf * TOWN:(hf + 1) * TOWN] = res.results[core]["out_o"]
    return out


# version 30
# speedup vs baseline: 1.0395x; 1.0395x over previous
import contextlib
import math
import numpy as np
import ml_dtypes
import concourse.bass as bass
import concourse.mybir as mybir
from concourse.bass_utils import run_bass_kernel_spmd

F32 = mybir.dt.float32
BF16 = mybir.dt.bfloat16
AF = mybir.ActivationFunctionType
ALU = mybir.AluOpType
AX = mybir.AxisListType

P = 128
D = 1024
NB = 4
S = 8192
TOWN = 4096
H = 1152
OH = H + TOWN
NC = 8
C_A = float(np.float32(1.0986122886681098) / np.float32(32.0))
HID = 2730
W1C = 5460
INNER = 512
DHEAD = 64
NHEAD = 8
WIN = 512

N_PHASES = 5


_TBL = {}


def _tables():
    if _TBL:
        return _TBL
    import jax
    import jax.numpy as jnp
    cpu = jax.devices("cpu")[0]
    with jax.default_device(cpu):
        _tables_jax(jnp)
    return _TBL


def _tables_jax(jnp):
    import jax
    half = D // 2
    inv_freq = 1.0 / 10000.0 ** (jnp.arange(half, dtype=jnp.float32) / half)
    ang = jnp.arange(S, dtype=jnp.float32)[:, None] * inv_freq[None, :]
    cos = jnp.round(jnp.cos(ang))
    sin = jnp.round(jnp.sin(ang))
    C = jnp.concatenate([cos, cos], axis=-1)
    Sg = jnp.concatenate([sin, sin], axis=-1)
    Sf = np.asarray(Sg, np.float32).copy()
    Sf[:, :half] = -Sf[:, :half]
    rc = np.asarray(C, np.float32).T.astype(ml_dtypes.bfloat16)
    rs = Sf.T.astype(ml_dtypes.bfloat16)
    _TBL["rc"] = np.ascontiguousarray(rc)
    _TBL["rs"] = np.ascontiguousarray(rs)

    n = S
    half_idx = jnp.arange(0, DHEAD, 2, dtype=jnp.float32)
    inv_f = 1.0 / 10000.0 ** (half_idx / DHEAD)
    t_ext = jnp.arange(-H, n, dtype=jnp.float32)
    freqs = t_ext[:, None] * inv_f[None, :]
    freqs = jnp.concatenate([freqs, freqs], axis=-1)
    base = (half_idx + 0.4 * DHEAD) / (1.4 * DHEAD)
    power = (t_ext - n // 2) / float(WIN // 2)
    scale = base[None, :] ** power[:, None]
    scale = jnp.concatenate([scale, scale], axis=-1)
    cosx = jnp.cos(freqs)
    sinx = jnp.sin(freqs)
    qc = np.array(cosx * scale, np.float32)
    qs = np.array(sinx * scale, np.float32)
    kc = np.array(cosx / scale, np.float32)
    ks = np.array(sinx / scale, np.float32)
    for t in (qs, ks):
        t[:, :32] = -t[:, :32]
    def fm(t):
        tt = t.T
        return np.ascontiguousarray(np.concatenate([tt, tt], axis=0).astype(np.float32))
    _TBL["xqc"] = fm(qc)
    _TBL["xqs"] = fm(qs)
    _TBL["xkc"] = fm(kc)
    _TBL["xks"] = fm(ks)

    def band_mask(tbase):
        m = np.zeros((128, 640), np.float32)
        pp = np.arange(128)[:, None]
        cc = np.arange(640)[None, :]
        valid = (cc >= pp) & (cc <= pp + 512)
        if tbase < 512:
            valid &= (tbase - 512 + cc) >= 0
        m[~valid] = -1e30
        return m
    _TBL["mask_plain"] = band_mask(10_000)
    _TBL["mask_first"] = np.stack([band_mask(i * 128) for i in range(4)])

    pm = np.zeros((128, 128), np.float32)
    for kk in range(128):
        pm[kk, kk ^ 32] = 1.0
    _TBL["perm32"] = pm.astype(ml_dtypes.bfloat16)

    seg = np.zeros((128, 2), np.float32)
    seg[:64, 0] = 1.0
    seg[64:, 1] = 1.0
    _TBL["segbf"] = seg.astype(ml_dtypes.bfloat16)
    _TBL["segT"] = np.ascontiguousarray(seg.T)


class Ctr:
    def __init__(self):
        self.v = {}

    def add(self, sem, n):
        self.v[sem.name] = self.v.get(sem.name, 0) + n
        return self.v[sem.name]

    def cur(self, sem):
        return self.v.get(sem.name, 0)

    def snap(self):
        return dict(self.v)


class Env:
    pass


def _mk_env(nc, es):
    env = Env()
    env.nc = nc
    env.es = es
    sem = lambda n: es.enter_context(nc.semaphore(n))
    env.sD = sem("sD")
    env.sG = sem("sG")
    env.sT = sem("sT")
    env.sV = sem("sV")
    env.sA = sem("sA")
    env.c = Ctr()
    return env


def _fence(env, eng, sem, instr):
    v = env.c.add(sem, 1)
    instr.then_inc(sem, 1)
    eng.wait_ge(sem, v)


def _phase_barrier(env, blk, snap):
    for e in ("sync", "gpsimd", "tensor", "vector", "scalar"):
        rec = blk.recs[e]
        for s in (env.sD, env.sG, env.sT, env.sV, env.sA):
            v = snap.get(s.name, 0)
            if v:
                rec.calls.append((False, None, "wait_ge", (s, v), {}))


def _barrier_waits(env, eng, snap):
    for s in (env.sD, env.sG, env.sT, env.sV, env.sA):
        v = snap.get(s.name, 0)
        if v:
            eng.wait_ge(s, v)


class _ChainProxy:
    def __init__(self, rec, idx):
        self.rec = rec
        self.idx = idx

    def then_inc(self, sem, n):
        self.rec.calls.append((True, self.idx, "then_inc", (sem, n), {}))
        return self


class _Rec:
    def __init__(self):
        self.calls = []

    def __getattr__(self, name):
        rec = self

        def m(*a, **k):
            rec.calls.append((False, None, name, a, k))
            return _ChainProxy(rec, len(rec.calls) - 1)
        return m

    def replay(self, eng):
        res = []
        for chain, idx, name, a, k in self.calls:
            if chain:
                res.append(res[idx].then_inc(*a))
            else:
                res.append(getattr(eng, name)(*a, **k))


class DeferBlk:
    _ENGS = ("sync", "gpsimd", "tensor", "vector", "scalar")

    def __init__(self):
        self.recs = {e: _Rec() for e in self._ENGS}

    def _dec(self, e):
        def deco(f):
            f(self.recs[e])
        return deco

    @property
    def sync(self):
        return self._dec("sync")

    @property
    def gpsimd(self):
        return self._dec("gpsimd")

    @property
    def tensor(self):
        return self._dec("tensor")

    @property
    def vector(self):
        return self._dec("vector")

    @property
    def scalar(self):
        return self._dec("scalar")

    def flush(self, blk):
        for e in self._ENGS:
            rec = self.recs[e]
            if rec.calls:
                getattr(blk, e)(lambda eng, r=rec: r.replay(eng))


class _PhaseCtx:
    def __init__(self, env):
        self.env = env

    def __enter__(self):
        return self.env.dblk

    def __exit__(self, et, ev, tb):
        return False


class _DeferCtx:
    def __init__(self, nc, name, **kw):
        self.nc = nc
        self.name = name
        self.kw = kw
        self.d = DeferBlk()

    def __enter__(self):
        return self.d

    def __exit__(self, et, ev, tb):
        if et is None:
            with self.nc.Block(self.name, **self.kw) as b:
                self.d.flush(b)
        return False


def build_program(taps=()):
    nc = bass.Bass("TRN2", target_bir_lowering=False)
    es = contextlib.ExitStack()
    env = _mk_env(nc, es)
    env.dblk = DeferBlk()
    env.taps = set(taps)

    di = lambda n, s, d=F32: nc.dram_tensor(n, s, d, kind="ExternalInput").ap()
    I = {}
    I["x_oh"] = di("x_oh", [OH, D])
    I["x_oth"] = di("x_oth", [TOWN, D])
    I["gnorm"] = di("gnorm", [P, 8])
    I["wqkvA"] = di("wqkvA", [D, 3 * D])
    I["bqk_t"] = di("bqk_t", [P, 16])
    I["bqv_row"] = di("bqv_row", [1, D])
    I["wprojA"] = di("wprojA", [D, D])
    I["bprojA"] = di("bprojA", [P, 8])
    I["rc_own"] = di("rc_own", [D, TOWN], BF16)
    I["rs_own"] = di("rs_own", [D, TOWN], BF16)
    I["rc_oth"] = di("rc_oth", [D, TOWN], BF16)
    I["rs_oth"] = di("rs_oth", [D, TOWN], BF16)
    I["xqc"] = di("xqc", [P, OH])
    I["xqs"] = di("xqs", [P, OH])
    I["xkc"] = di("xkc", [P, OH])
    I["xks"] = di("xks", [P, OH])
    I["lmask"] = di("lmask", [5, P, 640])
    I["segbf"] = di("segbf", [P, 2], BF16)
    I["perm32"] = di("perm32", [P, P], BF16)
    I["segT"] = di("segT", [2, P])
    I["gate"] = di("gate", [P, 1])
    I["wqkv2"] = di("wqkv2", [2, D, 3 * INNER])
    I["qsc"] = di("qsc", [P, 2])
    I["ksc"] = di("ksc", [P, 2])
    I["wo2"] = di("wo2", [2, INNER, D])
    I["lng"] = di("lng", [P, 2, 8])
    I["lnb"] = di("lnb", [P, 2, 8])
    I["w12"] = di("w12", [2, D, W1C])
    I["b12"] = di("b12", [P, 2, 44])
    I["w22"] = di("w22", [2, HID, D])
    I["b22"] = di("b22", [P, 2, 8])
    env.I = I

    out_o = nc.dram_tensor("out_o", [TOWN, D], F32, kind="ExternalOutput").ap()

    def scratch(name, shape, dt):
        kind = "ExternalOutput" if name in env.taps else "Internal"
        return nc.dram_tensor(name, shape, dt, kind=kind).ap()

    Z = {}
    Z["xnF"] = scratch("xnF", [D, OH], F32)
    Z["yA"] = scratch("yA", [D, OH], F32)
    Z["yB"] = scratch("yB", [D, OH], F32)
    Z["qfA"] = scratch("qfA", [D, TOWN], BF16)
    Z["kfA"] = scratch("kfA", [D, S], BF16)
    Z["vtA"] = scratch("vtA", [S, D], BF16)
    Z["sF"] = scratch("sF", [D, TOWN], F32)
    Z["wqkvA_bf"] = scratch("wqkvA_bf", [D, 3 * D], BF16)
    Z["wprojA_bf"] = scratch("wprojA_bf", [D, D], BF16)
    Z["wqkv2_bf"] = scratch("wqkv2_bf", [2, D, 3 * INNER], BF16)
    Z["wo2_bf"] = scratch("wo2_bf", [2, INNER, D], BF16)
    Z["w12_bf"] = scratch("w12_bf", [2, D, W1C], BF16)
    Z["w22_bf"] = scratch("w22_bf", [2, HID, D], BF16)
    env.Z = Z
    env.out_o = out_o

    sb = lambda n, s, d: es.enter_context(nc.sbuf_tensor(n, s, d))
    pt = Env()
    pt.identb = sb("identb", [P, P], BF16)
    pt.ident3 = sb("ident3", [P, P], F32)
    pt.ones1r = sb("ones1r", [1, P], F32)
    pt.onesc = sb("onesc", [P, 1], F32)
    pt.onesb = sb("onesb", [P, 1], BF16)
    pt.gn = sb("gn", [P, 8], F32)
    pt.bqkv = sb("bqkv", [P, 16], F32)
    pt.bvb = sb("bvb", [P, D], F32)
    pt.bproj = sb("bproj", [P, 8], F32)
    pt.segb = sb("segb_sb", [P, 2], BF16)
    pt.perm32 = sb("perm32_sb", [P, P], BF16)
    pt.segT = sb("segT_sb", [2, P], F32)
    pt.gate = sb("gate_sb", [P, 1], F32)
    pt.qs8 = sb("qs8", [P, 2], F32)
    pt.ksc = sb("kscs", [P, 2], F32)
    pt.lng = sb("lngs", [P, 2, 8], F32)
    pt.lnb = sb("lnbs", [P, 2, 8], F32)
    pt.b2s = sb("b2s", [P, 2, 8], F32)
    pt.b1s = sb("b1s", [P, 2, 44], F32)
    pt.eps = sb("eps", [P, 1], F32)
    pt.eps6 = sb("eps6", [P, 1], F32)
    pt.eps24 = sb("eps24", [P, 1], F32)
    env.pt = pt

    _emit_init(env)
    with contextlib.ExitStack() as wces:
        _emit_wcast(env, wces)
        if N_PHASES >= 1:
            _emit_ph0(env)
    if N_PHASES >= 2:
        _emit_attnA(env)
    if N_PHASES >= 3:
        _emit_zeroinit(env)
        _emit_slA(env, 0)
        _emit_slF(env, 0)
        _emit_gate(env, "yB")
    if N_PHASES >= 4:
        _emit_slA(env, 1)
        _emit_gate(env, "yA")
        _emit_slF(env, 1)
    if N_PHASES >= 5:
        _emit_final(env)
    _emit_tail(env)

    with nc.Block("main") as _mb:
        env.dblk.flush(_mb)
    es.close()
    return nc


def _emit_init(env):
    nc, I, pt, c = env.nc, env.I, env.pt, env.c
    sD, sG = env.sD, env.sG

    with _PhaseCtx(env) as blk:
        @blk.gpsimd
        def _(g):
            g.memset(pt.identb[:], 0.0)
            g.affine_select(out=pt.identb[:], in_=pt.identb[:], compare_op=ALU.not_equal,
                            fill=1.0, base=0, pattern=[[-1, P]], channel_multiplier=1)
            g.memset(pt.ident3[:], 0.0)
            g.affine_select(out=pt.ident3[:], in_=pt.ident3[:], compare_op=ALU.not_equal,
                            fill=1.0, base=0, pattern=[[-1, P]], channel_multiplier=1)
            g.memset(pt.ones1r[:], 1.0)
            g.memset(pt.onesc[:], 1.0)
            g.memset(pt.onesb[:], 1.0)
            g.memset(pt.eps[:], 1e-5)
            g.memset(pt.eps6[:], 1e-6)
            g.memset(pt.eps24[:], 1e-24).then_inc(env.sG, 1)
            c.add(sG, 1)

        @blk.sync
        def _(s):
            def ld(dst, src):
                s.dma_start(out=dst, in_=src).then_inc(sD, 16)
                c.add(sD, 16)
            ld(pt.gn[:], I["gnorm"][:])
            ld(pt.bqkv[:], I["bqk_t"][:])
            ld(pt.bproj[:], I["bprojA"][:])
            ld(pt.segb[:], I["segbf"][:])
            ld(pt.perm32[:], I["perm32"][:])
            ld(pt.segT[:], I["segT"][:])
            ld(pt.gate[:], I["gate"][:])
            ld(pt.qs8[:], I["qsc"][:])
            ld(pt.ksc[:], I["ksc"][:])
            ld(pt.lng[:], I["lng"][:])
            ld(pt.lnb[:], I["lnb"][:])
            ld(pt.b2s[:], I["b22"][:])
            ld(pt.b1s[:], I["b12"][:])

    snap0 = c.snap()

    with (nc.psum_tensor("ps_init", [P, 512], F32) as ps,
          nc.sbuf_tensor("bvrow", [1, D], F32) as bvrow,
          _PhaseCtx(env) as blk):
        @blk.sync
        def _(s):
            _barrier_waits(env, s, snap0)
            s.dma_start(out=bvrow[:], in_=I["bqv_row"][:]).then_inc(sD, 16)
            c.add(sD, 16)

        vready = c.cur(env.sD)

        @blk.tensor
        def _(t):
            t.wait_ge(sD, vready)
            t.matmul(ps[:, 0:512], env.pt.ones1r[0:1, :], bvrow[0:1, 0:512],
                     start=True, stop=True).then_inc(env.sT, 1)
            c.add(env.sT, 1)

        @blk.vector
        def _(v):
            v.wait_ge(env.sT, c.cur(env.sT))
            v.tensor_copy(pt.bvb[:, 0:512], ps[:, 0:512]).then_inc(env.sV, 1)
            c.add(env.sV, 1)

        t2 = c.cur(env.sV)

        @blk.tensor
        def _(t):
            t.wait_ge(env.sV, t2)
            t.matmul(ps[:, 0:512], env.pt.ones1r[0:1, :], bvrow[0:1, 512:1024],
                     start=True, stop=True).then_inc(env.sT, 1)
            c.add(env.sT, 1)

        @blk.vector
        def _(v):
            v.wait_ge(env.sT, c.cur(env.sT))
            v.tensor_copy(pt.bvb[:, 512:1024], ps[:, 0:512]).then_inc(env.sV, 1)
            c.add(env.sV, 1)
            v.wait_ge(env.sG, 1)
            v.tensor_scalar_mul(pt.qs8[:], pt.qs8[:], 8.0).then_inc(env.sV, 1)
            c.add(env.sV, 1)

    env.snap = c.snap()


def _emit_wcast(env, wces):
    nc, I, Z, c = env.nc, env.I, env.Z, env.c
    sG = env.sG

    jobs = []
    def add2d(srcap, dstap, rows, ncols):
        for r in range(0, rows, 128):
            rr = min(128, rows - r)
            for c0 in range(0, ncols, HID):
                cw = min(HID, ncols - c0)
                jobs.append((srcap[r:r + rr, c0:c0 + cw],
                             dstap[r:r + rr, c0:c0 + cw], rr, cw))
    add2d(I["wqkvA"], Z["wqkvA_bf"], D, 3 * D)
    add2d(I["wprojA"], Z["wprojA_bf"], D, D)
    for l in range(2):
        add2d(I["wqkv2"][l], Z["wqkv2_bf"][l], D, 3 * INNER)
        add2d(I["wo2"][l], Z["wo2_bf"][l], INNER, D)
        add2d(I["w12"][l], Z["w12_bf"][l], D, W1C)
        add2d(I["w22"][l], Z["w22_bf"][l], HID, D)
    wq_jobs = 16

    b32 = [wces.enter_context(nc.sbuf_tensor("wc_f0", [P, HID], F32))]
    b16 = [wces.enter_context(nc.sbuf_tensor("wc_b0", [P, HID], BF16))]

    snap = env.snap
    wq_mark = [0]

    with _PhaseCtx(env) as blk:
        @blk.gpsimd
        def _(g):
            _barrier_waits(env, g, snap)
            store_done = [0]
            for i, (sap, dap, rr, ncols) in enumerate(jobs):
                sl = 0
                if store_done[sl]:
                    g.wait_ge(sG, store_done[sl])
                g.dma_start(out=b32[sl][0:rr, 0:ncols], in_=sap).then_inc(sG, 16)
                ld = c.add(sG, 16)
                g.wait_ge(sG, ld)
                g.tensor_copy(b16[sl][0:rr, 0:ncols], b32[sl][0:rr, 0:ncols]).then_inc(sG, 1)
                cv = c.add(sG, 1)
                g.wait_ge(sG, cv)
                g.dma_start(out=dap, in_=b16[sl][0:rr, 0:ncols]).then_inc(sG, 16)
                store_done[sl] = c.add(sG, 16)
                if i == wq_jobs - 1:
                    wq_mark[0] = max(store_done)
    env.wq_ready = wq_mark[0]


def _emit_ph0(env):
    nc, I, Z, pt, c = env.nc, env.I, env.Z, env.pt, env.c
    sD, sG, sT, sV, sA = env.sD, env.sG, env.sT, env.sV, env.sA

    groups = []
    groups.append(("oh", 0, 512, "halo"))
    groups.append(("oh", 512, 512, "halo"))
    groups.append(("oh", 1024, 128, "halo"))
    for g in range(8):
        groups.append(("oh", 1152 + 512 * g, 512, "own"))
    for g in range(8):
        groups.append(("oth", 512 * g, 512, "oth"))

    with contextlib.ExitStack() as ph:
        sb = lambda n, s, d: ph.enter_context(nc.sbuf_tensor(n, s, d))
        wq_bf = sb("ph0_wq", [P, 8, 3 * D], BF16)
        xin = [sb(f"ph0_xin{i}", [P, 4, D], F32) for i in range(2)]
        stat = [sb(f"ph0_stat{i}", [P, 4 * 6], F32) for i in range(2)]
        xc = [sb(f"ph0_xc{i}", [P, 4, D], F32) for i in range(2)]
        ha_t = sb("ph0_hat", [P, 4, D], BF16)
        ha_f = sb("ph0_haf", [P, 8, 512], BF16)
        stg = sb("ph0_stg", [P, 512], F32)
        qtmp = sb("ph0_qtmp", [P, 8, 512], F32)
        rstg = sb("ph0_rstg", [P, 512], BF16)
        rc_s = [sb("ph0_rc0", [P, 8, 512], BF16)]
        rs_s = [sb("ph0_rs0", [P, 8, 512], BF16)]
        vstg = sb("ph0_vstg", [P, 1024], BF16)
        rstg2 = sb("ph0_rstg2", [P, 512], F32)
        ps1 = ph.enter_context(nc.psum_tensor("ph0_ps1", [P, 512], F32))
        ps2 = ph.enter_context(nc.psum_tensor("ph0_ps2", [P, 512], F32))
        psb = ph.enter_context(nc.psum_tensor("ph0_psb", [P, 1024], BF16))
        psv = ph.enter_context(nc.psum_tensor("ph0_psv", [P, 1024], F32))

        with _PhaseCtx(env) as blk:
            snap = env.snap
            _phase_barrier(env, blk, snap)

            @blk.sync
            def _(s):
                _barrier_waits(env, s, snap)
                s.wait_ge(env.sG, env.wq_ready)
                s.dma_start(out=wq_bf[:], in_=Z["wqkvA_bf"].rearrange("(c p) n -> p c n", p=P)
                            ).then_inc(sD, 16)
                c.add(sD, 16)

            bufs = dict(wq_bf=wq_bf, xin=xin, stat=stat, xc=xc, ha_t=ha_t,
                        ha_f=ha_f, stg=stg, qtmp=qtmp, rstg=rstg, rc_s=rc_s,
                        rs_s=rs_s, vstg=vstg, rstg2=rstg2, ps1=ps1, ps2=ps2,
                        psb=psb, psv=psv)
            prev = {}
            pend = None
            for gi, (src, r0, nr, kind) in enumerate(groups):
                if pend is not None:
                    _ph0_back(env, blk, bufs, pend)
                fr = _ph0_front(env, blk, bufs, gi, src, r0, nr, kind, prev)
                prev = fr["prev2"]
                pend = fr
            if pend is not None:
                _ph0_back(env, blk, bufs, pend)

    env.snap = c.snap()


def _ph0_front(env, blk, bufs, gi, src, r0, nr, kind, prev):
    nc, I, Z, pt, c = env.nc, env.I, env.Z, env.pt, env.c
    sD, sG, sT, sV, sA = env.sD, env.sG, env.sT, env.sV, env.sA
    nt = nr // 128
    par = gi % 2
    xin, stat, xc = bufs["xin"][par], bufs["stat"][par], bufs["xc"][par]
    ha_t, stg, psv = bufs["ha_t"], bufs["stg"], bufs["psv"]
    xsrc = I["x_oh"] if src == "oh" else I["x_oth"]
    do_xn = kind in ("halo", "own")
    do_qkv = kind in ("own", "oth")
    own_q = kind == "own"
    xn_c0 = r0 if src == "oh" else None
    key_c0 = (r0 - H) if src == "oh" else (TOWN + r0)
    pf = prev.get(par, {})
    out = {}

    @blk.sync
    def _(s, pf=pf):
        if pf.get("xin_T"):
            s.wait_ge(sT, pf["xin_T"])
        if pf.get("xin_V"):
            s.wait_ge(sV, pf["xin_V"])
        s.dma_start(out=xin[:, 0:nt, :],
                    in_=xsrc[r0:r0 + nr, :].rearrange("(a p) f -> p a f", p=P)
                    ).then_inc(sD, 16)
        c.add(sD, 16)
    ld_done = c.cur(sD)

    @blk.vector
    def _(v, pf=pf):
        v.wait_ge(sD, ld_done)
        if pf.get("stat_A"):
            v.wait_ge(sA, pf["stat_A"])
        for a in range(nt):
            last = v.reduce_sum(stat[:, a * 6:a * 6 + 1], xin[:, a, :], axis=AX.X)
        _fence(env, v, sV, last)
        for a in range(nt):
            last = v.tensor_scalar_mul(stat[:, a * 6 + 1:a * 6 + 2], stat[:, a * 6:a * 6 + 1], 1.0 / D)
        _fence(env, v, sV, last)
        for a in range(nt):
            last = v.tensor_scalar(xc[:, a, :], xin[:, a, :], stat[:, a * 6 + 1:a * 6 + 2],
                                   None, op0=ALU.subtract)
        last.then_inc(sV, 1)
        c.add(sV, 1)
    xc_done = c.cur(sV)

    @blk.scalar
    def _(sc, tw=c.cur(sT)):
        sc.wait_ge(sV, xc_done)
        sc.wait_ge(sT, tw)
        for a in range(nt):
            last = sc.activation(out=ha_t[:, a, :], in_=xc[:, a, :], func=AF.Square)
        last.then_inc(sA, 1)
        c.add(sA, 1)
    sq_done = c.cur(sA)

    @blk.vector
    def _(v):
        v.wait_ge(sA, sq_done)
        for a in range(nt):
            last = v.reduce_sum(stat[:, a * 6 + 2:a * 6 + 3], ha_t[:, a, :], axis=AX.X)
        last.then_inc(sV, 1)
        c.add(sV, 1)
    ss_done = c.cur(sV)

    @blk.scalar
    def _(sc):
        sc.wait_ge(sV, ss_done)
        for a in range(nt):
            last = sc.activation(out=stat[:, a * 6 + 3:a * 6 + 4], in_=stat[:, a * 6 + 2:a * 6 + 3],
                                 func=AF.Ln, bias=pt.eps[:, 0:1], scale=1.0 / D)
        _fence(env, sc, sA, last)
        for a in range(nt):
            last = sc.activation(out=stat[:, a * 6 + 2:a * 6 + 3], in_=stat[:, a * 6 + 3:a * 6 + 4],
                                 func=AF.Exp, scale=-0.5)
        if do_qkv:
            for a in range(nt):
                last = sc.activation(out=stat[:, a * 6 + 4:a * 6 + 5], in_=stat[:, a * 6 + 3:a * 6 + 4],
                                     func=AF.Exp, scale=-1.0)
        last.then_inc(sA, 1)
        c.add(sA, 1)
    r1_done = c.cur(sA)

    if do_qkv:
        @blk.vector
        def _(v):
            v.wait_ge(sA, r1_done)
            for a in range(nt):
                last = v.tensor_scalar(stat[:, a * 6 + 4:a * 6 + 5], stat[:, a * 6 + 4:a * 6 + 5],
                                       -1e-5, 1.0, op0=ALU.mult, op1=ALU.add)
            last.then_inc(sV, 1)
            c.add(sV, 1)
        ms_done = c.cur(sV)

        @blk.scalar
        def _(sc):
            sc.wait_ge(sV, ms_done)
            for a in range(nt):
                last = sc.activation(out=stat[:, a * 6 + 5:a * 6 + 6], in_=stat[:, a * 6 + 4:a * 6 + 5],
                                     func=AF.Ln, bias=pt.eps6[:, 0:1])
            last.then_inc(sA, 1)
            c.add(sA, 1)
        w_done = c.cur(sA)

        @blk.vector
        def _(v):
            v.wait_ge(sA, w_done)
            for a in range(nt):
                last = v.tensor_add(stat[:, a * 6 + 5:a * 6 + 6], stat[:, a * 6 + 5:a * 6 + 6],
                                    stat[:, a * 6 + 3:a * 6 + 4])
            last.then_inc(sV, 1)
            c.add(sV, 1)
        uw_done = c.cur(sV)

        @blk.scalar
        def _(sc):
            sc.wait_ge(sV, uw_done)
            for a in range(nt):
                last = sc.activation(out=stat[:, a * 6 + 5:a * 6 + 6], in_=stat[:, a * 6 + 5:a * 6 + 6],
                                     func=AF.Exp, scale=-0.5)
            last.then_inc(sA, 1)
            c.add(sA, 1)

    stats_done = c.cur(sA)
    out["stat_A"] = stats_done

    @blk.vector
    def _(v):
        v.wait_ge(sA, stats_done)
        last = None
        if do_qkv:
            for a in range(nt):
                last = v.tensor_scalar(ha_t[:, a, :], xc[:, a, :], stat[:, a * 6 + 5:a * 6 + 6],
                                       None, op0=ALU.mult)
        if do_xn:
            for a in range(nt):
                last = v.tensor_scalar(xin[:, a, :], xc[:, a, :], stat[:, a * 6 + 2:a * 6 + 3],
                                       None, op0=ALU.mult)
        last.then_inc(sV, 1)
        c.add(sV, 1)
    norm_done = c.cur(sV)

    if do_xn:
        for cc in range(8):
            tw = c.cur(sV)
            aw3 = c.cur(sA)

            @blk.tensor
            def _(t, cc=cc, tw=tw, aw3=aw3):
                t.wait_ge(sV, tw)
                t.wait_ge(sA, aw3)
                for a in range(nt):
                    last = t.transpose(psv[:, a * 128:(a + 1) * 128],
                                       xin[:, a, cc * 128:(cc + 1) * 128], pt.ident3[:])
                last.then_inc(sT, 1)
                c.add(sT, 1)
            tp_done = c.cur(sT)
            dprev = c.cur(sD)

            @blk.scalar
            def _(sc, cc=cc, tp_done=tp_done, dprev=dprev):
                sc.wait_ge(sT, tp_done)
                sc.wait_ge(sD, dprev)
                sc.activation(out=stg[:, 0:nr], in_=psv[:, 0:nr],
                              func=AF.Copy).then_inc(sA, 1)
                c.add(sA, 1)
            ev_done = c.cur(sA)

            @blk.sync
            def _(s, cc=cc, ev_done=ev_done):
                s.wait_ge(sA, ev_done)
                s.dma_start(out=Z["xnF"][cc * 128:(cc + 1) * 128, xn_c0:xn_c0 + nr],
                            in_=stg[:, 0:nr]).then_inc(sD, 16)
                c.add(sD, 16)
    out["xin_T"] = c.cur(sT)
    out["xin_V"] = norm_done

    prev2 = dict(prev)
    prev2[par] = out
    return dict(prev2=prev2, gi=gi, src=src, r0=r0, nr=nr, kind=kind, par=par,
                out=out, norm_done=norm_done, key_c0=key_c0)


def _ph0_back(env, blk, bufs, fr):
    nc, I, Z, pt, c = env.nc, env.I, env.Z, env.pt, env.c
    sD, sG, sT, sV, sA = env.sD, env.sG, env.sT, env.sV, env.sA
    kind, r0, nr, par = fr["kind"], fr["r0"], fr["nr"], fr["par"]
    do_qkv = kind in ("own", "oth")
    if not do_qkv:
        return
    own_q = kind == "own"
    key_c0 = fr["key_c0"]
    nt = nr // 128
    norm_done = fr["norm_done"]
    wq_bf, ha_t, ha_f = bufs["wq_bf"], bufs["ha_t"], bufs["ha_f"]
    qtmp, rstg, rstg2 = bufs["qtmp"], bufs["rstg"], bufs["rstg2"]
    rc_s, rs_s = bufs["rc_s"][0], bufs["rs_s"][0]
    vstg, ps1, psb, psv = bufs["vstg"], bufs["ps1"], bufs["psb"], bufs["psv"]

    @blk.sync
    def _(s, vw0=c.cur(sV)):
        s.wait_ge(sV, vw0)
        rc = I["rc_own"] if own_q else I["rc_oth"]
        rs = I["rs_own"] if own_q else I["rs_oth"]
        col = (r0 - H) if own_q else r0
        s.dma_start(out=rc_s[:],
                    in_=rc[:, col:col + 512].rearrange("(c p) n -> p c n", p=P)
                    ).then_inc(sD, 16)
        c.add(sD, 16)
        s.dma_start(out=rs_s[:],
                    in_=rs[:, col:col + 512].rearrange("(c p) n -> p c n", p=P)
                    ).then_inc(sD, 16)
        c.add(sD, 16)
    tbl_ok = c.cur(sD)

    for cc in range(8):
        tw = c.cur(sV)
        aw2 = c.cur(sA)

        @blk.tensor
        def _(t, cc=cc, tw=tw, aw2=aw2):
            t.wait_ge(sV, max(tw, norm_done))
            t.wait_ge(sA, aw2)
            for a in range(nt):
                last = t.transpose(psb[:, a * 128:(a + 1) * 128],
                                   ha_t[:, a, cc * 128:(cc + 1) * 128], pt.identb[:])
            last.then_inc(sT, 1)
            c.add(sT, 1)
        tp2 = c.cur(sT)

        @blk.scalar
        def _(sc, cc=cc, tp2=tp2):
            sc.wait_ge(sT, tp2)
            sc.activation(out=ha_f[:, cc, 0:nr], in_=psb[:, 0:nr], func=AF.Copy,
                          scale=pt.gn[:, cc:cc + 1]).then_inc(sA, 1)
            c.add(sA, 1)
    haf_done = c.cur(sA)

    col_sets = []
    if own_q:
        col_sets.append(("q", 0))
    col_sets.append(("k", 8))
    for nmk, cbase in col_sets:
        for oc in range(8):
            aw = c.cur(sA)
            vw = c.cur(sV)

            @blk.tensor
            def _(t, oc=oc, cbase=cbase, aw=aw):
                t.wait_ge(sA, max(haf_done, aw))
                for f in range(8):
                    last = t.matmul(ps1[:, 0:nr],
                                    wq_bf[:, f, (cbase + oc) * 128:(cbase + oc + 1) * 128],
                                    ha_f[:, f, 0:nr], start=(f == 0), stop=(f == 7))
                last.then_inc(sT, 1)
                c.add(sT, 1)
            mm_done = c.cur(sT)

            @blk.scalar
            def _(sc, oc=oc, cbase=cbase, mm_done=mm_done, vw=vw):
                sc.wait_ge(sT, mm_done)
                sc.wait_ge(sV, vw)
                sc.activation(out=qtmp[:, oc, 0:nr], in_=ps1[:, 0:nr], func=AF.Identity,
                              bias=pt.bqkv[:, cbase + oc:cbase + oc + 1]).then_inc(sA, 1)
                c.add(sA, 1)
        qk_done = c.cur(sA)

        for oc in range(8):
            dw = c.cur(sD)

            @blk.vector
            def _(v, oc=oc, dw=dw, qk_done=qk_done):
                v.wait_ge(sA, qk_done)
                v.wait_ge(sD, max(dw, tbl_ok))
                v.tensor_tensor(rstg2[:, 0:nr], qtmp[:, oc, 0:nr], rc_s[:, oc, 0:nr],
                                op=ALU.mult)
                last = v.tensor_tensor(bufs["xc"][par][:, 0, 0:nr], qtmp[:, oc ^ 4, 0:nr],
                                       rs_s[:, oc, 0:nr], op=ALU.mult)
                _fence(env, v, sV, last)
                v.tensor_tensor(rstg[:, 0:nr], rstg2[:, 0:nr],
                                bufs["xc"][par][:, 0, 0:nr], op=ALU.add).then_inc(sV, 1)
                c.add(sV, 1)
            rp_done = c.cur(sV)

            @blk.sync
            def _(s, oc=oc, nmk=nmk, rp_done=rp_done):
                s.wait_ge(sV, rp_done)
                dst = Z["qfA"] if nmk == "q" else Z["kfA"]
                col0 = (r0 - H) if nmk == "q" else key_c0
                s.dma_start(out=dst[oc * 128:(oc + 1) * 128, col0:col0 + nr],
                            in_=rstg[:, 0:nr]).then_inc(sD, 16)
                c.add(sD, 16)

    for a in range(nt):
        vw = c.cur(sV)
        aw4 = c.cur(sA)

        @blk.tensor
        def _(t, a=a, vw=vw, aw4=aw4):
            t.wait_ge(sA, max(haf_done, aw4))
            t.wait_ge(sV, vw)
            for f in range(8):
                for vh in range(2):
                    last = t.matmul(psv[:, vh * 512:(vh + 1) * 512],
                                    ha_f[:, f, a * 128:(a + 1) * 128],
                                    wq_bf[:, f, 2048 + vh * 512:2048 + (vh + 1) * 512],
                                    start=(f == 0), stop=(f == 7))
            last.then_inc(sT, 1)
            c.add(sT, 1)
        vm_done = c.cur(sT)
        dw = c.cur(sD)

        @blk.vector
        def _(v, a=a, vm_done=vm_done, dw=dw):
            v.wait_ge(sT, vm_done)
            v.wait_ge(sD, dw)
            v.tensor_tensor(vstg[:, 0:1024], psv[:, 0:1024], pt.bvb[:, 0:1024],
                            op=ALU.add).then_inc(sV, 1)
            c.add(sV, 1)
        ve_done = c.cur(sV)

        @blk.sync
        def _(s, a=a, ve_done=ve_done):
            s.wait_ge(sV, ve_done)
            s.dma_start(out=Z["vtA"][key_c0 + a * 128:key_c0 + (a + 1) * 128, :],
                        in_=vstg[:, 0:1024]).then_inc(sD, 16)
            c.add(sD, 16)


def _emit_attnA(env):
    nc, I, Z, pt, c = env.nc, env.I, env.Z, env.pt, env.c
    sD, sG, sT, sV, sA = env.sD, env.sG, env.sT, env.sV, env.sA

    with contextlib.ExitStack() as ph:
        sb = lambda n, s, d: ph.enter_context(nc.sbuf_tensor(n, s, d))
        QSB = sb("aA_q", [P, 8, 1024], BF16)
        KSB = [sb(f"aA_k{i}", [P, 8, 1024], BF16) for i in range(2)]
        VSB = [sb(f"aA_v{i}", [P, 8, 1024], BF16) for i in range(2)]
        OACC = sb("aA_oacc", [P, 8, 1024], F32)
        PSB = [sb(f"aA_p{i}", [P, 1024], BF16) for i in range(2)]
        PTS = [sb(f"aA_pt{i}", [P, 1024], BF16) for i in range(2)]
        OFSB = sb("aA_of", [P, 8, 1024], BF16)
        WP = sb("aA_wp", [P, 8, 1024], BF16)
        mcur = sb("aA_m", [P, 8], F32)
        lcur = sb("aA_l", [P, 8], F32)
        mx = [sb(f"aA_mx{i}", [P, 1], F32) for i in range(2)]
        mnew = [sb(f"aA_mn{i}", [P, 1], F32) for i in range(2)]
        dd = [sb(f"aA_dd{i}", [P, 1], F32) for i in range(2)]
        af = [sb(f"aA_af{i}", [P, 1], F32) for i in range(2)]
        bias = [sb(f"aA_bias{i}", [P, 1], F32) for i in range(2)]
        lch = [sb(f"aA_lch{i}", [P, 1], F32) for i in range(2)]
        rl = sb("aA_rl", [P, 1], F32)
        obf = sb("aA_obf", [P, 1024], BF16)
        stg = sb("aA_stg", [P, 512], F32)
        xns = sb("aA_xns", [P, 512], F32)
        psS = [ph.enter_context(nc.psum_tensor(f"aA_psS{i}", [P, 1024], F32))
               for i in range(2)]
        psT = ph.enter_context(nc.psum_tensor("aA_psT", [P, 1024], BF16))
        psO = ph.enter_context(nc.psum_tensor("aA_psO", [P, 1024], F32))
        ps1 = ph.enter_context(nc.psum_tensor("aA_ps1", [P, 512], F32))

        with _PhaseCtx(env) as blk:
            snap = env.snap
            _phase_barrier(env, blk, snap)

            @blk.sync
            def _(s):
                _barrier_waits(env, s, snap)
                s.dma_start(out=WP[:], in_=Z["wprojA_bf"].rearrange("(c p) n -> p c n", p=P)
                            ).then_inc(sD, 16)
                c.add(sD, 16)

            for qpass in range(4):
                _attnA_pass(env, blk, qpass, QSB, KSB, VSB, OACC, PSB, PTS, OFSB,
                            WP, mcur, lcur, mx, mnew, dd, af, bias, lch, rl, obf,
                            stg, xns, psS, psT, psO, ps1)

    env.snap = c.snap()


def _attnA_pass(env, blk, qpass, QSB, KSB, VSB, OACC, PSB, PTS, OFSB, WP,
                mcur, lcur, mx, mnew, dd, af, bias, lch, rl, obf, stg, xns,
                psS, psT, psO, ps1):
    nc, I, Z, pt, c = env.nc, env.I, env.Z, env.pt, env.c
    sD, sG, sT, sV, sA = env.sD, env.sG, env.sT, env.sV, env.sA
    q0 = qpass * 1024

    exp_done = [0, 0]
    trans_done = [0, 0]
    pv_done = [0, 0]
    mch_done = [0, 0]
    lo_done = [0, 0]

    tprev, vprev, aprev = c.cur(sT), c.cur(sV), c.cur(sA)

    @blk.sync
    def _(s):
        s.wait_ge(sT, tprev)
        s.wait_ge(sV, vprev)
        s.wait_ge(sA, aprev)
        for f in range(8):
            s.dma_start(out=QSB[:, f, :],
                        in_=Z["qfA"][f * 128:(f + 1) * 128, q0:q0 + 1024]).then_inc(sD, 16)
            c.add(sD, 16)

    kv_users = [tprev, tprev]
    kv_ok = [0] * 9

    def load_kv(kc):
        sl = kc % 2

        @blk.sync
        def _(s, kc=kc, sl=sl, tw=kv_users[sl]):
            s.wait_ge(sT, tw)
            for f in range(8):
                s.dma_start(out=KSB[sl][:, f, :],
                            in_=Z["kfA"][f * 128:(f + 1) * 128, kc * 1024:(kc + 1) * 1024]
                            ).then_inc(sD, 16)
                c.add(sD, 16)
            for u in range(8):
                s.dma_start(out=VSB[sl][:, u, :],
                            in_=Z["vtA"][kc * 1024 + u * 128:kc * 1024 + (u + 1) * 128, :]
                            ).then_inc(sD, 16)
                c.add(sD, 16)
        kv_ok[kc] = c.cur(sD)

    load_kv(0)
    for kc in range(8):
        sl = kc % 2
        if kc + 1 < 8:
            load_kv(kc + 1)

        def emit_S(qt):
            par = qt % 2

            @blk.tensor
            def _(t, qt=qt, par=par, sl=sl, ko=kv_ok[kc], ew=exp_done[par]):
                t.wait_ge(sD, ko)
                if ew:
                    t.wait_ge(sA, ew)
                for f in range(8):
                    for kk in range(2):
                        last = t.matmul(psS[par][:, kk * 512:(kk + 1) * 512],
                                        QSB[:, f, qt * 128:(qt + 1) * 128],
                                        KSB[sl][:, f, kk * 512:(kk + 1) * 512],
                                        start=(f == 0), stop=(f == 7))
                last.then_inc(sT, 1)
                c.add(sT, 1)
            s_done = c.cur(sT)

            @blk.vector
            def _(v, qt=qt, kc=kc, par=par, s_done=s_done, aw2=exp_done[par]):
                v.wait_ge(sT, s_done)
                if aw2:
                    v.wait_ge(sA, aw2)
                last = v.reduce_max(mx[par][:], psS[par][:, 0:1024], axis=AX.X)
                _fence(env, v, sV, last)
                if kc == 0:
                    v.tensor_copy(mcur[:, qt:qt + 1], mx[par][:])
                    v.tensor_scalar_mul(bias[par][:], mx[par][:], -C_A).then_inc(sV, 1)
                else:
                    last = v.tensor_max(mnew[par][:], mcur[:, qt:qt + 1], mx[par][:])
                    _fence(env, v, sV, last)
                    v.tensor_scalar_mul(bias[par][:], mnew[par][:], -C_A)
                    v.tensor_tensor(dd[par][:], mcur[:, qt:qt + 1], mnew[par][:],
                                    op=ALU.subtract)
                    v.tensor_copy(mcur[:, qt:qt + 1], mnew[par][:]).then_inc(sV, 1)
                c.add(sV, 1)
            mch_done[par] = c.cur(sV)
            m_done = mch_done[par]

            @blk.scalar
            def _(sc, qt=qt, kc=kc, par=par, m_done=m_done, tw=trans_done[par],
                  lw=lo_done[par]):
                sc.wait_ge(sV, m_done)
                if tw:
                    sc.wait_ge(sT, tw)
                if lw:
                    sc.wait_ge(sV, lw)
                last = sc.activation(out=PSB[par][:], in_=psS[par][:, 0:1024],
                                     func=AF.Exp, bias=bias[par][:, 0:1], scale=C_A,
                                     accum_out=lch[par][:, 0:1])
                if kc > 0:
                    last = sc.activation(out=af[par][:], in_=dd[par][:, 0:1],
                                         func=AF.Exp, scale=C_A)
                last.then_inc(sA, 1)
                c.add(sA, 1)
            exp_done[par] = c.cur(sA)

        def emit_tail(qt):
            par = qt % 2
            e_done = exp_done[par]

            @blk.tensor
            def _(t, qt=qt, par=par, e_done=e_done, vw=c.cur(sV)):
                t.wait_ge(sA, e_done)
                t.wait_ge(sV, vw)
                for f in range(8):
                    last = t.transpose(psT[:, f * 128:(f + 1) * 128],
                                       PSB[par][:, f * 128:(f + 1) * 128], pt.identb[:])
                last.then_inc(sT, 1)
                c.add(sT, 1)
            trans_done[par] = c.cur(sT)
            tr_done = trans_done[par]

            @blk.vector
            def _(v, par=par, tr_done=tr_done, pw=pv_done[par]):
                v.wait_ge(sT, tr_done)
                if pw:
                    v.wait_ge(sT, pw)
                v.tensor_copy(PTS[par][:], psT[:, 0:1024]).then_inc(sV, 1)
                c.add(sV, 1)
            pt_ok = c.cur(sV)

            @blk.tensor
            def _(t, par=par, sl=sl, pt_ok=pt_ok):
                t.wait_ge(sV, pt_ok)
                for f in range(8):
                    for vh in range(2):
                        last = t.matmul(psO[:, vh * 512:(vh + 1) * 512],
                                        PTS[par][:, f * 128:(f + 1) * 128],
                                        VSB[sl][:, f, vh * 512:(vh + 1) * 512],
                                        start=(f == 0), stop=(f == 7))
                last.then_inc(sT, 1)
                c.add(sT, 1)
            pv_done[par] = c.cur(sT)
            pv_ok = pv_done[par]

            @blk.vector
            def _(v, qt=qt, kc=kc, par=par, pv_ok=pv_ok):
                v.wait_ge(sT, pv_ok)
                if kc == 0:
                    v.tensor_copy(lcur[:, qt:qt + 1], lch[par][:])
                    v.tensor_copy(OACC[:, qt, :], psO[:, 0:1024]).then_inc(sV, 1)
                else:
                    v.scalar_tensor_tensor(out=lcur[:, qt:qt + 1], in0=lcur[:, qt:qt + 1],
                                           scalar=af[par][:, 0:1], in1=lch[par][:],
                                           op0=ALU.mult, op1=ALU.add)
                    v.scalar_tensor_tensor(out=OACC[:, qt, :], in0=OACC[:, qt, :],
                                           scalar=af[par][:, 0:1], in1=psO[:, 0:1024],
                                           op0=ALU.mult, op1=ALU.add).then_inc(sV, 1)
                c.add(sV, 1)
            lo_done[par] = c.cur(sV)

        for qt in range(8):
            emit_S(qt)
            if qt >= 1:
                emit_tail(qt - 1)
        emit_tail(7)
        kv_users[sl] = c.cur(sT)

    for qt in range(8):
        @blk.vector
        def _(v, qt=qt, tw=c.cur(sT)):
            v.wait_ge(sT, tw)
            last = v.reciprocal(rl[:], lcur[:, qt:qt + 1])
            _fence(env, v, sV, last)
            v.tensor_scalar(obf[:], OACC[:, qt, :], rl[:, 0:1], None,
                            op0=ALU.mult).then_inc(sV, 1)
            c.add(sV, 1)
        n_done = c.cur(sV)

        @blk.tensor
        def _(t, qt=qt, n_done=n_done):
            t.wait_ge(sV, n_done)
            for cc in range(8):
                last = t.transpose(psT[:, cc * 128:(cc + 1) * 128],
                                   obf[:, cc * 128:(cc + 1) * 128], pt.identb[:])
            last.then_inc(sT, 1)
            c.add(sT, 1)
        ot_done = c.cur(sT)

        @blk.vector
        def _(v, qt=qt, ot_done=ot_done):
            v.wait_ge(sT, ot_done)
            for cc in range(8):
                last = v.tensor_copy(OFSB[:, cc, qt * 128:(qt + 1) * 128],
                                     psT[:, cc * 128:(cc + 1) * 128])
            last.then_inc(sV, 1)
            c.add(sV, 1)

    of_done = c.cur(sV)
    gw = c.cur(sG)

    for pc in range(8):
        for ntile in range(2):
            aw = c.cur(sA)

            @blk.sync
            def _(s, pc=pc, ntile=ntile, dw=c.cur(sD), vw2=c.cur(sV)):
                s.wait_ge(sD, dw)
                s.wait_ge(sV, vw2)
                s.dma_start(out=xns[:],
                            in_=Z["xnF"][pc * 128:(pc + 1) * 128,
                                         H + q0 + ntile * 512:H + q0 + (ntile + 1) * 512]
                            ).then_inc(sD, 16)
                c.add(sD, 16)
            xn_ok = c.cur(sD)

            @blk.tensor
            def _(t, pc=pc, ntile=ntile, of_done=of_done, aw=aw):
                t.wait_ge(sV, of_done)
                t.wait_ge(sA, aw)
                for f in range(8):
                    last = t.matmul(ps1[:, 0:512],
                                    WP[:, f, pc * 128:(pc + 1) * 128],
                                    OFSB[:, f, ntile * 512:(ntile + 1) * 512],
                                    start=(f == 0), stop=(f == 7))
                last.then_inc(sT, 1)
                c.add(sT, 1)
            pj_done = c.cur(sT)

            @blk.scalar
            def _(sc, pc=pc, pj_done=pj_done, dw2=c.cur(sD)):
                sc.wait_ge(sT, pj_done)
                sc.wait_ge(sD, dw2)
                sc.activation(out=stg[:], in_=ps1[:, 0:512], func=AF.Identity,
                              bias=pt.bproj[:, pc:pc + 1]).then_inc(sA, 1)
                c.add(sA, 1)
            ev_done = c.cur(sA)

            @blk.vector
            def _(v, xn_ok=xn_ok, ev_done=ev_done):
                v.wait_ge(sD, xn_ok)
                v.wait_ge(sA, ev_done)
                v.tensor_add(stg[:], stg[:], xns[:]).then_inc(sV, 1)
                c.add(sV, 1)
            add_done = c.cur(sV)

            @blk.sync
            def _(s, pc=pc, ntile=ntile, add_done=add_done):
                s.wait_ge(sV, add_done)
                s.dma_start(out=Z["sF"][pc * 128:(pc + 1) * 128,
                                        q0 + ntile * 512:q0 + (ntile + 1) * 512],
                            in_=stg[:]).then_inc(sD, 16)
                c.add(sD, 16)


def _slA_groups():
    gs = [(0, 512), (512, 512), (1024, 128)]
    for g in range(8):
        gs.append((1152 + 512 * g, 512))
    return gs


def _emit_zeroinit(env):
    nc, Z, c = env.nc, env.Z, env.c
    sD, sV = env.sD, env.sV
    with contextlib.ExitStack() as ph:
        zt = ph.enter_context(nc.sbuf_tensor("zi_z", [P, 512], F32))
        with _PhaseCtx(env) as blk:
            snap = env.snap
            _phase_barrier(env, blk, snap)

            @blk.vector
            def _(v):
                _barrier_waits(env, v, snap)
                v.memset(zt[:], 0.0).then_inc(sV, 1)
                c.add(sV, 1)
            zok = c.cur(sV)

            @blk.sync
            def _(s):
                s.wait_ge(sV, zok)
                for buf in ("yA", "yB"):
                    for cc in range(8):
                        s.dma_start(out=env.Z[buf][cc * 128:(cc + 1) * 128, 0:512],
                                    in_=zt[:]).then_inc(sD, 16)
                        c.add(sD, 16)
    env.snap = c.snap()


def _emit_gate(env, bufname):
    nc, Z, pt, c = env.nc, env.Z, env.pt, env.c
    sD, sV = env.sD, env.sV
    with contextlib.ExitStack() as ph:
        colt = ph.enter_context(nc.sbuf_tensor("gt_col_" + bufname, [P, 8], F32))
        with _PhaseCtx(env) as blk:
            snap = env.snap
            _phase_barrier(env, blk, snap)

            @blk.sync
            def _(s):
                _barrier_waits(env, s, snap)
                s.dma_start(out=colt[:], in_=Z[bufname][:, H - 1:H].rearrange("(c p) one -> p c one", p=P),
                            allow_slow_non_contiguous=True).then_inc(sD, 16)
                c.add(sD, 16)
            lok = c.cur(sD)

            @blk.vector
            def _(v):
                v.wait_ge(sD, lok)
                v.tensor_scalar(colt[:], colt[:], pt.gate[:, 0:1], None,
                                op0=ALU.mult).then_inc(sV, 1)
                c.add(sV, 1)
            gok = c.cur(sV)

            @blk.sync
            def _(s):
                s.wait_ge(sV, gok)
                s.dma_start(out=Z[bufname][:, H - 1:H].rearrange("(c p) one -> p c one", p=P),
                            in_=colt[:], allow_slow_non_contiguous=True).then_inc(sD, 16)
                c.add(sD, 16)
    env.snap = c.snap()


def _emit_slA(env, l):
    nc, I, Z, pt, c = env.nc, env.I, env.Z, env.pt, env.c
    sD, sG, sT, sV, sA = env.sD, env.sG, env.sT, env.sV, env.sA
    ysrc = Z["xnF"] if l == 0 else Z["yB"]
    qstart = 512 if l == 0 else 1024

    with contextlib.ExitStack() as ph:
        sb = lambda n, s, d: ph.enter_context(nc.sbuf_tensor(f"l{l}{n}", s, d))
        WQ2 = sb("sa_wq", [P, 8, 1536], BF16)
        lmk = sb("sa_lmk", [P, 5, 640], F32)
        WO2 = sb("sa_wo", [P, 4, 1024], BF16)
        KF2 = sb("sa_k", [P, 4, OH], BF16)
        VT2 = sb("sa_v", [P, 41, 512], BF16)
        QF2 = sb("sa_q", [P, 4, 512], BF16)
        ysh = sb("sa_ysh", [P, 8, 512], BF16)
        yf32 = sb("sa_yf32", [P, 8, 513], F32)
        qraw = sb("sa_qraw", [P, 4, 512], F32)
        qnrm = sb("sa_qnrm", [P, 512], BF16)
        sqb = sb("sa_sqb", [P, 512], BF16)
        rr = sb("sa_rr", [P, 512], F32)
        xc1 = sb("sa_xc1", [P, 512], F32)
        xc2 = sb("sa_xc2", [P, 512], F32)
        xc3 = sb("sa_xc3", [P, 512], F32)
        xc4 = sb("sa_xc4", [P, 512], F32)
        Ssb0 = sb("sa_S0", [P, 2, 640], F32)
        Ssb = [Ssb0, Ssb0]
        gx1 = Ssb0[:, 0, 0:512]
        gx2 = Ssb0[:, 1, 0:512]
        Pb0 = sb("sa_P0", [P, 2, 640], BF16)
        Pb = [Pb0, Pb0]
        PTS = [sb(f"sa_PT{i}", [P, 2, 640], BF16) for i in range(2)]
        mxp = [sb(f"sa_mx{i}", [P, 2], F32) for i in range(2)]
        nm2 = [sb(f"sa_nm{i}", [P, 2], F32) for i in range(2)]
        ll2 = [sb(f"sa_ll{i}", [P, 2], F32) for i in range(2)]
        rli2 = [sb(f"sa_rl{i}", [P, 2], F32) for i in range(2)]
        o_loc = sb("sa_ol", [P, 4, 512], BF16)
        oF = sb("sa_of", [P, 4, 512], BF16)
        ystg = sb("sa_ystg", [P, 512], F32)
        PS_BIG = ph.enter_context(nc.psum_tensor(f"l{l}sa_big", [P, 2048], F32))
        PS_T2 = ph.enter_context(nc.psum_tensor(f"l{l}sa_t2", [P, 2048], BF16))
        PS_O = [ph.enter_context(nc.psum_tensor(f"l{l}sa_o{i}", [P, 512], F32))
                for i in range(2)]

        with _PhaseCtx(env) as blk:
            snap = env.snap
            _phase_barrier(env, blk, snap)

            @blk.sync
            def _(s):
                _barrier_waits(env, s, snap)
                s.dma_start(out=WQ2[:], in_=Z["wqkv2_bf"][l].rearrange("(c p) n -> p c n", p=P)
                            ).then_inc(sD, 16)
                c.add(sD, 16)
                s.dma_start(out=WO2[:], in_=Z["wo2_bf"][l].rearrange("(c p) n -> p c n", p=P)
                            ).then_inc(sD, 16)
                c.add(sD, 16)
                s.dma_start(out=lmk[:], in_=I["lmask"].rearrange("m p n -> p m n")
                            ).then_inc(sD, 16)
                c.add(sD, 16)

            buf = Env()
            buf.WQ2, buf.WO2, buf.KF2, buf.VT2, buf.QF2 = WQ2, WO2, KF2, VT2, QF2
            buf.ysh, buf.yf32, buf.qraw, buf.qnrm = ysh, yf32, qraw, qnrm
            buf.gx1, buf.gx2, buf.sqb, buf.rr = gx1, gx2, sqb, rr
            buf.xc = (xc1, xc2, xc3, xc4)
            buf.Ssb, buf.Pb, buf.PTS = Ssb, Pb, PTS
            buf.mxp, buf.nm2, buf.ll2, buf.rli2 = mxp, nm2, ll2, rli2
            buf.o_loc, buf.oF, buf.ystg, buf.lmk = o_loc, oF, ystg, lmk
            buf.PS_BIG, buf.PS_T2, buf.PS_O = PS_BIG, PS_T2, PS_O

            for (r0, nr) in _slA_groups():
                _slA_group(env, blk, l, ysrc, qstart, r0, nr, buf)

    env.snap = c.snap()


def _slA_group(env, blk, l, ysrc, qstart, r0, nr, buf):
    nc, I, Z, pt, c = env.nc, env.I, env.Z, env.pt, env.c
    sD, sG, sT, sV, sA = env.sD, env.sG, env.sT, env.sV, env.sA
    nt = nr // 128
    do_attn = (r0 + nr) > qstart
    B = buf

    tprev, vprev, aprev = c.cur(sT), c.cur(sV), c.cur(sA)

    @blk.sync
    def _(s):
        s.wait_ge(sT, tprev)
        s.wait_ge(sV, vprev)
        s.wait_ge(sA, aprev)
        if r0 == 0:
            s.dma_start(out=B.yf32[:, :, 1:513],
                        in_=ysrc[:, 0:512].rearrange("(c p) n -> p c n", p=P)
                        ).then_inc(sD, 16)
            c.add(sD, 16)
        else:
            s.dma_start(out=B.yf32[:, :, 0:nr + 1],
                        in_=ysrc[:, r0 - 1:r0 + nr].rearrange("(c p) n -> p c n", p=P)
                        ).then_inc(sD, 16)
            c.add(sD, 16)
        for name, bufx in (("xqc", B.xc[0]), ("xqs", B.xc[1]),
                           ("xkc", B.xc[2]), ("xks", B.xc[3])):
            s.dma_start(out=bufx[:, 0:nr], in_=I[name][:, r0:r0 + nr]).then_inc(sD, 16)
            c.add(sD, 16)
    yld = c.cur(sD)

    @blk.vector
    def _(v, yld=yld):
        v.wait_ge(sD, yld)
        if r0 == 0:
            last = v.memset(B.yf32[:, 4:8, 0:1], 0.0)
            _fence(env, v, sV, last)
        v.tensor_copy(B.ysh[:, 0:4, 0:nr], B.yf32[:, 0:4, 1:nr + 1])
        v.tensor_copy(B.ysh[:, 4:8, 0:nr], B.yf32[:, 4:8, 0:nr]).then_inc(sV, 1)
        c.add(sV, 1)
    ysh_ok = c.cur(sV)
    ld_ok = (c.cur(sD), c.cur(sG))

    for cc in range(8):
        aw = c.cur(sA)
        par = cc % 2
        psq = B.PS_BIG[:, par * 512:(par + 1) * 512]

        @blk.tensor
        def _(t, cc=cc, aw=aw, psq=psq):
            t.wait_ge(sV, ysh_ok)
            t.wait_ge(sA, aw)
            for f in range(8):
                last = t.matmul(psq[:, 0:nr], B.WQ2[:, f, cc * 128:(cc + 1) * 128],
                                B.ysh[:, f, 0:nr], start=(f == 0), stop=(f == 7))
            last.then_inc(sT, 1)
            c.add(sT, 1)
        mm_ok = c.cur(sT)

        @blk.scalar
        def _(sc, cc=cc, mm_ok=mm_ok, vw=c.cur(sV), psq=psq):
            sc.wait_ge(sT, mm_ok)
            sc.wait_ge(sV, vw)
            sc.activation(out=B.qraw[:, cc % 4, 0:nr], in_=psq[:, 0:nr],
                          func=AF.Identity).then_inc(sA, 1)
            c.add(sA, 1)

        if cc % 4 == 3:
            isq = cc < 4
            for u in range(4):
                _slA_norm_chunk(env, blk, l, isq, u, r0, nr, B, ld_ok)

    for a in range(nt):
        vw = c.cur(sV)

        @blk.tensor
        def _(t, a=a, vw=vw):
            t.wait_ge(sV, max(vw, ysh_ok))
            for f in range(8):
                last = t.matmul(B.PS_O[0][:, 0:512], B.ysh[:, f, a * 128:(a + 1) * 128],
                                B.WQ2[:, f, 1024:1536], start=(f == 0), stop=(f == 7))
            last.then_inc(sT, 1)
            c.add(sT, 1)
        vm_ok = c.cur(sT)

        @blk.vector
        def _(v, a=a, vm_ok=vm_ok):
            v.wait_ge(sT, vm_ok)
            v.tensor_copy(B.VT2[:, (r0 // 128) + a, :], B.PS_O[0][:, 0:512]).then_inc(sV, 1)
            c.add(sV, 1)

    if not do_attn:
        return

    ttr_done = [0, 0]
    exp_done = [0, 0]
    trans_done = [0, 0]
    evac_done = [0, 0]
    pv_done = [0, 0]
    o_done = [0, 0]

    pairs = []
    for a in range(nt):
        r_tile = r0 + a * 128
        if r_tile + 128 <= qstart:
            continue
        if r_tile >= H:
            j = (r_tile - H) // 128
            midx = j if j < 4 else 4
        else:
            midx = 4
        for hp2 in range(4):
            pairs.append((a, hp2, r_tile, midx))

    qkv_done = (c.cur(sV), c.cur(sA), c.cur(sT))

    def emit_S(i):
        a, hp2, r_tile, midx = pairs[i]
        par = i % 2
        k0 = r_tile - 512

        @blk.tensor
        def _(t, a=a, hp2=hp2, k0=k0, tw=max(ttr_done)):
            if i == 0:
                t.wait_ge(sV, qkv_done[0])
                t.wait_ge(sA, qkv_done[1])
            if tw:
                t.wait_ge(sV, tw)
            qA = B.QF2[0:64, hp2, a * 128:(a + 1) * 128]
            qB = B.QF2[64:128, hp2, a * 128:(a + 1) * 128]
            kA = B.KF2[0:64, hp2, :]
            kB = B.KF2[64:128, hp2, :]
            t.matmul(B.PS_BIG[:, 0:512], qA, kA[:, k0:k0 + 512], start=True, stop=True)
            t.matmul(B.PS_BIG[:, 512:640], qA, kA[:, k0 + 512:k0 + 640], start=True, stop=True)
            t.matmul(B.PS_BIG[:, 1024:1536], qB, kB[:, k0:k0 + 512], start=True, stop=True)
            t.matmul(B.PS_BIG[:, 1536:1664], qB, kB[:, k0 + 512:k0 + 640],
                     start=True, stop=True).then_inc(sT, 1)
            c.add(sT, 1)
        s_ok = c.cur(sT)

        @blk.vector
        def _(v, midx=midx, par=par, s_ok=s_ok, aw=c.cur(sA)):
            v.wait_ge(sT, s_ok)
            if aw:
                v.wait_ge(sA, aw)
            pview = B.PS_BIG.rearrange("p (h n) -> p h n", n=1024)
            last = v.tensor_tensor(B.Ssb[par][:, :, :], pview[:, :, 0:640],
                                   B.lmk[:, midx, :].unsqueeze(1).broadcast_to([P, 2, 640]),
                                   op=ALU.add)
            _fence(env, v, sV, last)
            v.reduce_max(B.nm2[par][:, 0:2], B.Ssb[par][:, :, :], axis=AX.X,
                         negate=True).then_inc(sV, 1)
            c.add(sV, 1)
        ttr_done[par] = c.cur(sV)

    def emit_exp(i):
        par = i % 2
        m_ok = ttr_done[par]

        @blk.scalar
        def _(sc, par=par, m_ok=m_ok, tw=max(trans_done), ow=o_done[par]):
            sc.wait_ge(sV, m_ok)
            if tw:
                sc.wait_ge(sT, tw)
            if ow:
                sc.wait_ge(sV, ow)
            sc.activation(out=B.Pb[par][:, 0, :], in_=B.Ssb[par][:, 0, :], func=AF.Exp,
                          bias=B.nm2[par][:, 0:1], accum_out=B.ll2[par][:, 0:1])
            sc.activation(out=B.Pb[par][:, 1, :], in_=B.Ssb[par][:, 1, :], func=AF.Exp,
                          bias=B.nm2[par][:, 1:2], accum_out=B.ll2[par][:, 1:2]
                          ).then_inc(sA, 1)
            c.add(sA, 1)
        exp_done[par] = c.cur(sA)

    def emit_tail(i):
        a, hp2, r_tile, midx = pairs[i]
        par = i % 2
        e_ok = exp_done[par]

        @blk.tensor
        def _(t, par=par, e_ok=e_ok, vw=c.cur(sV)):
            t.wait_ge(sA, e_ok)
            t.wait_ge(sV, vw)
            for i5 in range(5):
                t.transpose(B.PS_T2[:, i5 * 128:i5 * 128 + 128],
                            B.Pb[par][:, 0, i5 * 128:(i5 + 1) * 128], pt.identb[:])
            for i5 in range(5):
                last = t.transpose(B.PS_T2[:, 1024 + i5 * 128:1024 + i5 * 128 + 128],
                                   B.Pb[par][:, 1, i5 * 128:(i5 + 1) * 128], pt.identb[:])
            last.then_inc(sT, 1)
            c.add(sT, 1)
        trans_done[par] = c.cur(sT)
        tr_ok = trans_done[par]

        @blk.vector
        def _(v, par=par, tr_ok=tr_ok, pw=pv_done[par]):
            v.wait_ge(sT, tr_ok)
            if pw:
                v.wait_ge(sT, pw)
            tv = B.PS_T2.rearrange("p (h n) -> p h n", n=1024)
            v.tensor_copy(B.PTS[par][:, 0, :], tv[:, 0, 0:640])
            last = v.tensor_copy(B.PTS[par][:, 1, :], tv[:, 1, 0:640])
            _fence(env, v, sV, last)
            last = v.reciprocal(rli_ap(B, par), B.ll2[par][:])
            last.then_inc(sV, 1)
            c.add(sV, 1)
        pt_ok = c.cur(sV)

        @blk.tensor
        def _(t, a=a, hp2=hp2, r_tile=r_tile, par=par, pt_ok=pt_ok):
            t.wait_ge(sV, pt_ok)
            sub0 = (r_tile - 512) // 128
            for h2 in range(2):
                h = hp2 * 2 + h2
                for i5 in range(5):
                    last = t.matmul(B.PS_O[h2][:, 0:64],
                                    B.PTS[par][:, h2, i5 * 128:(i5 + 1) * 128],
                                    B.VT2[:, sub0 + i5, h * 64:(h + 1) * 64],
                                    start=(i5 == 0), stop=(i5 == 4))
            last.then_inc(sT, 1)
            c.add(sT, 1)
        pv_done[par] = c.cur(sT)
        pv_ok = pv_done[par]

        @blk.vector
        def _(v, a=a, hp2=hp2, par=par, pv_ok=pv_ok):
            v.wait_ge(sT, pv_ok)
            h0 = hp2 * 2
            v.tensor_scalar(B.o_loc[:, a, h0 * 64:(h0 + 1) * 64], B.PS_O[0][:, 0:64],
                            B.rli2[par][:, 0:1], None, op0=ALU.mult)
            v.tensor_scalar(B.o_loc[:, a, (h0 + 1) * 64:(h0 + 2) * 64], B.PS_O[1][:, 0:64],
                            B.rli2[par][:, 1:2], None, op0=ALU.mult).then_inc(sV, 1)
            c.add(sV, 1)
        o_done[par] = c.cur(sV)

    for i in range(len(pairs)):
        emit_S(i)
        if i >= 1:
            emit_tail(i - 1)
        emit_exp(i)
    if pairs:
        emit_tail(len(pairs) - 1)

    a0 = max(0, (qstart - r0) // 128)
    for a in range(a0, nt):
        ow = c.cur(sV)

        @blk.tensor
        def _(t, a=a, ow=ow):
            t.wait_ge(sV, ow)
            for oc4 in range(4):
                last = t.transpose(B.PS_T2[:, oc4 * 128:(oc4 + 1) * 128],
                                   B.o_loc[:, a, oc4 * 128:(oc4 + 1) * 128], pt.identb[:])
            last.then_inc(sT, 1)
            c.add(sT, 1)
        ot_ok = c.cur(sT)

        @blk.vector
        def _(v, a=a, ot_ok=ot_ok):
            v.wait_ge(sT, ot_ok)
            for oc4 in range(4):
                last = v.tensor_copy(B.oF[:, oc4, a * 128:(a + 1) * 128],
                                     B.PS_T2[:, oc4 * 128:(oc4 + 1) * 128])
            last.then_inc(sV, 1)
            c.add(sV, 1)
    of_ok = c.cur(sV)

    wbase = qstart - r0 if qstart > r0 else 0
    wn = nr - wbase
    if wn <= 0:
        return
    for oc in range(8):
        par = oc % 2
        psw = B.PS_BIG[:, par * 512:(par + 1) * 512]

        @blk.tensor
        def _(t, oc=oc, psw=psw, of_ok=of_ok, vw4=c.cur(sV)):
            t.wait_ge(sV, of_ok)
            t.wait_ge(sV, vw4)
            for kk in range(4):
                last = t.matmul(psw[:, 0:wn], B.WO2[:, kk, oc * 128:(oc + 1) * 128],
                                B.oF[:, kk, wbase:nr], start=(kk == 0), stop=(kk == 3))
            last.then_inc(sT, 1)
            c.add(sT, 1)
        wm_ok = c.cur(sT)

        @blk.vector
        def _(v, oc=oc, psw=psw, wm_ok=wm_ok, dw2=c.cur(sD)):
            v.wait_ge(sT, wm_ok)
            v.wait_ge(sD, dw2)
            v.tensor_add(B.ystg[:, 0:wn], psw[:, 0:wn],
                         B.yf32[:, oc, 1 + wbase:1 + nr]).then_inc(sV, 1)
            c.add(sV, 1)
        ad_ok = c.cur(sV)

        @blk.sync
        def _(s, oc=oc, ad_ok=ad_ok):
            s.wait_ge(sV, ad_ok)
            s.dma_start(out=Z["yA"][oc * 128:(oc + 1) * 128, r0 + wbase:r0 + nr],
                        in_=B.ystg[:, 0:wn]).then_inc(sD, 16)
            c.add(sD, 16)


def rli_ap(B, par):
    return B.rli2[par][:]


def _slA_norm_chunk(env, blk, l, isq, u, r0, nr, B, ld_ok):
    nc, I, Z, pt, c = env.nc, env.I, env.Z, env.pt, env.c
    sD, sG, sT, sV, sA = env.sD, env.sG, env.sT, env.sV, env.sA
    tct = B.xc[0] if isq else B.xc[2]
    tst = B.xc[1] if isq else B.xc[3]
    psSS = B.PS_BIG[:, 1024:1536]
    psBC = B.PS_BIG[:, 1536:2048]
    psROT = B.PS_O[1]

    aw0 = c.cur(sA)

    @blk.scalar
    def _(sc, u=u, aw0=aw0, tw0=c.cur(sT)):
        sc.wait_ge(sA, aw0)
        sc.wait_ge(sT, tw0)
        sc.activation(out=B.sqb[:, 0:nr], in_=B.qraw[:, u, 0:nr],
                      func=AF.Square).then_inc(sA, 1)
        c.add(sA, 1)
    sq_ok = c.cur(sA)

    @blk.tensor
    def _(t, u=u, sq_ok=sq_ok, aw1=c.cur(sA)):
        t.wait_ge(sA, max(sq_ok, aw1))
        t.matmul(psSS[0:2, 0:nr], pt.segb[:, 0:2], B.sqb[:, 0:nr],
                 start=True, stop=True).then_inc(sT, 1)
        c.add(sT, 1)
    ssm_ok = c.cur(sT)

    @blk.scalar
    def _(sc, ssm_ok=ssm_ok, tw1=c.cur(sT)):
        sc.wait_ge(sT, max(ssm_ok, tw1))
        last = sc.activation(out=B.rr[0:2, 0:nr], in_=psSS[0:2, 0:nr], func=AF.Ln,
                             bias=pt.eps24[0:2, 0:1])
        _fence(env, sc, sA, last)
        sc.activation(out=B.rr[0:2, 0:nr], in_=B.rr[0:2, 0:nr], func=AF.Exp,
                      scale=-0.5).then_inc(sA, 1)
        c.add(sA, 1)
    rr_ok = c.cur(sA)

    @blk.tensor
    def _(t, rr_ok=rr_ok, vw2=c.cur(sV)):
        t.wait_ge(sA, rr_ok)
        t.wait_ge(sV, vw2)
        t.matmul(psBC[:, 0:nr], pt.segT[0:2, :], B.rr[0:2, 0:nr],
                 start=True, stop=True).then_inc(sT, 1)
        c.add(sT, 1)
    bc_ok = c.cur(sT)

    @blk.vector
    def _(v, u=u, isq=isq, bc_ok=bc_ok, ld2=ld_ok[0], tw2=c.cur(sT)):
        v.wait_ge(sT, bc_ok)
        v.wait_ge(sD, ld2)
        sc_ap = pt.qs8[:, l:l + 1] if isq else pt.ksc[:, l:l + 1]
        v.scalar_tensor_tensor(out=B.qnrm[:, 0:nr], in0=B.qraw[:, u, 0:nr],
                               scalar=sc_ap, in1=psBC[:, 0:nr],
                               op0=ALU.mult, op1=ALU.mult).then_inc(sV, 1)
        c.add(sV, 1)
    qn_ok = c.cur(sV)

    @blk.tensor
    def _(t, qn_ok=qn_ok):
        t.wait_ge(sV, qn_ok)
        t.matmul(psROT[:, 0:nr], pt.perm32[:], B.qnrm[:, 0:nr],
                 start=True, stop=True).then_inc(sT, 1)
        c.add(sT, 1)
    rot_ok = c.cur(sT)

    @blk.vector
    def _(v, u=u, isq=isq, r0=r0, rot_ok=rot_ok, tct=tct, tst=tst):
        v.wait_ge(sT, rot_ok)
        v.tensor_tensor(B.gx1[:, 0:nr], B.qnrm[:, 0:nr], tct[:, 0:nr], op=ALU.mult)
        last = v.tensor_tensor(B.gx2[:, 0:nr], psROT[:, 0:nr], tst[:, 0:nr], op=ALU.mult)
        _fence(env, v, sV, last)
        dst = B.QF2[:, u, 0:nr] if isq else B.KF2[:, u, r0:r0 + nr]
        v.tensor_add(dst, B.gx1[:, 0:nr], B.gx2[:, 0:nr]).then_inc(sV, 1)
        c.add(sV, 1)


def _emit_slF(env, l):
    nc, I, Z, pt, c = env.nc, env.I, env.Z, env.pt, env.c
    sD, sG, sT, sV, sA = env.sD, env.sG, env.sT, env.sV, env.sA

    if l == 0:
        groups = [(512, 512), (1024, 128)] + [(1152 + 512 * g, 512) for g in range(8)]
    else:
        groups = [(1152 + 512 * g, 512) for g in range(8)]

    with contextlib.ExitStack() as ph:
        sb = lambda n, s, d: ph.enter_context(nc.sbuf_tensor(f"l{l}{n}", s, d))
        W1 = sb("sf_w1", [P, 8, W1C], BF16)
        W2 = sb("sf_w2", [P, 22, D], BF16)
        ysh = sb("sf_ysh", [P, 8, 512], F32)
        sq8 = sb("sf_sq8", [P, 8, 512], BF16)
        yn2 = sb("sf_yn2", [P, 8, 512], BF16)
        tshb = sb("sf_tsh", [P, 22, 512], BF16)
        gelb = [sb(f"sf_gel{i}", [P, 512], BF16) for i in range(2)]
        yinc = [sb(f"sf_yin{i}", [P, 512], F32) for i in range(2)]
        ystg = [sb(f"sf_ystg{i}", [P, 512], F32) for i in range(2)]
        murow = sb("sf_mu", [1, 512], F32)
        v1row = sb("sf_v1", [1, 512], F32)
        varrow = sb("sf_var", [1, 512], F32)
        psS1 = ph.enter_context(nc.psum_tensor(f"l{l}sf_psS1", [P, 512], F32))
        psS2 = ph.enter_context(nc.psum_tensor(f"l{l}sf_psS2", [P, 512], F32))
        psBC = ph.enter_context(nc.psum_tensor(f"l{l}sf_psBC", [P, 1024], F32))
        psA1 = [ph.enter_context(nc.psum_tensor(f"l{l}sf_psA{i}", [P, 512], F32))
                for i in range(2)]
        psG = [ph.enter_context(nc.psum_tensor(f"l{l}sf_psG{i}", [P, 512], F32))
               for i in range(2)]

        with _PhaseCtx(env) as blk:
            snap = env.snap
            _phase_barrier(env, blk, snap)

            @blk.sync
            def _(s):
                _barrier_waits(env, s, snap)
                s.dma_start(out=W1[:], in_=Z["w12_bf"][l].rearrange("(c p) n -> p c n", p=P)
                            ).then_inc(sD, 16)
                c.add(sD, 16)
                s.dma_start(out=W2[:, 0:21, :],
                            in_=Z["w22_bf"][l][0:2688].rearrange("(c p) n -> p c n", p=P)
                            ).then_inc(sD, 16)
                c.add(sD, 16)
                s.dma_start(out=W2[0:42, 21, :], in_=Z["w22_bf"][l][2688:2730, :]
                            ).then_inc(sD, 16)
                c.add(sD, 16)

            prev = None
            for (r0, nr) in groups:
                prev = _slF_group(env, blk, l, r0, nr, W1, W2, ysh, sq8, yn2, tshb,
                                  gelb, yinc, ystg, murow, v1row, varrow,
                                  psS1, psS2, psBC, psA1, psG, prev)

    env.snap = c.snap()


def _slF_group(env, blk, l, r0, nr, W1, W2, ysh, sq8, yn2, tshb, gelb, yinc,
               ystg, murow, v1row, varrow, psS1, psS2, psBC, psA1, psG, prev):
    nc, I, Z, pt, c = env.nc, env.I, env.Z, env.pt, env.c
    sD, sG, sT, sV, sA = env.sD, env.sG, env.sT, env.sV, env.sA

    if prev is None:
        tprev, vprev, aprev = c.cur(sT), c.cur(sV), c.cur(sA)
    else:
        tprev, vprev, aprev = prev["ysh_T"], prev["ysh_V"], prev["ysh_A"]

    @blk.sync
    def _(s):
        s.wait_ge(sT, tprev)
        s.wait_ge(sV, vprev)
        s.wait_ge(sA, aprev)
        s.dma_start(out=ysh[:, 0:4, 0:nr],
                    in_=Z["yA"][0:512, r0:r0 + nr].rearrange("(c p) n -> p c n", p=P)
                    ).then_inc(sD, 16)
        c.add(sD, 16)
        s.dma_start(out=ysh[:, 4:8, 0:nr],
                    in_=Z["yA"][512:1024, r0 - 1:r0 + nr - 1].rearrange("(c p) n -> p c n", p=P)
                    ).then_inc(sD, 16)
        c.add(sD, 16)
    ld_ok = c.cur(sD)

    @blk.scalar
    def _(sc, ld_ok=ld_ok, tw=(prev or {}).get("sq8_T", 0)):
        sc.wait_ge(sD, ld_ok)
        if tw:
            sc.wait_ge(sT, tw)
        for f in range(8):
            last = sc.activation(out=sq8[:, f, 0:nr], in_=ysh[:, f, 0:nr],
                                 func=AF.Square)
        last.then_inc(sA, 1)
        c.add(sA, 1)
    sq_ok = c.cur(sA)

    @blk.tensor
    def _(t, ld_ok=ld_ok, sq_ok=sq_ok, vw=(prev or {}).get("stats_V", 0)):
        t.wait_ge(sD, ld_ok)
        t.wait_ge(sA, sq_ok)
        if vw:
            t.wait_ge(sV, vw)
        for f in range(8):
            t.matmul(psS1[0:1, 0:nr], pt.onesc[:, 0:1], ysh[:, f, 0:nr],
                     start=(f == 0), stop=(f == 7))
        for f in range(8):
            last = t.matmul(psS2[0:1, 0:nr], pt.onesb[:, 0:1], sq8[:, f, 0:nr],
                            start=(f == 0), stop=(f == 7))
        last.then_inc(sT, 1)
        c.add(sT, 1)
    sums_ok = c.cur(sT)
    sq8_T = sums_ok

    @blk.vector
    def _(v, sums_ok=sums_ok, tw2=(prev or {}).get("rows_T", 0)):
        v.wait_ge(sT, sums_ok)
        if tw2:
            v.wait_ge(sT, tw2)
        v.tensor_scalar_mul(murow[0:1, 0:nr], psS1[0:1, 0:nr], 1.0 / D)
        last = v.tensor_scalar_mul(v1row[0:1, 0:nr], psS2[0:1, 0:nr], 1.0 / D)
        _fence(env, v, sV, last)
        last = v.tensor_tensor(varrow[0:1, 0:nr], murow[0:1, 0:nr], murow[0:1, 0:nr],
                               op=ALU.mult)
        _fence(env, v, sV, last)
        v.tensor_tensor(varrow[0:1, 0:nr], v1row[0:1, 0:nr], varrow[0:1, 0:nr],
                        op=ALU.subtract).then_inc(sV, 1)
        c.add(sV, 1)
    var_ok = c.cur(sV)
    stats_V = var_ok

    @blk.scalar
    def _(sc, var_ok=var_ok):
        sc.wait_ge(sV, var_ok)
        last = sc.activation(out=varrow[0:1, 0:nr], in_=varrow[0:1, 0:nr],
                             func=AF.Ln, bias=pt.eps[0:1, 0:1])
        _fence(env, sc, sA, last)
        sc.activation(out=varrow[0:1, 0:nr], in_=varrow[0:1, 0:nr],
                      func=AF.Exp, scale=-0.5).then_inc(sA, 1)
        c.add(sA, 1)
    rstd_ok = c.cur(sA)

    @blk.tensor
    def _(t, rstd_ok=rstd_ok, vw2=(prev or {}).get("bc_V", 0)):
        t.wait_ge(sA, rstd_ok)
        if vw2:
            t.wait_ge(sV, vw2)
        t.matmul(psBC[:, 0:nr], pt.ones1r[0:1, :], murow[0:1, 0:nr],
                 start=True, stop=True)
        t.matmul(psBC[:, 512:512 + nr], pt.ones1r[0:1, :], varrow[0:1, 0:nr],
                 start=True, stop=True).then_inc(sT, 1)
        c.add(sT, 1)
    bc_ok = c.cur(sT)
    rows_T = bc_ok

    @blk.vector
    def _(v, bc_ok=bc_ok, tw3=(prev or {}).get("yn2_T", 0)):
        v.wait_ge(sT, bc_ok)
        if tw3:
            v.wait_ge(sT, tw3)
        for f in range(8):
            last = v.tensor_tensor(ysh[:, f, 0:nr], ysh[:, f, 0:nr], psBC[:, 0:nr],
                                   op=ALU.subtract)
        _fence(env, v, sV, last)
        for f in range(8):
            last = v.scalar_tensor_tensor(out=yn2[:, f, 0:nr], in0=ysh[:, f, 0:nr],
                                          scalar=pt.lng[:, l, f:f + 1],
                                          in1=psBC[:, 512:512 + nr],
                                          op0=ALU.mult, op1=ALU.mult)
        _fence(env, v, sV, last)
        for f in range(8):
            last = v.tensor_scalar(yn2[:, f, 0:nr], yn2[:, f, 0:nr],
                                   pt.lnb[:, l, f:f + 1], None, op0=ALU.add)
        last.then_inc(sV, 1)
        c.add(sV, 1)
    yn_ok = c.cur(sV)
    bc_V = yn_ok
    ysh_V = yn_ok

    gel_done = [0, 0]
    gelu_done = [0, 0]
    for j in range(22):
        par = j % 2
        mj = 128 if j < 21 else 42
        ac0 = j * 128
        gc0 = HID + j * 128

        @blk.tensor
        def _(t, j=j, par=par, mj=mj, ac0=ac0, gc0=gc0, yn_ok=yn_ok,
              aw=gelu_done[par], vw3=gel_done[par]):
            t.wait_ge(sV, yn_ok)
            if aw:
                t.wait_ge(sA, aw)
            if vw3:
                t.wait_ge(sV, vw3)
            for f in range(8):
                t.matmul(psA1[par][0:mj, 0:nr], W1[:, f, ac0:ac0 + mj],
                         yn2[:, f, 0:nr], start=(f == 0), stop=(f == 7))
            for f in range(8):
                last = t.matmul(psG[par][0:mj, 0:nr], W1[:, f, gc0:gc0 + mj],
                                yn2[:, f, 0:nr], start=(f == 0), stop=(f == 7))
            last.then_inc(sT, 1)
            c.add(sT, 1)
        w1_ok = c.cur(sT)

        @blk.scalar
        def _(sc, j=j, par=par, mj=mj, w1_ok=w1_ok, vw4=gel_done[par]):
            sc.wait_ge(sT, w1_ok)
            if vw4:
                sc.wait_ge(sV, vw4)
            sc.activation(out=gelb[par][0:mj, 0:nr], in_=psG[par][0:mj, 0:nr],
                          func=AF.Gelu,
                          bias=pt.b1s[0:mj, l, 22 + j:23 + j]).then_inc(sA, 1)
            c.add(sA, 1)
        gelu_done[par] = c.cur(sA)
        gel_ok = gelu_done[par]

        @blk.vector
        def _(v, j=j, par=par, mj=mj, gel_ok=gel_ok):
            v.wait_ge(sA, gel_ok)
            v.scalar_tensor_tensor(out=tshb[0:mj, j, 0:nr], in0=psA1[par][0:mj, 0:nr],
                                   scalar=pt.b1s[0:mj, l, j:j + 1],
                                   in1=gelb[par][0:mj, 0:nr],
                                   op0=ALU.add, op1=ALU.mult).then_inc(sV, 1)
            c.add(sV, 1)
        gel_done[par] = c.cur(sV)
    t_ok = c.cur(sV)
    yn2_T = c.cur(sT)

    w2v = [gel_done[0], gel_done[1]]
    ystg_store = [0, 0]
    for oc in range(8):
        par = oc % 2

        @blk.sync
        def _(s, oc=oc, par=par, vw5=w2v[par]):
            if vw5:
                s.wait_ge(sV, vw5)
            s.dma_start(out=yinc[par][:, 0:nr],
                        in_=Z["yA"][oc * 128:(oc + 1) * 128, r0:r0 + nr]
                        ).then_inc(sD, 16)
            c.add(sD, 16)
        yi_ok = c.cur(sD)

        @blk.tensor
        def _(t, oc=oc, par=par, t_ok=t_ok, vw6=w2v[par]):
            t.wait_ge(sV, t_ok)
            if vw6:
                t.wait_ge(sV, vw6)
            for j in range(22):
                mj = 128 if j < 21 else 42
                last = t.matmul(psA1[par][:, 0:nr], W2[0:mj, j, oc * 128:(oc + 1) * 128],
                                tshb[0:mj, j, 0:nr], start=(j == 0), stop=(j == 21))
            last.then_inc(sT, 1)
            c.add(sT, 1)
        w2_ok = c.cur(sT)

        @blk.vector
        def _(v, oc=oc, par=par, w2_ok=w2_ok, yi_ok=yi_ok, sw=ystg_store[par]):
            v.wait_ge(sT, w2_ok)
            v.wait_ge(sD, max(yi_ok, sw))
            last = v.scalar_tensor_tensor(out=ystg[par][:, 0:nr], in0=psA1[par][:, 0:nr],
                                          scalar=pt.b2s[:, l, oc:oc + 1],
                                          in1=yinc[par][:, 0:nr],
                                          op0=ALU.add, op1=ALU.add)
            last.then_inc(sV, 1)
            c.add(sV, 1)
        w2v[par] = c.cur(sV)
        st_ok = w2v[par]

        @blk.sync
        def _(s, oc=oc, par=par, st_ok=st_ok):
            s.wait_ge(sV, st_ok)
            s.dma_start(out=Z["yB"][oc * 128:(oc + 1) * 128, r0:r0 + nr],
                        in_=ystg[par][:, 0:nr]).then_inc(sD, 16)
            c.add(sD, 16)
        ystg_store[par] = c.cur(sD)

    return {
        "ysh_T": sums_ok, "ysh_V": ysh_V, "ysh_A": sq_ok,
        "sq8_T": sq8_T, "stats_V": stats_V, "rows_T": rows_T,
        "bc_V": bc_V, "yn2_T": yn2_T,
    }


def _emit_final(env):
    nc, I, Z, pt, c = env.nc, env.I, env.Z, env.pt, env.c
    sD, sG, sT, sV, sA = env.sD, env.sG, env.sT, env.sV, env.sA

    with contextlib.ExitStack() as ph:
        sb = lambda n, s, d: ph.enter_context(nc.sbuf_tensor(n, s, d))
        yb = sb("fn_y", [P, 8, 512], F32)
        sfb = sb("fn_s", [P, 8, 512], F32)
        gch = sb("fn_g", [P, 8, 512], F32)
        xtk = sb("fn_x", [P, 4, 1024], F32)
        osb = sb("fn_o", [P, 1024], F32)
        psF = ph.enter_context(nc.psum_tensor("fn_psF", [P, 1024], F32))

        with _PhaseCtx(env) as blk:
            snap = env.snap
            _phase_barrier(env, blk, snap)
            first = [True]
            for g in range(8):
                c0 = H + g * 512
                tprev, vprev, aprev = c.cur(sT), c.cur(sV), c.cur(sA)

                @blk.sync
                def _(s, g=g, c0=c0, tprev=tprev, vprev=vprev, aprev=aprev,
                      fst=first[0]):
                    if fst:
                        _barrier_waits(env, s, snap)
                    s.wait_ge(sT, tprev)
                    s.wait_ge(sV, vprev)
                    s.wait_ge(sA, aprev)
                    s.dma_start(out=yb[:], in_=Z["yB"][:, c0:c0 + 512]
                                .rearrange("(c p) n -> p c n", p=P)).then_inc(sD, 16)
                    c.add(sD, 16)
                    s.dma_start(out=sfb[:], in_=Z["sF"][:, g * 512:(g + 1) * 512]
                                .rearrange("(c p) n -> p c n", p=P)).then_inc(sD, 16)
                    c.add(sD, 16)
                    s.dma_start(out=xtk[:], in_=I["x_oh"][c0:c0 + 512, :]
                                .rearrange("(a p) f -> p a f", p=P)).then_inc(sD, 16)
                    c.add(sD, 16)
                first[0] = False
                ld_ok = c.cur(sD)

                @blk.scalar
                def _(sc, ld_ok=ld_ok):
                    sc.wait_ge(sD, ld_ok)
                    for f in range(8):
                        last = sc.activation(out=gch[:, f, :], in_=yb[:, f, :],
                                             func=AF.Silu)
                    last.then_inc(sA, 1)
                    c.add(sA, 1)
                si_ok = c.cur(sA)

                @blk.vector
                def _(v, si_ok=si_ok):
                    v.wait_ge(sA, si_ok)
                    for f in range(8):
                        last = v.tensor_tensor(gch[:, f, :], gch[:, f, :],
                                               sfb[:, f, :], op=ALU.mult)
                    last.then_inc(sV, 1)
                    c.add(sV, 1)
                t_ok = c.cur(sV)

                for a in range(4):
                    vw = c.cur(sV)

                    @blk.tensor
                    def _(t, a=a, t_ok=t_ok, vw=vw):
                        t.wait_ge(sV, vw)
                        for f in range(8):
                            last = t.transpose(psF[:, f * 128:(f + 1) * 128],
                                               gch[:, f, a * 128:(a + 1) * 128],
                                               pt.ident3[:])
                        last.then_inc(sT, 1)
                        c.add(sT, 1)
                    tr_ok = c.cur(sT)
                    dw = c.cur(sD)

                    @blk.vector
                    def _(v, a=a, tr_ok=tr_ok, dw=dw):
                        v.wait_ge(sT, tr_ok)
                        v.wait_ge(sD, dw)
                        v.tensor_tensor(osb[:], psF[:, 0:1024], xtk[:, a, :],
                                        op=ALU.add).then_inc(sV, 1)
                        c.add(sV, 1)
                    ev_ok = c.cur(sV)

                    @blk.sync
                    def _(s, g=g, a=a, ev_ok=ev_ok):
                        s.wait_ge(sV, ev_ok)
                        s.dma_start(out=env.out_o[g * 512 + a * 128:
                                                  g * 512 + (a + 1) * 128, :],
                                    in_=osb[:]).then_inc(sD, 16)
                        c.add(sD, 16)

    env.snap = c.snap()


def _emit_tail(env):
    nc, c = env.nc, env.c
    with _PhaseCtx(env) as blk:
        @blk.sync
        def _(s):
            s.wait_ge(env.sD, c.cur(env.sD))
            s.wait_ge(env.sG, c.cur(env.sG))


def pk(vec, ncol):
    return np.ascontiguousarray(np.asarray(vec, np.float32).reshape(ncol, P).T)


def pk8(vec):
    return pk(vec, 8)


def pk2x8(arr):
    a = np.asarray(arr, np.float32).reshape(2, 8, P)
    return np.ascontiguousarray(a.transpose(2, 0, 1))


def pk_head(arr):
    a = np.asarray(arr, np.float32)
    return np.ascontiguousarray(np.concatenate([a, a], axis=1).T)


def pk_b1(arr):
    out = np.zeros((P, 2, 44), np.float32)
    a = np.asarray(arr, np.float32)
    for l in range(2):
        for half in range(2):
            seg = a[l, half * 2730:(half + 1) * 2730]
            full = seg[0:2688].reshape(21, P).T
            out[:, l, half * 22:half * 22 + 21] = full
            out[0:42, l, half * 22 + 21] = seg[2688:2730]
    return out


def _in_maps(inputs):
    t = _tables()
    x = np.ascontiguousarray(inputs["L0_attn_x"], dtype=np.float32)
    maps = []
    for core in range(NC):
        b, hf = core // 2, core % 2
        t0 = hf * TOWN
        xb = x[b]
        if hf == 0:
            x_oh = np.concatenate([np.zeros((H, D), np.float32), xb[0:TOWN]], axis=0)
            x_oth = xb[TOWN:]
            rc_own = t["rc"][:, 0:TOWN]
            rs_own = t["rs"][:, 0:TOWN]
            rc_oth = t["rc"][:, TOWN:]
            rs_oth = t["rs"][:, TOWN:]
            xslc = slice(H - H, H - H + OH)
            lm = np.concatenate([t["mask_first"], t["mask_plain"][None]], axis=0)
            gate = np.zeros((P, 1), np.float32)
        else:
            x_oh = xb[TOWN - H:]
            x_oth = xb[0:TOWN]
            rc_own = t["rc"][:, TOWN:]
            rs_own = t["rs"][:, TOWN:]
            rc_oth = t["rc"][:, 0:TOWN]
            rs_oth = t["rs"][:, 0:TOWN]
            xslc = slice(TOWN, TOWN + OH)
            lm = np.concatenate([np.broadcast_to(t["mask_plain"], (4, P, 640)),
                                 t["mask_plain"][None]], axis=0)
            gate = np.ones((P, 1), np.float32)
        xoff = t0
        m = {
            "x_oh": np.ascontiguousarray(x_oh),
            "x_oth": np.ascontiguousarray(x_oth),
            "gnorm": pk8(inputs["L0_attn_g_norm"]),
            "wqkvA": np.ascontiguousarray(inputs["L0_attn_w_qkv"], np.float32),
            "bqk_t": pk(np.asarray(inputs["L0_attn_b_qkv"], np.float32)[0:2048], 16),
            "bqv_row": np.ascontiguousarray(
                np.asarray(inputs["L0_attn_b_qkv"], np.float32)[None, 2048:3072]),
            "wprojA": np.ascontiguousarray(inputs["L0_attn_w_proj"], np.float32),
            "bprojA": pk8(inputs["L0_attn_b_proj"]),
            "rc_own": np.ascontiguousarray(rc_own),
            "rs_own": np.ascontiguousarray(rs_own),
            "rc_oth": np.ascontiguousarray(rc_oth),
            "rs_oth": np.ascontiguousarray(rs_oth),
            "xqc": np.ascontiguousarray(t["xqc"][:, xoff:xoff + OH]),
            "xqs": np.ascontiguousarray(t["xqs"][:, xoff:xoff + OH]),
            "xkc": np.ascontiguousarray(t["xkc"][:, xoff:xoff + OH]),
            "xks": np.ascontiguousarray(t["xks"][:, xoff:xoff + OH]),
            "lmask": np.ascontiguousarray(lm),
            "segbf": t["segbf"],
            "perm32": t["perm32"],
            "segT": t["segT"],
            "gate": gate,
            "wqkv2": np.ascontiguousarray(inputs["L0_ffn_wqkv"], np.float32),
            "qsc": pk_head(inputs["L0_ffn_q_scale"]),
            "ksc": pk_head(inputs["L0_ffn_k_scale"]),
            "wo2": np.ascontiguousarray(inputs["L0_ffn_wo"], np.float32),
            "lng": pk2x8(inputs["L0_ffn_ln_g"]),
            "lnb": pk2x8(inputs["L0_ffn_ln_b"]),
            "w12": np.ascontiguousarray(inputs["L0_ffn_w1"], np.float32),
            "b12": pk_b1(inputs["L0_ffn_b1"]),
            "w22": np.ascontiguousarray(inputs["L0_ffn_w2"], np.float32),
            "b22": pk2x8(inputs["L0_ffn_b2"]),
        }
        maps.append(m)
    return maps


def kernel(**inputs):
    maps = _in_maps(inputs)
    nc = build_program()
    res = run_bass_kernel_spmd(nc, maps, core_ids=list(range(NC)))
    out = np.empty((NB, S, D), np.float32)
    for core in range(NC):
        b, hf = core // 2, core % 2
        out[b, hf * TOWN:(hf + 1) * TOWN] = res.results[core]["out_o"]
    return out


# version 31
# speedup vs baseline: 1.0503x; 1.0104x over previous
import contextlib
import math
import numpy as np
import ml_dtypes
import concourse.bass as bass
import concourse.mybir as mybir
from concourse.bass_utils import run_bass_kernel_spmd

F32 = mybir.dt.float32
BF16 = mybir.dt.bfloat16
AF = mybir.ActivationFunctionType
ALU = mybir.AluOpType
AX = mybir.AxisListType

P = 128
D = 1024
NB = 4
S = 8192
TOWN = 4096
H = 1152
OH = H + TOWN
NC = 8
C_A = float(np.float32(1.0986122886681098) / np.float32(32.0))
HID = 2730
W1C = 5460
INNER = 512
DHEAD = 64
NHEAD = 8
WIN = 512

N_PHASES = 5


_TBL = {}


def _tables():
    if _TBL:
        return _TBL
    import jax
    import jax.numpy as jnp
    cpu = jax.devices("cpu")[0]
    with jax.default_device(cpu):
        _tables_jax(jnp)
    return _TBL


def _tables_jax(jnp):
    import jax
    half = D // 2
    inv_freq = 1.0 / 10000.0 ** (jnp.arange(half, dtype=jnp.float32) / half)
    ang = jnp.arange(S, dtype=jnp.float32)[:, None] * inv_freq[None, :]
    cos = jnp.round(jnp.cos(ang))
    sin = jnp.round(jnp.sin(ang))
    C = jnp.concatenate([cos, cos], axis=-1)
    Sg = jnp.concatenate([sin, sin], axis=-1)
    Sf = np.asarray(Sg, np.float32).copy()
    Sf[:, :half] = -Sf[:, :half]
    rc = np.asarray(C, np.float32).T.astype(ml_dtypes.bfloat16)
    rs = Sf.T.astype(ml_dtypes.bfloat16)
    _TBL["rc"] = np.ascontiguousarray(rc)
    _TBL["rs"] = np.ascontiguousarray(rs)

    n = S
    half_idx = jnp.arange(0, DHEAD, 2, dtype=jnp.float32)
    inv_f = 1.0 / 10000.0 ** (half_idx / DHEAD)
    t_ext = jnp.arange(-H, n, dtype=jnp.float32)
    freqs = t_ext[:, None] * inv_f[None, :]
    freqs = jnp.concatenate([freqs, freqs], axis=-1)
    base = (half_idx + 0.4 * DHEAD) / (1.4 * DHEAD)
    power = (t_ext - n // 2) / float(WIN // 2)
    scale = base[None, :] ** power[:, None]
    scale = jnp.concatenate([scale, scale], axis=-1)
    cosx = jnp.cos(freqs)
    sinx = jnp.sin(freqs)
    qc = np.array(cosx * scale, np.float32)
    qs = np.array(sinx * scale, np.float32)
    kc = np.array(cosx / scale, np.float32)
    ks = np.array(sinx / scale, np.float32)
    for t in (qs, ks):
        t[:, :32] = -t[:, :32]
    def fm(t):
        tt = t.T
        return np.ascontiguousarray(np.concatenate([tt, tt], axis=0).astype(np.float32))
    _TBL["xqc"] = fm(qc)
    _TBL["xqs"] = fm(qs)
    _TBL["xkc"] = fm(kc)
    _TBL["xks"] = fm(ks)

    def band_mask(tbase):
        m = np.zeros((128, 640), np.float32)
        pp = np.arange(128)[:, None]
        cc = np.arange(640)[None, :]
        valid = (cc >= pp) & (cc <= pp + 512)
        if tbase < 512:
            valid &= (tbase - 512 + cc) >= 0
        m[~valid] = -1e30
        return m
    _TBL["mask_plain"] = band_mask(10_000)
    _TBL["mask_first"] = np.stack([band_mask(i * 128) for i in range(4)])

    pm = np.zeros((128, 128), np.float32)
    for kk in range(128):
        pm[kk, kk ^ 32] = 1.0
    _TBL["perm32"] = pm.astype(ml_dtypes.bfloat16)

    seg = np.zeros((128, 2), np.float32)
    seg[:64, 0] = 1.0
    seg[64:, 1] = 1.0
    _TBL["segbf"] = seg.astype(ml_dtypes.bfloat16)
    _TBL["segT"] = np.ascontiguousarray(seg.T)


class Ctr:
    def __init__(self):
        self.v = {}

    def add(self, sem, n):
        self.v[sem.name] = self.v.get(sem.name, 0) + n
        return self.v[sem.name]

    def cur(self, sem):
        return self.v.get(sem.name, 0)

    def snap(self):
        return dict(self.v)


class Env:
    pass


def _mk_env(nc, es):
    env = Env()
    env.nc = nc
    env.es = es
    sem = lambda n: es.enter_context(nc.semaphore(n))
    env.sD = sem("sD")
    env.sG = sem("sG")
    env.sT = sem("sT")
    env.sV = sem("sV")
    env.sA = sem("sA")
    env.c = Ctr()
    return env


def _fence(env, eng, sem, instr):
    v = env.c.add(sem, 1)
    instr.then_inc(sem, 1)
    eng.wait_ge(sem, v)


def _phase_barrier(env, blk, snap):
    for e in ("sync", "gpsimd", "tensor", "vector", "scalar"):
        rec = blk.recs[e]
        for s in (env.sD, env.sG, env.sT, env.sV, env.sA):
            v = snap.get(s.name, 0)
            if v:
                rec.calls.append((False, None, "wait_ge", (s, v), {}))


def _barrier_waits(env, eng, snap):
    for s in (env.sD, env.sG, env.sT, env.sV, env.sA):
        v = snap.get(s.name, 0)
        if v:
            eng.wait_ge(s, v)


class _ChainProxy:
    def __init__(self, rec, idx):
        self.rec = rec
        self.idx = idx

    def then_inc(self, sem, n):
        self.rec.calls.append((True, self.idx, "then_inc", (sem, n), {}))
        return self


class _Rec:
    def __init__(self):
        self.calls = []

    def __getattr__(self, name):
        rec = self

        def m(*a, **k):
            rec.calls.append((False, None, name, a, k))
            return _ChainProxy(rec, len(rec.calls) - 1)
        return m

    def replay(self, eng):
        res = []
        for chain, idx, name, a, k in self.calls:
            if chain:
                res.append(res[idx].then_inc(*a))
            else:
                res.append(getattr(eng, name)(*a, **k))


class DeferBlk:
    _ENGS = ("sync", "gpsimd", "tensor", "vector", "scalar")

    def __init__(self):
        self.recs = {e: _Rec() for e in self._ENGS}

    def _dec(self, e):
        def deco(f):
            f(self.recs[e])
        return deco

    @property
    def sync(self):
        return self._dec("sync")

    @property
    def gpsimd(self):
        return self._dec("gpsimd")

    @property
    def tensor(self):
        return self._dec("tensor")

    @property
    def vector(self):
        return self._dec("vector")

    @property
    def scalar(self):
        return self._dec("scalar")

    def flush(self, blk):
        for e in self._ENGS:
            rec = self.recs[e]
            if rec.calls:
                getattr(blk, e)(lambda eng, r=rec: r.replay(eng))


class _PhaseCtx:
    def __init__(self, env):
        self.env = env

    def __enter__(self):
        return self.env.dblk

    def __exit__(self, et, ev, tb):
        return False


class _DeferCtx:
    def __init__(self, nc, name, **kw):
        self.nc = nc
        self.name = name
        self.kw = kw
        self.d = DeferBlk()

    def __enter__(self):
        return self.d

    def __exit__(self, et, ev, tb):
        if et is None:
            with self.nc.Block(self.name, **self.kw) as b:
                self.d.flush(b)
        return False


def build_program(taps=()):
    nc = bass.Bass("TRN2", target_bir_lowering=False)
    es = contextlib.ExitStack()
    env = _mk_env(nc, es)
    env.dblk = DeferBlk()
    env.taps = set(taps)

    di = lambda n, s, d=F32: nc.dram_tensor(n, s, d, kind="ExternalInput").ap()
    I = {}
    I["x_oh"] = di("x_oh", [OH, D])
    I["x_oth"] = di("x_oth", [TOWN, D])
    I["gnorm"] = di("gnorm", [P, 8])
    I["wqkvA"] = di("wqkvA", [D, 3 * D])
    I["bqk_t"] = di("bqk_t", [P, 16])
    I["bqv_row"] = di("bqv_row", [1, D])
    I["wprojA"] = di("wprojA", [D, D])
    I["bprojA"] = di("bprojA", [P, 8])
    I["rc_own"] = di("rc_own", [D, TOWN], BF16)
    I["rs_own"] = di("rs_own", [D, TOWN], BF16)
    I["rc_oth"] = di("rc_oth", [D, TOWN], BF16)
    I["rs_oth"] = di("rs_oth", [D, TOWN], BF16)
    I["xqc"] = di("xqc", [P, OH])
    I["xqs"] = di("xqs", [P, OH])
    I["xkc"] = di("xkc", [P, OH])
    I["xks"] = di("xks", [P, OH])
    I["lmask"] = di("lmask", [5, P, 640])
    I["segbf"] = di("segbf", [P, 2], BF16)
    I["perm32"] = di("perm32", [P, P], BF16)
    I["segT"] = di("segT", [2, P])
    I["gate"] = di("gate", [P, 1])
    I["wqkv2"] = di("wqkv2", [2, D, 3 * INNER])
    I["qsc"] = di("qsc", [P, 2])
    I["ksc"] = di("ksc", [P, 2])
    I["wo2"] = di("wo2", [2, INNER, D])
    I["lng"] = di("lng", [P, 2, 8])
    I["lnb"] = di("lnb", [P, 2, 8])
    I["w12"] = di("w12", [2, D, W1C])
    I["b12"] = di("b12", [P, 2, 44])
    I["w22"] = di("w22", [2, HID, D])
    I["b22"] = di("b22", [P, 2, 8])
    env.I = I

    out_o = nc.dram_tensor("out_o", [TOWN, D], F32, kind="ExternalOutput").ap()

    def scratch(name, shape, dt):
        kind = "ExternalOutput" if name in env.taps else "Internal"
        return nc.dram_tensor(name, shape, dt, kind=kind).ap()

    Z = {}
    Z["xnF"] = scratch("xnF", [D, OH], F32)
    Z["yA"] = scratch("yA", [D, OH], F32)
    Z["yB"] = scratch("yB", [D, OH], F32)
    Z["qfA"] = scratch("qfA", [D, TOWN], BF16)
    Z["kfA"] = scratch("kfA", [D, S], BF16)
    Z["vtA"] = scratch("vtA", [S, D], BF16)
    Z["sF"] = scratch("sF", [D, TOWN], F32)
    Z["wqkvA_bf"] = scratch("wqkvA_bf", [D, 3 * D], BF16)
    Z["wprojA_bf"] = scratch("wprojA_bf", [D, D], BF16)
    Z["wqkv2_bf"] = scratch("wqkv2_bf", [2, D, 3 * INNER], BF16)
    Z["wo2_bf"] = scratch("wo2_bf", [2, INNER, D], BF16)
    Z["w12_bf"] = scratch("w12_bf", [2, D, W1C], BF16)
    Z["w22_bf"] = scratch("w22_bf", [2, HID, D], BF16)
    env.Z = Z
    env.out_o = out_o

    sb = lambda n, s, d: es.enter_context(nc.sbuf_tensor(n, s, d))
    pt = Env()
    pt.identb = sb("identb", [P, P], BF16)
    pt.ident3 = sb("ident3", [P, P], F32)
    pt.ones1r = sb("ones1r", [1, P], F32)
    pt.onesc = sb("onesc", [P, 1], F32)
    pt.onesb = sb("onesb", [P, 1], BF16)
    pt.gn = sb("gn", [P, 8], F32)
    pt.bqkv = sb("bqkv", [P, 16], F32)
    pt.bvb = sb("bvb", [P, D], F32)
    pt.bproj = sb("bproj", [P, 8], F32)
    pt.segb = sb("segb_sb", [P, 2], BF16)
    pt.perm32 = sb("perm32_sb", [P, P], BF16)
    pt.segT = sb("segT_sb", [2, P], F32)
    pt.gate = sb("gate_sb", [P, 1], F32)
    pt.qs8 = sb("qs8", [P, 2], F32)
    pt.ksc = sb("kscs", [P, 2], F32)
    pt.lng = sb("lngs", [P, 2, 8], F32)
    pt.lnb = sb("lnbs", [P, 2, 8], F32)
    pt.b2s = sb("b2s", [P, 2, 8], F32)
    pt.b1s = sb("b1s", [P, 2, 44], F32)
    pt.eps = sb("eps", [P, 1], F32)
    pt.eps6 = sb("eps6", [P, 1], F32)
    pt.eps24 = sb("eps24", [P, 1], F32)
    env.pt = pt

    _emit_init(env)
    with contextlib.ExitStack() as wces:
        _emit_wcast(env, wces)
        if N_PHASES >= 1:
            _emit_ph0(env)
    if N_PHASES >= 2:
        _emit_attnA(env)
    if N_PHASES >= 3:
        _emit_zeroinit(env)
        _emit_slA(env, 0)
        _emit_slF(env, 0)
        _emit_gate(env, "yB")
    if N_PHASES >= 4:
        _emit_slA(env, 1)
        _emit_gate(env, "yA")
        _emit_slF(env, 1)
    if N_PHASES >= 5:
        _emit_final(env)
    _emit_tail(env)

    with nc.Block("main") as _mb:
        env.dblk.flush(_mb)
    es.close()
    return nc


def _emit_init(env):
    nc, I, pt, c = env.nc, env.I, env.pt, env.c
    sD, sG = env.sD, env.sG

    with _PhaseCtx(env) as blk:
        @blk.gpsimd
        def _(g):
            g.memset(pt.identb[:], 0.0).then_inc(env.sG, 1)
            g.affine_select(out=pt.identb[:], in_=pt.identb[:], compare_op=ALU.not_equal,
                            fill=1.0, base=0, pattern=[[-1, P]],
                            channel_multiplier=1).then_inc(env.sG, 1)
            g.memset(pt.ident3[:], 0.0).then_inc(env.sG, 1)
            g.affine_select(out=pt.ident3[:], in_=pt.ident3[:], compare_op=ALU.not_equal,
                            fill=1.0, base=0, pattern=[[-1, P]],
                            channel_multiplier=1).then_inc(env.sG, 1)
            g.memset(pt.ones1r[:], 1.0).then_inc(env.sG, 1)
            g.memset(pt.onesc[:], 1.0).then_inc(env.sG, 1)
            g.memset(pt.onesb[:], 1.0).then_inc(env.sG, 1)
            g.memset(pt.eps[:], 1e-5).then_inc(env.sG, 1)
            g.memset(pt.eps6[:], 1e-6).then_inc(env.sG, 1)
            g.memset(pt.eps24[:], 1e-24).then_inc(env.sG, 1)
            c.add(sG, 10)

        @blk.sync
        def _(s):
            def ld(dst, src):
                s.dma_start(out=dst, in_=src).then_inc(sD, 16)
                c.add(sD, 16)
            ld(pt.gn[:], I["gnorm"][:])
            ld(pt.bqkv[:], I["bqk_t"][:])
            ld(pt.bproj[:], I["bprojA"][:])
            ld(pt.segb[:], I["segbf"][:])
            ld(pt.perm32[:], I["perm32"][:])
            ld(pt.segT[:], I["segT"][:])
            ld(pt.gate[:], I["gate"][:])
            ld(pt.qs8[:], I["qsc"][:])
            ld(pt.ksc[:], I["ksc"][:])
            ld(pt.lng[:], I["lng"][:])
            ld(pt.lnb[:], I["lnb"][:])
            ld(pt.b2s[:], I["b22"][:])
            ld(pt.b1s[:], I["b12"][:])

    snap0 = c.snap()

    with (nc.psum_tensor("ps_init", [P, 512], F32) as ps,
          nc.sbuf_tensor("bvrow", [1, D], F32) as bvrow,
          _PhaseCtx(env) as blk):
        @blk.sync
        def _(s):
            _barrier_waits(env, s, snap0)
            s.dma_start(out=bvrow[:], in_=I["bqv_row"][:]).then_inc(sD, 16)
            c.add(sD, 16)

        vready = c.cur(env.sD)

        @blk.tensor
        def _(t):
            t.wait_ge(sD, vready)
            t.matmul(ps[:, 0:512], env.pt.ones1r[0:1, :], bvrow[0:1, 0:512],
                     start=True, stop=True).then_inc(env.sT, 1)
            c.add(env.sT, 1)

        @blk.vector
        def _(v):
            v.wait_ge(env.sT, c.cur(env.sT))
            v.tensor_copy(pt.bvb[:, 0:512], ps[:, 0:512]).then_inc(env.sV, 1)
            c.add(env.sV, 1)

        t2 = c.cur(env.sV)

        @blk.tensor
        def _(t):
            t.wait_ge(env.sV, t2)
            t.matmul(ps[:, 0:512], env.pt.ones1r[0:1, :], bvrow[0:1, 512:1024],
                     start=True, stop=True).then_inc(env.sT, 1)
            c.add(env.sT, 1)

        @blk.vector
        def _(v):
            v.wait_ge(env.sT, c.cur(env.sT))
            v.tensor_copy(pt.bvb[:, 512:1024], ps[:, 0:512]).then_inc(env.sV, 1)
            c.add(env.sV, 1)
            v.wait_ge(env.sG, 1)
            v.tensor_scalar_mul(pt.qs8[:], pt.qs8[:], 8.0).then_inc(env.sV, 1)
            c.add(env.sV, 1)

    env.snap = c.snap()


def _emit_wcast(env, wces):
    nc, I, Z, c = env.nc, env.I, env.Z, env.c
    sG = env.sG

    jobs = []
    def add2d(srcap, dstap, rows, ncols):
        for r in range(0, rows, 128):
            rr = min(128, rows - r)
            for c0 in range(0, ncols, HID):
                cw = min(HID, ncols - c0)
                jobs.append((srcap[r:r + rr, c0:c0 + cw],
                             dstap[r:r + rr, c0:c0 + cw], rr, cw))
    add2d(I["wqkvA"], Z["wqkvA_bf"], D, 3 * D)
    add2d(I["wprojA"], Z["wprojA_bf"], D, D)
    for l in range(2):
        add2d(I["wqkv2"][l], Z["wqkv2_bf"][l], D, 3 * INNER)
        add2d(I["wo2"][l], Z["wo2_bf"][l], INNER, D)
        add2d(I["w12"][l], Z["w12_bf"][l], D, W1C)
        add2d(I["w22"][l], Z["w22_bf"][l], HID, D)
    wq_jobs = 16

    b32 = [wces.enter_context(nc.sbuf_tensor("wc_f0", [P, HID], F32))]
    b16 = [wces.enter_context(nc.sbuf_tensor("wc_b0", [P, HID], BF16))]

    snap = env.snap
    wq_mark = [0]

    with _PhaseCtx(env) as blk:
        @blk.gpsimd
        def _(g):
            _barrier_waits(env, g, snap)
            store_done = [0]
            for i, (sap, dap, rr, ncols) in enumerate(jobs):
                sl = 0
                if store_done[sl]:
                    g.wait_ge(sG, store_done[sl])
                g.dma_start(out=b32[sl][0:rr, 0:ncols], in_=sap).then_inc(sG, 16)
                ld = c.add(sG, 16)
                g.wait_ge(sG, ld)
                g.tensor_copy(b16[sl][0:rr, 0:ncols], b32[sl][0:rr, 0:ncols]).then_inc(sG, 1)
                cv = c.add(sG, 1)
                g.wait_ge(sG, cv)
                g.dma_start(out=dap, in_=b16[sl][0:rr, 0:ncols]).then_inc(sG, 16)
                store_done[sl] = c.add(sG, 16)
                if i == wq_jobs - 1:
                    wq_mark[0] = max(store_done)
    env.wq_ready = wq_mark[0]


def _emit_ph0(env):
    nc, I, Z, pt, c = env.nc, env.I, env.Z, env.pt, env.c
    sD, sG, sT, sV, sA = env.sD, env.sG, env.sT, env.sV, env.sA

    groups = []
    groups.append(("oh", 0, 512, "halo"))
    groups.append(("oh", 512, 512, "halo"))
    groups.append(("oh", 1024, 128, "halo"))
    for g in range(8):
        groups.append(("oh", 1152 + 512 * g, 512, "own"))
    for g in range(8):
        groups.append(("oth", 512 * g, 512, "oth"))

    with contextlib.ExitStack() as ph:
        sb = lambda n, s, d: ph.enter_context(nc.sbuf_tensor(n, s, d))
        wq_bf = sb("ph0_wq", [P, 8, 3 * D], BF16)
        xin = [sb(f"ph0_xin{i}", [P, 4, D], F32) for i in range(2)]
        stat = [sb(f"ph0_stat{i}", [P, 4 * 6], F32) for i in range(2)]
        xc = [sb(f"ph0_xc{i}", [P, 4, D], F32) for i in range(2)]
        ha_t = sb("ph0_hat", [P, 4, D], BF16)
        ha_f = sb("ph0_haf", [P, 8, 512], BF16)
        stg = sb("ph0_stg", [P, 512], F32)
        qtmp = sb("ph0_qtmp", [P, 8, 512], F32)
        rstg = sb("ph0_rstg", [P, 512], BF16)
        rc_s = [sb("ph0_rc0", [P, 8, 512], BF16)]
        rs_s = [sb("ph0_rs0", [P, 8, 512], BF16)]
        vstg = sb("ph0_vstg", [P, 1024], BF16)
        rstg2 = sb("ph0_rstg2", [P, 512], F32)
        ps1 = ph.enter_context(nc.psum_tensor("ph0_ps1", [P, 512], F32))
        ps2 = ph.enter_context(nc.psum_tensor("ph0_ps2", [P, 512], F32))
        psb = ph.enter_context(nc.psum_tensor("ph0_psb", [P, 1024], BF16))
        psv = ph.enter_context(nc.psum_tensor("ph0_psv", [P, 1024], F32))

        with _PhaseCtx(env) as blk:
            snap = env.snap
            _phase_barrier(env, blk, snap)

            @blk.sync
            def _(s):
                _barrier_waits(env, s, snap)
                s.wait_ge(env.sG, env.wq_ready)
                s.dma_start(out=wq_bf[:], in_=Z["wqkvA_bf"].rearrange("(c p) n -> p c n", p=P)
                            ).then_inc(sD, 16)
                c.add(sD, 16)

            bufs = dict(wq_bf=wq_bf, xin=xin, stat=stat, xc=xc, ha_t=ha_t,
                        ha_f=ha_f, stg=stg, qtmp=qtmp, rstg=rstg, rc_s=rc_s,
                        rs_s=rs_s, vstg=vstg, rstg2=rstg2, ps1=ps1, ps2=ps2,
                        psb=psb, psv=psv)
            prev = {}
            pend = None
            for gi, (src, r0, nr, kind) in enumerate(groups):
                if pend is not None:
                    _ph0_back(env, blk, bufs, pend)
                fr = _ph0_front(env, blk, bufs, gi, src, r0, nr, kind, prev)
                prev = fr["prev2"]
                pend = fr
            if pend is not None:
                _ph0_back(env, blk, bufs, pend)

    env.snap = c.snap()


def _ph0_front(env, blk, bufs, gi, src, r0, nr, kind, prev):
    nc, I, Z, pt, c = env.nc, env.I, env.Z, env.pt, env.c
    sD, sG, sT, sV, sA = env.sD, env.sG, env.sT, env.sV, env.sA
    nt = nr // 128
    par = gi % 2
    xin, stat, xc = bufs["xin"][par], bufs["stat"][par], bufs["xc"][par]
    ha_t, stg, psv = bufs["ha_t"], bufs["stg"], bufs["psv"]
    xsrc = I["x_oh"] if src == "oh" else I["x_oth"]
    do_xn = kind in ("halo", "own")
    do_qkv = kind in ("own", "oth")
    own_q = kind == "own"
    xn_c0 = r0 if src == "oh" else None
    key_c0 = (r0 - H) if src == "oh" else (TOWN + r0)
    pf = prev.get(par, {})
    out = {}

    @blk.sync
    def _(s, pf=pf):
        if pf.get("xin_T"):
            s.wait_ge(sT, pf["xin_T"])
        if pf.get("xin_V"):
            s.wait_ge(sV, pf["xin_V"])
        s.dma_start(out=xin[:, 0:nt, :],
                    in_=xsrc[r0:r0 + nr, :].rearrange("(a p) f -> p a f", p=P)
                    ).then_inc(sD, 16)
        c.add(sD, 16)
    ld_done = c.cur(sD)

    @blk.vector
    def _(v, pf=pf):
        v.wait_ge(sD, ld_done)
        if pf.get("stat_A"):
            v.wait_ge(sA, pf["stat_A"])
        for a in range(nt):
            last = v.reduce_sum(stat[:, a * 6:a * 6 + 1], xin[:, a, :], axis=AX.X)
        _fence(env, v, sV, last)
        for a in range(nt):
            last = v.tensor_scalar_mul(stat[:, a * 6 + 1:a * 6 + 2], stat[:, a * 6:a * 6 + 1], 1.0 / D)
        _fence(env, v, sV, last)
        for a in range(nt):
            last = v.tensor_scalar(xc[:, a, :], xin[:, a, :], stat[:, a * 6 + 1:a * 6 + 2],
                                   None, op0=ALU.subtract)
        last.then_inc(sV, 1)
        c.add(sV, 1)
    xc_done = c.cur(sV)

    @blk.scalar
    def _(sc, tw=c.cur(sT)):
        sc.wait_ge(sV, xc_done)
        sc.wait_ge(sT, tw)
        for a in range(nt):
            last = sc.activation(out=ha_t[:, a, :], in_=xc[:, a, :], func=AF.Square)
        last.then_inc(sA, 1)
        c.add(sA, 1)
    sq_done = c.cur(sA)

    @blk.vector
    def _(v):
        v.wait_ge(sA, sq_done)
        for a in range(nt):
            last = v.reduce_sum(stat[:, a * 6 + 2:a * 6 + 3], ha_t[:, a, :], axis=AX.X)
        last.then_inc(sV, 1)
        c.add(sV, 1)
    ss_done = c.cur(sV)

    @blk.scalar
    def _(sc):
        sc.wait_ge(sV, ss_done)
        for a in range(nt):
            last = sc.activation(out=stat[:, a * 6 + 3:a * 6 + 4], in_=stat[:, a * 6 + 2:a * 6 + 3],
                                 func=AF.Ln, bias=pt.eps[:, 0:1], scale=1.0 / D)
        _fence(env, sc, sA, last)
        for a in range(nt):
            last = sc.activation(out=stat[:, a * 6 + 2:a * 6 + 3], in_=stat[:, a * 6 + 3:a * 6 + 4],
                                 func=AF.Exp, scale=-0.5)
        if do_qkv:
            for a in range(nt):
                last = sc.activation(out=stat[:, a * 6 + 4:a * 6 + 5], in_=stat[:, a * 6 + 3:a * 6 + 4],
                                     func=AF.Exp, scale=-1.0)
        last.then_inc(sA, 1)
        c.add(sA, 1)
    r1_done = c.cur(sA)

    if do_qkv:
        @blk.vector
        def _(v):
            v.wait_ge(sA, r1_done)
            for a in range(nt):
                last = v.tensor_scalar(stat[:, a * 6 + 4:a * 6 + 5], stat[:, a * 6 + 4:a * 6 + 5],
                                       -1e-5, 1.0, op0=ALU.mult, op1=ALU.add)
            last.then_inc(sV, 1)
            c.add(sV, 1)
        ms_done = c.cur(sV)

        @blk.scalar
        def _(sc):
            sc.wait_ge(sV, ms_done)
            for a in range(nt):
                last = sc.activation(out=stat[:, a * 6 + 5:a * 6 + 6], in_=stat[:, a * 6 + 4:a * 6 + 5],
                                     func=AF.Ln, bias=pt.eps6[:, 0:1])
            last.then_inc(sA, 1)
            c.add(sA, 1)
        w_done = c.cur(sA)

        @blk.vector
        def _(v):
            v.wait_ge(sA, w_done)
            for a in range(nt):
                last = v.tensor_add(stat[:, a * 6 + 5:a * 6 + 6], stat[:, a * 6 + 5:a * 6 + 6],
                                    stat[:, a * 6 + 3:a * 6 + 4])
            last.then_inc(sV, 1)
            c.add(sV, 1)
        uw_done = c.cur(sV)

        @blk.scalar
        def _(sc):
            sc.wait_ge(sV, uw_done)
            for a in range(nt):
                last = sc.activation(out=stat[:, a * 6 + 5:a * 6 + 6], in_=stat[:, a * 6 + 5:a * 6 + 6],
                                     func=AF.Exp, scale=-0.5)
            last.then_inc(sA, 1)
            c.add(sA, 1)

    stats_done = c.cur(sA)
    out["stat_A"] = stats_done

    @blk.vector
    def _(v):
        v.wait_ge(sA, stats_done)
        last = None
        if do_qkv:
            for a in range(nt):
                last = v.tensor_scalar(ha_t[:, a, :], xc[:, a, :], stat[:, a * 6 + 5:a * 6 + 6],
                                       None, op0=ALU.mult)
        if do_xn:
            for a in range(nt):
                last = v.tensor_scalar(xin[:, a, :], xc[:, a, :], stat[:, a * 6 + 2:a * 6 + 3],
                                       None, op0=ALU.mult)
        last.then_inc(sV, 1)
        c.add(sV, 1)
    norm_done = c.cur(sV)

    if do_xn:
        for cc in range(8):
            tw = c.cur(sV)
            aw3 = c.cur(sA)

            @blk.tensor
            def _(t, cc=cc, tw=tw, aw3=aw3):
                t.wait_ge(sV, tw)
                t.wait_ge(sA, aw3)
                for a in range(nt):
                    last = t.transpose(psv[:, a * 128:(a + 1) * 128],
                                       xin[:, a, cc * 128:(cc + 1) * 128], pt.ident3[:])
                last.then_inc(sT, 1)
                c.add(sT, 1)
            tp_done = c.cur(sT)
            dprev = c.cur(sD)

            @blk.scalar
            def _(sc, cc=cc, tp_done=tp_done, dprev=dprev):
                sc.wait_ge(sT, tp_done)
                sc.wait_ge(sD, dprev)
                sc.activation(out=stg[:, 0:nr], in_=psv[:, 0:nr],
                              func=AF.Copy).then_inc(sA, 1)
                c.add(sA, 1)
            ev_done = c.cur(sA)

            @blk.sync
            def _(s, cc=cc, ev_done=ev_done):
                s.wait_ge(sA, ev_done)
                s.dma_start(out=Z["xnF"][cc * 128:(cc + 1) * 128, xn_c0:xn_c0 + nr],
                            in_=stg[:, 0:nr]).then_inc(sD, 16)
                c.add(sD, 16)
    out["xin_T"] = c.cur(sT)
    out["xin_V"] = norm_done

    prev2 = dict(prev)
    prev2[par] = out
    return dict(prev2=prev2, gi=gi, src=src, r0=r0, nr=nr, kind=kind, par=par,
                out=out, norm_done=norm_done, key_c0=key_c0)


def _ph0_back(env, blk, bufs, fr):
    nc, I, Z, pt, c = env.nc, env.I, env.Z, env.pt, env.c
    sD, sG, sT, sV, sA = env.sD, env.sG, env.sT, env.sV, env.sA
    kind, r0, nr, par = fr["kind"], fr["r0"], fr["nr"], fr["par"]
    do_qkv = kind in ("own", "oth")
    if not do_qkv:
        return
    own_q = kind == "own"
    key_c0 = fr["key_c0"]
    nt = nr // 128
    norm_done = fr["norm_done"]
    wq_bf, ha_t, ha_f = bufs["wq_bf"], bufs["ha_t"], bufs["ha_f"]
    qtmp, rstg, rstg2 = bufs["qtmp"], bufs["rstg"], bufs["rstg2"]
    rc_s, rs_s = bufs["rc_s"][0], bufs["rs_s"][0]
    vstg, ps1, psb, psv = bufs["vstg"], bufs["ps1"], bufs["psb"], bufs["psv"]

    @blk.sync
    def _(s, vw0=c.cur(sV)):
        s.wait_ge(sV, vw0)
        rc = I["rc_own"] if own_q else I["rc_oth"]
        rs = I["rs_own"] if own_q else I["rs_oth"]
        col = (r0 - H) if own_q else r0
        s.dma_start(out=rc_s[:],
                    in_=rc[:, col:col + 512].rearrange("(c p) n -> p c n", p=P)
                    ).then_inc(sD, 16)
        c.add(sD, 16)
        s.dma_start(out=rs_s[:],
                    in_=rs[:, col:col + 512].rearrange("(c p) n -> p c n", p=P)
                    ).then_inc(sD, 16)
        c.add(sD, 16)
    tbl_ok = c.cur(sD)

    for cc in range(8):
        tw = c.cur(sV)
        aw2 = c.cur(sA)

        @blk.tensor
        def _(t, cc=cc, tw=tw, aw2=aw2):
            t.wait_ge(sV, max(tw, norm_done))
            t.wait_ge(sA, aw2)
            for a in range(nt):
                last = t.transpose(psb[:, a * 128:(a + 1) * 128],
                                   ha_t[:, a, cc * 128:(cc + 1) * 128], pt.identb[:])
            last.then_inc(sT, 1)
            c.add(sT, 1)
        tp2 = c.cur(sT)

        @blk.scalar
        def _(sc, cc=cc, tp2=tp2):
            sc.wait_ge(sT, tp2)
            sc.activation(out=ha_f[:, cc, 0:nr], in_=psb[:, 0:nr], func=AF.Copy,
                          scale=pt.gn[:, cc:cc + 1]).then_inc(sA, 1)
            c.add(sA, 1)
    haf_done = c.cur(sA)

    col_sets = []
    if own_q:
        col_sets.append(("q", 0))
    col_sets.append(("k", 8))
    for nmk, cbase in col_sets:
        for oc in range(8):
            aw = c.cur(sA)
            vw = c.cur(sV)

            @blk.tensor
            def _(t, oc=oc, cbase=cbase, aw=aw):
                t.wait_ge(sA, max(haf_done, aw))
                for f in range(8):
                    last = t.matmul(ps1[:, 0:nr],
                                    wq_bf[:, f, (cbase + oc) * 128:(cbase + oc + 1) * 128],
                                    ha_f[:, f, 0:nr], start=(f == 0), stop=(f == 7))
                last.then_inc(sT, 1)
                c.add(sT, 1)
            mm_done = c.cur(sT)

            @blk.scalar
            def _(sc, oc=oc, cbase=cbase, mm_done=mm_done, vw=vw):
                sc.wait_ge(sT, mm_done)
                sc.wait_ge(sV, vw)
                sc.activation(out=qtmp[:, oc, 0:nr], in_=ps1[:, 0:nr], func=AF.Identity,
                              bias=pt.bqkv[:, cbase + oc:cbase + oc + 1]).then_inc(sA, 1)
                c.add(sA, 1)
        qk_done = c.cur(sA)

        for oc in range(8):
            dw = c.cur(sD)

            @blk.vector
            def _(v, oc=oc, dw=dw, qk_done=qk_done):
                v.wait_ge(sA, qk_done)
                v.wait_ge(sD, max(dw, tbl_ok))
                v.tensor_tensor(rstg2[:, 0:nr], qtmp[:, oc, 0:nr], rc_s[:, oc, 0:nr],
                                op=ALU.mult)
                last = v.tensor_tensor(bufs["xc"][par][:, 0, 0:nr], qtmp[:, oc ^ 4, 0:nr],
                                       rs_s[:, oc, 0:nr], op=ALU.mult)
                _fence(env, v, sV, last)
                v.tensor_tensor(rstg[:, 0:nr], rstg2[:, 0:nr],
                                bufs["xc"][par][:, 0, 0:nr], op=ALU.add).then_inc(sV, 1)
                c.add(sV, 1)
            rp_done = c.cur(sV)

            @blk.sync
            def _(s, oc=oc, nmk=nmk, rp_done=rp_done):
                s.wait_ge(sV, rp_done)
                dst = Z["qfA"] if nmk == "q" else Z["kfA"]
                col0 = (r0 - H) if nmk == "q" else key_c0
                s.dma_start(out=dst[oc * 128:(oc + 1) * 128, col0:col0 + nr],
                            in_=rstg[:, 0:nr]).then_inc(sD, 16)
                c.add(sD, 16)

    for a in range(nt):
        vw = c.cur(sV)
        aw4 = c.cur(sA)

        @blk.tensor
        def _(t, a=a, vw=vw, aw4=aw4):
            t.wait_ge(sA, max(haf_done, aw4))
            t.wait_ge(sV, vw)
            for f in range(8):
                for vh in range(2):
                    last = t.matmul(psv[:, vh * 512:(vh + 1) * 512],
                                    ha_f[:, f, a * 128:(a + 1) * 128],
                                    wq_bf[:, f, 2048 + vh * 512:2048 + (vh + 1) * 512],
                                    start=(f == 0), stop=(f == 7))
            last.then_inc(sT, 1)
            c.add(sT, 1)
        vm_done = c.cur(sT)
        dw = c.cur(sD)

        @blk.vector
        def _(v, a=a, vm_done=vm_done, dw=dw):
            v.wait_ge(sT, vm_done)
            v.wait_ge(sD, dw)
            v.tensor_tensor(vstg[:, 0:1024], psv[:, 0:1024], pt.bvb[:, 0:1024],
                            op=ALU.add).then_inc(sV, 1)
            c.add(sV, 1)
        ve_done = c.cur(sV)

        @blk.sync
        def _(s, a=a, ve_done=ve_done):
            s.wait_ge(sV, ve_done)
            s.dma_start(out=Z["vtA"][key_c0 + a * 128:key_c0 + (a + 1) * 128, :],
                        in_=vstg[:, 0:1024]).then_inc(sD, 16)
            c.add(sD, 16)


def _emit_attnA(env):
    nc, I, Z, pt, c = env.nc, env.I, env.Z, env.pt, env.c
    sD, sG, sT, sV, sA = env.sD, env.sG, env.sT, env.sV, env.sA

    with contextlib.ExitStack() as ph:
        sb = lambda n, s, d: ph.enter_context(nc.sbuf_tensor(n, s, d))
        QSB = sb("aA_q", [P, 8, 1024], BF16)
        KSB = [sb(f"aA_k{i}", [P, 8, 1024], BF16) for i in range(2)]
        VSB = [sb(f"aA_v{i}", [P, 8, 1024], BF16) for i in range(2)]
        OACC = sb("aA_oacc", [P, 8, 1024], F32)
        PSB = [sb(f"aA_p{i}", [P, 1024], BF16) for i in range(2)]
        PTS = [sb(f"aA_pt{i}", [P, 1024], BF16) for i in range(2)]
        OFSB = sb("aA_of", [P, 8, 1024], BF16)
        WP = sb("aA_wp", [P, 8, 1024], BF16)
        mcur = sb("aA_m", [P, 8], F32)
        lcur = sb("aA_l", [P, 8], F32)
        mx = [sb(f"aA_mx{i}", [P, 1], F32) for i in range(2)]
        mnew = [sb(f"aA_mn{i}", [P, 1], F32) for i in range(2)]
        dd = [sb(f"aA_dd{i}", [P, 1], F32) for i in range(2)]
        af = [sb(f"aA_af{i}", [P, 1], F32) for i in range(2)]
        bias = [sb(f"aA_bias{i}", [P, 1], F32) for i in range(2)]
        lch = [sb(f"aA_lch{i}", [P, 1], F32) for i in range(2)]
        rl = sb("aA_rl", [P, 1], F32)
        obf = sb("aA_obf", [P, 1024], BF16)
        stg = sb("aA_stg", [P, 512], F32)
        xns = sb("aA_xns", [P, 512], F32)
        psS = [ph.enter_context(nc.psum_tensor(f"aA_psS{i}", [P, 1024], F32))
               for i in range(2)]
        psT = ph.enter_context(nc.psum_tensor("aA_psT", [P, 1024], BF16))
        psO = ph.enter_context(nc.psum_tensor("aA_psO", [P, 1024], F32))
        ps1 = ph.enter_context(nc.psum_tensor("aA_ps1", [P, 512], F32))

        with _PhaseCtx(env) as blk:
            snap = env.snap
            _phase_barrier(env, blk, snap)

            @blk.sync
            def _(s):
                _barrier_waits(env, s, snap)
                s.dma_start(out=WP[:], in_=Z["wprojA_bf"].rearrange("(c p) n -> p c n", p=P)
                            ).then_inc(sD, 16)
                c.add(sD, 16)

            for qpass in range(4):
                _attnA_pass(env, blk, qpass, QSB, KSB, VSB, OACC, PSB, PTS, OFSB,
                            WP, mcur, lcur, mx, mnew, dd, af, bias, lch, rl, obf,
                            stg, xns, psS, psT, psO, ps1)

    env.snap = c.snap()


def _attnA_pass(env, blk, qpass, QSB, KSB, VSB, OACC, PSB, PTS, OFSB, WP,
                mcur, lcur, mx, mnew, dd, af, bias, lch, rl, obf, stg, xns,
                psS, psT, psO, ps1):
    nc, I, Z, pt, c = env.nc, env.I, env.Z, env.pt, env.c
    sD, sG, sT, sV, sA = env.sD, env.sG, env.sT, env.sV, env.sA
    q0 = qpass * 1024

    exp_done = [0, 0]
    trans_done = [0, 0]
    pv_done = [0, 0]
    mch_done = [0, 0]
    lo_done = [0, 0]

    tprev, vprev, aprev = c.cur(sT), c.cur(sV), c.cur(sA)

    @blk.sync
    def _(s):
        s.wait_ge(sT, tprev)
        s.wait_ge(sV, vprev)
        s.wait_ge(sA, aprev)
        for f in range(8):
            s.dma_start(out=QSB[:, f, :],
                        in_=Z["qfA"][f * 128:(f + 1) * 128, q0:q0 + 1024]).then_inc(sD, 16)
            c.add(sD, 16)

    kv_users = [tprev, tprev]
    kv_ok = [0] * 9

    def load_kv(kc):
        sl = kc % 2

        @blk.sync
        def _(s, kc=kc, sl=sl, tw=kv_users[sl]):
            s.wait_ge(sT, tw)
            for f in range(8):
                s.dma_start(out=KSB[sl][:, f, :],
                            in_=Z["kfA"][f * 128:(f + 1) * 128, kc * 1024:(kc + 1) * 1024]
                            ).then_inc(sD, 16)
                c.add(sD, 16)
            for u in range(8):
                s.dma_start(out=VSB[sl][:, u, :],
                            in_=Z["vtA"][kc * 1024 + u * 128:kc * 1024 + (u + 1) * 128, :]
                            ).then_inc(sD, 16)
                c.add(sD, 16)
        kv_ok[kc] = c.cur(sD)

    load_kv(0)
    for kc in range(8):
        sl = kc % 2
        if kc + 1 < 8:
            load_kv(kc + 1)

        def emit_S(qt):
            par = qt % 2

            @blk.tensor
            def _(t, qt=qt, par=par, sl=sl, ko=kv_ok[kc], ew=exp_done[par]):
                t.wait_ge(sD, ko)
                if ew:
                    t.wait_ge(sA, ew)
                for f in range(8):
                    for kk in range(2):
                        last = t.matmul(psS[par][:, kk * 512:(kk + 1) * 512],
                                        QSB[:, f, qt * 128:(qt + 1) * 128],
                                        KSB[sl][:, f, kk * 512:(kk + 1) * 512],
                                        start=(f == 0), stop=(f == 7))
                last.then_inc(sT, 1)
                c.add(sT, 1)
            s_done = c.cur(sT)

            @blk.vector
            def _(v, qt=qt, kc=kc, par=par, s_done=s_done, aw2=exp_done[par]):
                v.wait_ge(sT, s_done)
                if aw2:
                    v.wait_ge(sA, aw2)
                last = v.reduce_max(mx[par][:], psS[par][:, 0:1024], axis=AX.X)
                _fence(env, v, sV, last)
                if kc == 0:
                    v.tensor_copy(mcur[:, qt:qt + 1], mx[par][:])
                    v.tensor_scalar_mul(bias[par][:], mx[par][:], -C_A).then_inc(sV, 1)
                else:
                    last = v.tensor_max(mnew[par][:], mcur[:, qt:qt + 1], mx[par][:])
                    _fence(env, v, sV, last)
                    v.tensor_scalar_mul(bias[par][:], mnew[par][:], -C_A)
                    v.tensor_tensor(dd[par][:], mcur[:, qt:qt + 1], mnew[par][:],
                                    op=ALU.subtract)
                    v.tensor_copy(mcur[:, qt:qt + 1], mnew[par][:]).then_inc(sV, 1)
                c.add(sV, 1)
            mch_done[par] = c.cur(sV)
            m_done = mch_done[par]

            @blk.scalar
            def _(sc, qt=qt, kc=kc, par=par, m_done=m_done, tw=trans_done[par],
                  lw=lo_done[par]):
                sc.wait_ge(sV, m_done)
                if tw:
                    sc.wait_ge(sT, tw)
                if lw:
                    sc.wait_ge(sV, lw)
                last = sc.activation(out=PSB[par][:], in_=psS[par][:, 0:1024],
                                     func=AF.Exp, bias=bias[par][:, 0:1], scale=C_A,
                                     accum_out=lch[par][:, 0:1])
                if kc > 0:
                    last = sc.activation(out=af[par][:], in_=dd[par][:, 0:1],
                                         func=AF.Exp, scale=C_A)
                last.then_inc(sA, 1)
                c.add(sA, 1)
            exp_done[par] = c.cur(sA)

        def emit_tail(qt):
            par = qt % 2
            e_done = exp_done[par]

            @blk.tensor
            def _(t, qt=qt, par=par, e_done=e_done, vw=c.cur(sV)):
                t.wait_ge(sA, e_done)
                t.wait_ge(sV, vw)
                for f in range(8):
                    last = t.transpose(psT[:, f * 128:(f + 1) * 128],
                                       PSB[par][:, f * 128:(f + 1) * 128], pt.identb[:])
                last.then_inc(sT, 1)
                c.add(sT, 1)
            trans_done[par] = c.cur(sT)
            tr_done = trans_done[par]

            @blk.vector
            def _(v, par=par, tr_done=tr_done, pw=pv_done[par]):
                v.wait_ge(sT, tr_done)
                if pw:
                    v.wait_ge(sT, pw)
                v.tensor_copy(PTS[par][:], psT[:, 0:1024]).then_inc(sV, 1)
                c.add(sV, 1)
            pt_ok = c.cur(sV)

            @blk.tensor
            def _(t, par=par, sl=sl, pt_ok=pt_ok):
                t.wait_ge(sV, pt_ok)
                for f in range(8):
                    for vh in range(2):
                        last = t.matmul(psO[:, vh * 512:(vh + 1) * 512],
                                        PTS[par][:, f * 128:(f + 1) * 128],
                                        VSB[sl][:, f, vh * 512:(vh + 1) * 512],
                                        start=(f == 0), stop=(f == 7))
                last.then_inc(sT, 1)
                c.add(sT, 1)
            pv_done[par] = c.cur(sT)
            pv_ok = pv_done[par]

            @blk.vector
            def _(v, qt=qt, kc=kc, par=par, pv_ok=pv_ok):
                v.wait_ge(sT, pv_ok)
                if kc == 0:
                    v.tensor_copy(lcur[:, qt:qt + 1], lch[par][:])
                    v.tensor_copy(OACC[:, qt, :], psO[:, 0:1024]).then_inc(sV, 1)
                else:
                    v.scalar_tensor_tensor(out=lcur[:, qt:qt + 1], in0=lcur[:, qt:qt + 1],
                                           scalar=af[par][:, 0:1], in1=lch[par][:],
                                           op0=ALU.mult, op1=ALU.add)
                    v.scalar_tensor_tensor(out=OACC[:, qt, :], in0=OACC[:, qt, :],
                                           scalar=af[par][:, 0:1], in1=psO[:, 0:1024],
                                           op0=ALU.mult, op1=ALU.add).then_inc(sV, 1)
                c.add(sV, 1)
            lo_done[par] = c.cur(sV)

        for qt in range(8):
            emit_S(qt)
            if qt >= 1:
                emit_tail(qt - 1)
        emit_tail(7)
        kv_users[sl] = c.cur(sT)

    for qt in range(8):
        @blk.vector
        def _(v, qt=qt, tw=c.cur(sT)):
            v.wait_ge(sT, tw)
            last = v.reciprocal(rl[:], lcur[:, qt:qt + 1])
            _fence(env, v, sV, last)
            v.tensor_scalar(obf[:], OACC[:, qt, :], rl[:, 0:1], None,
                            op0=ALU.mult).then_inc(sV, 1)
            c.add(sV, 1)
        n_done = c.cur(sV)

        @blk.tensor
        def _(t, qt=qt, n_done=n_done):
            t.wait_ge(sV, n_done)
            for cc in range(8):
                last = t.transpose(psT[:, cc * 128:(cc + 1) * 128],
                                   obf[:, cc * 128:(cc + 1) * 128], pt.identb[:])
            last.then_inc(sT, 1)
            c.add(sT, 1)
        ot_done = c.cur(sT)

        @blk.vector
        def _(v, qt=qt, ot_done=ot_done):
            v.wait_ge(sT, ot_done)
            for cc in range(8):
                last = v.tensor_copy(OFSB[:, cc, qt * 128:(qt + 1) * 128],
                                     psT[:, cc * 128:(cc + 1) * 128])
            last.then_inc(sV, 1)
            c.add(sV, 1)

    of_done = c.cur(sV)
    gw = c.cur(sG)

    for pc in range(8):
        for ntile in range(2):
            aw = c.cur(sA)

            @blk.sync
            def _(s, pc=pc, ntile=ntile, dw=c.cur(sD), vw2=c.cur(sV)):
                s.wait_ge(sD, dw)
                s.wait_ge(sV, vw2)
                s.dma_start(out=xns[:],
                            in_=Z["xnF"][pc * 128:(pc + 1) * 128,
                                         H + q0 + ntile * 512:H + q0 + (ntile + 1) * 512]
                            ).then_inc(sD, 16)
                c.add(sD, 16)
            xn_ok = c.cur(sD)

            @blk.tensor
            def _(t, pc=pc, ntile=ntile, of_done=of_done, aw=aw):
                t.wait_ge(sV, of_done)
                t.wait_ge(sA, aw)
                for f in range(8):
                    last = t.matmul(ps1[:, 0:512],
                                    WP[:, f, pc * 128:(pc + 1) * 128],
                                    OFSB[:, f, ntile * 512:(ntile + 1) * 512],
                                    start=(f == 0), stop=(f == 7))
                last.then_inc(sT, 1)
                c.add(sT, 1)
            pj_done = c.cur(sT)

            @blk.scalar
            def _(sc, pc=pc, pj_done=pj_done, dw2=c.cur(sD)):
                sc.wait_ge(sT, pj_done)
                sc.wait_ge(sD, dw2)
                sc.activation(out=stg[:], in_=ps1[:, 0:512], func=AF.Identity,
                              bias=pt.bproj[:, pc:pc + 1]).then_inc(sA, 1)
                c.add(sA, 1)
            ev_done = c.cur(sA)

            @blk.vector
            def _(v, xn_ok=xn_ok, ev_done=ev_done):
                v.wait_ge(sD, xn_ok)
                v.wait_ge(sA, ev_done)
                v.tensor_add(stg[:], stg[:], xns[:]).then_inc(sV, 1)
                c.add(sV, 1)
            add_done = c.cur(sV)

            @blk.sync
            def _(s, pc=pc, ntile=ntile, add_done=add_done):
                s.wait_ge(sV, add_done)
                s.dma_start(out=Z["sF"][pc * 128:(pc + 1) * 128,
                                        q0 + ntile * 512:q0 + (ntile + 1) * 512],
                            in_=stg[:]).then_inc(sD, 16)
                c.add(sD, 16)


def _slA_groups():
    gs = [(0, 512), (512, 512), (1024, 128)]
    for g in range(8):
        gs.append((1152 + 512 * g, 512))
    return gs


def _emit_zeroinit(env):
    nc, Z, c = env.nc, env.Z, env.c
    sD, sV = env.sD, env.sV
    with contextlib.ExitStack() as ph:
        zt = ph.enter_context(nc.sbuf_tensor("zi_z", [P, 512], F32))
        with _PhaseCtx(env) as blk:
            snap = env.snap
            _phase_barrier(env, blk, snap)

            @blk.vector
            def _(v):
                _barrier_waits(env, v, snap)
                v.memset(zt[:], 0.0).then_inc(sV, 1)
                c.add(sV, 1)
            zok = c.cur(sV)

            @blk.sync
            def _(s):
                s.wait_ge(sV, zok)
                for buf in ("yA", "yB"):
                    for cc in range(8):
                        s.dma_start(out=env.Z[buf][cc * 128:(cc + 1) * 128, 0:512],
                                    in_=zt[:]).then_inc(sD, 16)
                        c.add(sD, 16)
    env.snap = c.snap()


def _emit_gate(env, bufname):
    nc, Z, pt, c = env.nc, env.Z, env.pt, env.c
    sD, sV = env.sD, env.sV
    with contextlib.ExitStack() as ph:
        colt = ph.enter_context(nc.sbuf_tensor("gt_col_" + bufname, [P, 8], F32))
        with _PhaseCtx(env) as blk:
            snap = env.snap
            _phase_barrier(env, blk, snap)

            @blk.sync
            def _(s):
                _barrier_waits(env, s, snap)
                s.dma_start(out=colt[:], in_=Z[bufname][:, H - 1:H].rearrange("(c p) one -> p c one", p=P),
                            allow_slow_non_contiguous=True).then_inc(sD, 16)
                c.add(sD, 16)
            lok = c.cur(sD)

            @blk.vector
            def _(v):
                v.wait_ge(sD, lok)
                v.tensor_scalar(colt[:], colt[:], pt.gate[:, 0:1], None,
                                op0=ALU.mult).then_inc(sV, 1)
                c.add(sV, 1)
            gok = c.cur(sV)

            @blk.sync
            def _(s):
                s.wait_ge(sV, gok)
                s.dma_start(out=Z[bufname][:, H - 1:H].rearrange("(c p) one -> p c one", p=P),
                            in_=colt[:], allow_slow_non_contiguous=True).then_inc(sD, 16)
                c.add(sD, 16)
    env.snap = c.snap()


def _emit_slA(env, l):
    nc, I, Z, pt, c = env.nc, env.I, env.Z, env.pt, env.c
    sD, sG, sT, sV, sA = env.sD, env.sG, env.sT, env.sV, env.sA
    ysrc = Z["xnF"] if l == 0 else Z["yB"]
    qstart = 512 if l == 0 else 1024

    with contextlib.ExitStack() as ph:
        sb = lambda n, s, d: ph.enter_context(nc.sbuf_tensor(f"l{l}{n}", s, d))
        WQ2 = sb("sa_wq", [P, 8, 1536], BF16)
        lmk = sb("sa_lmk", [P, 5, 640], F32)
        WO2 = sb("sa_wo", [P, 4, 1024], BF16)
        KF2 = sb("sa_k", [P, 4, OH], BF16)
        VT2 = sb("sa_v", [P, 41, 512], BF16)
        QF2 = sb("sa_q", [P, 4, 512], BF16)
        ysh = sb("sa_ysh", [P, 8, 512], BF16)
        yf32 = sb("sa_yf32", [P, 8, 513], F32)
        qraw = sb("sa_qraw", [P, 4, 512], F32)
        qnrm = sb("sa_qnrm", [P, 512], BF16)
        sqb = sb("sa_sqb", [P, 512], BF16)
        rr = sb("sa_rr", [P, 512], F32)
        xc1 = sb("sa_xc1", [P, 512], F32)
        xc2 = sb("sa_xc2", [P, 512], F32)
        xc3 = sb("sa_xc3", [P, 512], F32)
        xc4 = sb("sa_xc4", [P, 512], F32)
        Ssb0 = sb("sa_S0", [P, 2, 640], F32)
        Ssb = [Ssb0, Ssb0]
        gx1 = Ssb0[:, 0, 0:512]
        gx2 = Ssb0[:, 1, 0:512]
        Pb0 = sb("sa_P0", [P, 2, 640], BF16)
        Pb = [Pb0, Pb0]
        PTS = [sb(f"sa_PT{i}", [P, 2, 640], BF16) for i in range(2)]
        mxp = [sb(f"sa_mx{i}", [P, 2], F32) for i in range(2)]
        nm2 = [sb(f"sa_nm{i}", [P, 2], F32) for i in range(2)]
        ll2 = [sb(f"sa_ll{i}", [P, 2], F32) for i in range(2)]
        rli2 = [sb(f"sa_rl{i}", [P, 2], F32) for i in range(2)]
        o_loc = sb("sa_ol", [P, 4, 512], BF16)
        oF = sb("sa_of", [P, 4, 512], BF16)
        ystg = sb("sa_ystg", [P, 512], F32)
        PS_BIG = ph.enter_context(nc.psum_tensor(f"l{l}sa_big", [P, 2048], F32))
        PS_T2 = ph.enter_context(nc.psum_tensor(f"l{l}sa_t2", [P, 2048], BF16))
        PS_O = [ph.enter_context(nc.psum_tensor(f"l{l}sa_o{i}", [P, 512], F32))
                for i in range(2)]

        with _PhaseCtx(env) as blk:
            snap = env.snap
            _phase_barrier(env, blk, snap)

            @blk.sync
            def _(s):
                _barrier_waits(env, s, snap)
                s.dma_start(out=WQ2[:], in_=Z["wqkv2_bf"][l].rearrange("(c p) n -> p c n", p=P)
                            ).then_inc(sD, 16)
                c.add(sD, 16)
                s.dma_start(out=WO2[:], in_=Z["wo2_bf"][l].rearrange("(c p) n -> p c n", p=P)
                            ).then_inc(sD, 16)
                c.add(sD, 16)
                s.dma_start(out=lmk[:], in_=I["lmask"].rearrange("m p n -> p m n")
                            ).then_inc(sD, 16)
                c.add(sD, 16)

            buf = Env()
            buf.WQ2, buf.WO2, buf.KF2, buf.VT2, buf.QF2 = WQ2, WO2, KF2, VT2, QF2
            buf.ysh, buf.yf32, buf.qraw, buf.qnrm = ysh, yf32, qraw, qnrm
            buf.gx1, buf.gx2, buf.sqb, buf.rr = gx1, gx2, sqb, rr
            buf.xc = (xc1, xc2, xc3, xc4)
            buf.Ssb, buf.Pb, buf.PTS = Ssb, Pb, PTS
            buf.mxp, buf.nm2, buf.ll2, buf.rli2 = mxp, nm2, ll2, rli2
            buf.o_loc, buf.oF, buf.ystg, buf.lmk = o_loc, oF, ystg, lmk
            buf.PS_BIG, buf.PS_T2, buf.PS_O = PS_BIG, PS_T2, PS_O

            for (r0, nr) in _slA_groups():
                _slA_group(env, blk, l, ysrc, qstart, r0, nr, buf)

    env.snap = c.snap()


def _slA_group(env, blk, l, ysrc, qstart, r0, nr, buf):
    nc, I, Z, pt, c = env.nc, env.I, env.Z, env.pt, env.c
    sD, sG, sT, sV, sA = env.sD, env.sG, env.sT, env.sV, env.sA
    nt = nr // 128
    do_attn = (r0 + nr) > qstart
    B = buf

    tprev, vprev, aprev = c.cur(sT), c.cur(sV), c.cur(sA)

    @blk.sync
    def _(s):
        s.wait_ge(sT, tprev)
        s.wait_ge(sV, vprev)
        s.wait_ge(sA, aprev)
        if r0 == 0:
            s.dma_start(out=B.yf32[:, :, 1:513],
                        in_=ysrc[:, 0:512].rearrange("(c p) n -> p c n", p=P)
                        ).then_inc(sD, 16)
            c.add(sD, 16)
        else:
            s.dma_start(out=B.yf32[:, :, 0:nr + 1],
                        in_=ysrc[:, r0 - 1:r0 + nr].rearrange("(c p) n -> p c n", p=P)
                        ).then_inc(sD, 16)
            c.add(sD, 16)
        for name, bufx in (("xqc", B.xc[0]), ("xqs", B.xc[1]),
                           ("xkc", B.xc[2]), ("xks", B.xc[3])):
            s.dma_start(out=bufx[:, 0:nr], in_=I[name][:, r0:r0 + nr]).then_inc(sD, 16)
            c.add(sD, 16)
    yld = c.cur(sD)

    @blk.vector
    def _(v, yld=yld):
        v.wait_ge(sD, yld)
        if r0 == 0:
            last = v.memset(B.yf32[:, 4:8, 0:1], 0.0)
            _fence(env, v, sV, last)
        v.tensor_copy(B.ysh[:, 0:4, 0:nr], B.yf32[:, 0:4, 1:nr + 1])
        v.tensor_copy(B.ysh[:, 4:8, 0:nr], B.yf32[:, 4:8, 0:nr]).then_inc(sV, 1)
        c.add(sV, 1)
    ysh_ok = c.cur(sV)
    ld_ok = (c.cur(sD), c.cur(sG))

    for cc in range(8):
        aw = c.cur(sA)
        par = cc % 2
        psq = B.PS_BIG[:, par * 512:(par + 1) * 512]

        @blk.tensor
        def _(t, cc=cc, aw=aw, psq=psq):
            t.wait_ge(sV, ysh_ok)
            t.wait_ge(sA, aw)
            for f in range(8):
                last = t.matmul(psq[:, 0:nr], B.WQ2[:, f, cc * 128:(cc + 1) * 128],
                                B.ysh[:, f, 0:nr], start=(f == 0), stop=(f == 7))
            last.then_inc(sT, 1)
            c.add(sT, 1)
        mm_ok = c.cur(sT)

        @blk.scalar
        def _(sc, cc=cc, mm_ok=mm_ok, vw=c.cur(sV), psq=psq):
            sc.wait_ge(sT, mm_ok)
            sc.wait_ge(sV, vw)
            sc.activation(out=B.qraw[:, cc % 4, 0:nr], in_=psq[:, 0:nr],
                          func=AF.Identity).then_inc(sA, 1)
            c.add(sA, 1)

        if cc % 4 == 3:
            isq = cc < 4
            for u in range(4):
                _slA_norm_chunk(env, blk, l, isq, u, r0, nr, B, ld_ok)

    for a in range(nt):
        vw = c.cur(sV)

        @blk.tensor
        def _(t, a=a, vw=vw):
            t.wait_ge(sV, max(vw, ysh_ok))
            for f in range(8):
                last = t.matmul(B.PS_O[0][:, 0:512], B.ysh[:, f, a * 128:(a + 1) * 128],
                                B.WQ2[:, f, 1024:1536], start=(f == 0), stop=(f == 7))
            last.then_inc(sT, 1)
            c.add(sT, 1)
        vm_ok = c.cur(sT)

        @blk.vector
        def _(v, a=a, vm_ok=vm_ok):
            v.wait_ge(sT, vm_ok)
            v.tensor_copy(B.VT2[:, (r0 // 128) + a, :], B.PS_O[0][:, 0:512]).then_inc(sV, 1)
            c.add(sV, 1)

    if not do_attn:
        return

    ttr_done = [0, 0]
    exp_done = [0, 0]
    trans_done = [0, 0]
    evac_done = [0, 0]
    pv_done = [0, 0]
    o_done = [0, 0]

    pairs = []
    for a in range(nt):
        r_tile = r0 + a * 128
        if r_tile + 128 <= qstart:
            continue
        if r_tile >= H:
            j = (r_tile - H) // 128
            midx = j if j < 4 else 4
        else:
            midx = 4
        for hp2 in range(4):
            pairs.append((a, hp2, r_tile, midx))

    qkv_done = (c.cur(sV), c.cur(sA), c.cur(sT))

    def emit_S(i):
        a, hp2, r_tile, midx = pairs[i]
        par = i % 2
        k0 = r_tile - 512

        @blk.tensor
        def _(t, a=a, hp2=hp2, k0=k0, tw=max(ttr_done)):
            if i == 0:
                t.wait_ge(sV, qkv_done[0])
                t.wait_ge(sA, qkv_done[1])
            if tw:
                t.wait_ge(sV, tw)
            qA = B.QF2[0:64, hp2, a * 128:(a + 1) * 128]
            qB = B.QF2[64:128, hp2, a * 128:(a + 1) * 128]
            kA = B.KF2[0:64, hp2, :]
            kB = B.KF2[64:128, hp2, :]
            t.matmul(B.PS_BIG[:, 0:512], qA, kA[:, k0:k0 + 512], start=True, stop=True)
            t.matmul(B.PS_BIG[:, 512:640], qA, kA[:, k0 + 512:k0 + 640], start=True, stop=True)
            t.matmul(B.PS_BIG[:, 1024:1536], qB, kB[:, k0:k0 + 512], start=True, stop=True)
            t.matmul(B.PS_BIG[:, 1536:1664], qB, kB[:, k0 + 512:k0 + 640],
                     start=True, stop=True).then_inc(sT, 1)
            c.add(sT, 1)
        s_ok = c.cur(sT)

        @blk.vector
        def _(v, midx=midx, par=par, s_ok=s_ok, aw=c.cur(sA)):
            v.wait_ge(sT, s_ok)
            if aw:
                v.wait_ge(sA, aw)
            pview = B.PS_BIG.rearrange("p (h n) -> p h n", n=1024)
            last = v.tensor_tensor(B.Ssb[par][:, :, :], pview[:, :, 0:640],
                                   B.lmk[:, midx, :].unsqueeze(1).broadcast_to([P, 2, 640]),
                                   op=ALU.add)
            _fence(env, v, sV, last)
            v.reduce_max(B.nm2[par][:, 0:2], B.Ssb[par][:, :, :], axis=AX.X,
                         negate=True).then_inc(sV, 1)
            c.add(sV, 1)
        ttr_done[par] = c.cur(sV)

    def emit_exp(i):
        par = i % 2
        m_ok = ttr_done[par]

        @blk.scalar
        def _(sc, par=par, m_ok=m_ok, tw=max(trans_done), ow=o_done[par]):
            sc.wait_ge(sV, m_ok)
            if tw:
                sc.wait_ge(sT, tw)
            if ow:
                sc.wait_ge(sV, ow)
            sc.activation(out=B.Pb[par][:, 0, :], in_=B.Ssb[par][:, 0, :], func=AF.Exp,
                          bias=B.nm2[par][:, 0:1], accum_out=B.ll2[par][:, 0:1])
            sc.activation(out=B.Pb[par][:, 1, :], in_=B.Ssb[par][:, 1, :], func=AF.Exp,
                          bias=B.nm2[par][:, 1:2], accum_out=B.ll2[par][:, 1:2]
                          ).then_inc(sA, 1)
            c.add(sA, 1)
        exp_done[par] = c.cur(sA)

    def emit_tail(i):
        a, hp2, r_tile, midx = pairs[i]
        par = i % 2
        e_ok = exp_done[par]

        @blk.tensor
        def _(t, par=par, e_ok=e_ok, vw=c.cur(sV)):
            t.wait_ge(sA, e_ok)
            t.wait_ge(sV, vw)
            for i5 in range(5):
                t.transpose(B.PS_T2[:, i5 * 128:i5 * 128 + 128],
                            B.Pb[par][:, 0, i5 * 128:(i5 + 1) * 128], pt.identb[:])
            for i5 in range(5):
                last = t.transpose(B.PS_T2[:, 1024 + i5 * 128:1024 + i5 * 128 + 128],
                                   B.Pb[par][:, 1, i5 * 128:(i5 + 1) * 128], pt.identb[:])
            last.then_inc(sT, 1)
            c.add(sT, 1)
        trans_done[par] = c.cur(sT)
        tr_ok = trans_done[par]

        @blk.vector
        def _(v, par=par, tr_ok=tr_ok, pw=pv_done[par]):
            v.wait_ge(sT, tr_ok)
            if pw:
                v.wait_ge(sT, pw)
            tv = B.PS_T2.rearrange("p (h n) -> p h n", n=1024)
            v.tensor_copy(B.PTS[par][:, 0, :], tv[:, 0, 0:640])
            last = v.tensor_copy(B.PTS[par][:, 1, :], tv[:, 1, 0:640])
            _fence(env, v, sV, last)
            last = v.reciprocal(rli_ap(B, par), B.ll2[par][:])
            last.then_inc(sV, 1)
            c.add(sV, 1)
        pt_ok = c.cur(sV)

        @blk.tensor
        def _(t, a=a, hp2=hp2, r_tile=r_tile, par=par, pt_ok=pt_ok):
            t.wait_ge(sV, pt_ok)
            sub0 = (r_tile - 512) // 128
            for h2 in range(2):
                h = hp2 * 2 + h2
                for i5 in range(5):
                    last = t.matmul(B.PS_O[h2][:, 0:64],
                                    B.PTS[par][:, h2, i5 * 128:(i5 + 1) * 128],
                                    B.VT2[:, sub0 + i5, h * 64:(h + 1) * 64],
                                    start=(i5 == 0), stop=(i5 == 4))
            last.then_inc(sT, 1)
            c.add(sT, 1)
        pv_done[par] = c.cur(sT)
        pv_ok = pv_done[par]

        @blk.vector
        def _(v, a=a, hp2=hp2, par=par, pv_ok=pv_ok):
            v.wait_ge(sT, pv_ok)
            h0 = hp2 * 2
            v.tensor_scalar(B.o_loc[:, a, h0 * 64:(h0 + 1) * 64], B.PS_O[0][:, 0:64],
                            B.rli2[par][:, 0:1], None, op0=ALU.mult)
            v.tensor_scalar(B.o_loc[:, a, (h0 + 1) * 64:(h0 + 2) * 64], B.PS_O[1][:, 0:64],
                            B.rli2[par][:, 1:2], None, op0=ALU.mult).then_inc(sV, 1)
            c.add(sV, 1)
        o_done[par] = c.cur(sV)

    for i in range(len(pairs)):
        emit_S(i)
        if i >= 1:
            emit_tail(i - 1)
        emit_exp(i)
    if pairs:
        emit_tail(len(pairs) - 1)

    a0 = max(0, (qstart - r0) // 128)
    for a in range(a0, nt):
        ow = c.cur(sV)

        @blk.tensor
        def _(t, a=a, ow=ow):
            t.wait_ge(sV, ow)
            for oc4 in range(4):
                last = t.transpose(B.PS_T2[:, oc4 * 128:(oc4 + 1) * 128],
                                   B.o_loc[:, a, oc4 * 128:(oc4 + 1) * 128], pt.identb[:])
            last.then_inc(sT, 1)
            c.add(sT, 1)
        ot_ok = c.cur(sT)

        @blk.vector
        def _(v, a=a, ot_ok=ot_ok):
            v.wait_ge(sT, ot_ok)
            for oc4 in range(4):
                last = v.tensor_copy(B.oF[:, oc4, a * 128:(a + 1) * 128],
                                     B.PS_T2[:, oc4 * 128:(oc4 + 1) * 128])
            last.then_inc(sV, 1)
            c.add(sV, 1)
    of_ok = c.cur(sV)

    wbase = qstart - r0 if qstart > r0 else 0
    wn = nr - wbase
    if wn <= 0:
        return
    for oc in range(8):
        par = oc % 2
        psw = B.PS_BIG[:, par * 512:(par + 1) * 512]

        @blk.tensor
        def _(t, oc=oc, psw=psw, of_ok=of_ok, vw4=c.cur(sV)):
            t.wait_ge(sV, of_ok)
            t.wait_ge(sV, vw4)
            for kk in range(4):
                last = t.matmul(psw[:, 0:wn], B.WO2[:, kk, oc * 128:(oc + 1) * 128],
                                B.oF[:, kk, wbase:nr], start=(kk == 0), stop=(kk == 3))
            last.then_inc(sT, 1)
            c.add(sT, 1)
        wm_ok = c.cur(sT)

        @blk.vector
        def _(v, oc=oc, psw=psw, wm_ok=wm_ok, dw2=c.cur(sD)):
            v.wait_ge(sT, wm_ok)
            v.wait_ge(sD, dw2)
            v.tensor_add(B.ystg[:, 0:wn], psw[:, 0:wn],
                         B.yf32[:, oc, 1 + wbase:1 + nr]).then_inc(sV, 1)
            c.add(sV, 1)
        ad_ok = c.cur(sV)

        @blk.sync
        def _(s, oc=oc, ad_ok=ad_ok):
            s.wait_ge(sV, ad_ok)
            s.dma_start(out=Z["yA"][oc * 128:(oc + 1) * 128, r0 + wbase:r0 + nr],
                        in_=B.ystg[:, 0:wn]).then_inc(sD, 16)
            c.add(sD, 16)


def rli_ap(B, par):
    return B.rli2[par][:]


def _slA_norm_chunk(env, blk, l, isq, u, r0, nr, B, ld_ok):
    nc, I, Z, pt, c = env.nc, env.I, env.Z, env.pt, env.c
    sD, sG, sT, sV, sA = env.sD, env.sG, env.sT, env.sV, env.sA
    tct = B.xc[0] if isq else B.xc[2]
    tst = B.xc[1] if isq else B.xc[3]
    psSS = B.PS_BIG[:, 1024:1536]
    psBC = B.PS_BIG[:, 1536:2048]
    psROT = B.PS_O[1]

    aw0 = c.cur(sA)

    @blk.scalar
    def _(sc, u=u, aw0=aw0, tw0=c.cur(sT)):
        sc.wait_ge(sA, aw0)
        sc.wait_ge(sT, tw0)
        sc.activation(out=B.sqb[:, 0:nr], in_=B.qraw[:, u, 0:nr],
                      func=AF.Square).then_inc(sA, 1)
        c.add(sA, 1)
    sq_ok = c.cur(sA)

    @blk.tensor
    def _(t, u=u, sq_ok=sq_ok, aw1=c.cur(sA)):
        t.wait_ge(sA, max(sq_ok, aw1))
        t.matmul(psSS[0:2, 0:nr], pt.segb[:, 0:2], B.sqb[:, 0:nr],
                 start=True, stop=True).then_inc(sT, 1)
        c.add(sT, 1)
    ssm_ok = c.cur(sT)

    @blk.scalar
    def _(sc, ssm_ok=ssm_ok, tw1=c.cur(sT)):
        sc.wait_ge(sT, max(ssm_ok, tw1))
        last = sc.activation(out=B.rr[0:2, 0:nr], in_=psSS[0:2, 0:nr], func=AF.Ln,
                             bias=pt.eps24[0:2, 0:1])
        _fence(env, sc, sA, last)
        sc.activation(out=B.rr[0:2, 0:nr], in_=B.rr[0:2, 0:nr], func=AF.Exp,
                      scale=-0.5).then_inc(sA, 1)
        c.add(sA, 1)
    rr_ok = c.cur(sA)

    @blk.tensor
    def _(t, rr_ok=rr_ok, vw2=c.cur(sV)):
        t.wait_ge(sA, rr_ok)
        t.wait_ge(sV, vw2)
        t.matmul(psBC[:, 0:nr], pt.segT[0:2, :], B.rr[0:2, 0:nr],
                 start=True, stop=True).then_inc(sT, 1)
        c.add(sT, 1)
    bc_ok = c.cur(sT)

    @blk.vector
    def _(v, u=u, isq=isq, bc_ok=bc_ok, ld2=ld_ok[0], tw2=c.cur(sT)):
        v.wait_ge(sT, bc_ok)
        v.wait_ge(sD, ld2)
        sc_ap = pt.qs8[:, l:l + 1] if isq else pt.ksc[:, l:l + 1]
        v.scalar_tensor_tensor(out=B.qnrm[:, 0:nr], in0=B.qraw[:, u, 0:nr],
                               scalar=sc_ap, in1=psBC[:, 0:nr],
                               op0=ALU.mult, op1=ALU.mult).then_inc(sV, 1)
        c.add(sV, 1)
    qn_ok = c.cur(sV)

    @blk.tensor
    def _(t, qn_ok=qn_ok):
        t.wait_ge(sV, qn_ok)
        t.matmul(psROT[:, 0:nr], pt.perm32[:], B.qnrm[:, 0:nr],
                 start=True, stop=True).then_inc(sT, 1)
        c.add(sT, 1)
    rot_ok = c.cur(sT)

    @blk.vector
    def _(v, u=u, isq=isq, r0=r0, rot_ok=rot_ok, tct=tct, tst=tst):
        v.wait_ge(sT, rot_ok)
        v.tensor_tensor(B.gx1[:, 0:nr], B.qnrm[:, 0:nr], tct[:, 0:nr], op=ALU.mult)
        last = v.tensor_tensor(B.gx2[:, 0:nr], psROT[:, 0:nr], tst[:, 0:nr], op=ALU.mult)
        _fence(env, v, sV, last)
        dst = B.QF2[:, u, 0:nr] if isq else B.KF2[:, u, r0:r0 + nr]
        v.tensor_add(dst, B.gx1[:, 0:nr], B.gx2[:, 0:nr]).then_inc(sV, 1)
        c.add(sV, 1)


def _emit_slF(env, l):
    nc, I, Z, pt, c = env.nc, env.I, env.Z, env.pt, env.c
    sD, sG, sT, sV, sA = env.sD, env.sG, env.sT, env.sV, env.sA

    if l == 0:
        groups = [(512, 512), (1024, 128)] + [(1152 + 512 * g, 512) for g in range(8)]
    else:
        groups = [(1152 + 512 * g, 512) for g in range(8)]

    with contextlib.ExitStack() as ph:
        sb = lambda n, s, d: ph.enter_context(nc.sbuf_tensor(f"l{l}{n}", s, d))
        W1 = sb("sf_w1", [P, 8, W1C], BF16)
        W2 = sb("sf_w2", [P, 22, D], BF16)
        ysh = sb("sf_ysh", [P, 8, 512], F32)
        sq8 = sb("sf_sq8", [P, 8, 512], BF16)
        yn2 = sb("sf_yn2", [P, 8, 512], BF16)
        tshb = sb("sf_tsh", [P, 22, 512], BF16)
        gelb = [sb(f"sf_gel{i}", [P, 512], BF16) for i in range(2)]
        yinc = [sb(f"sf_yin{i}", [P, 512], F32) for i in range(2)]
        ystg = [sb(f"sf_ystg{i}", [P, 512], F32) for i in range(2)]
        murow = sb("sf_mu", [1, 512], F32)
        v1row = sb("sf_v1", [1, 512], F32)
        varrow = sb("sf_var", [1, 512], F32)
        psS1 = ph.enter_context(nc.psum_tensor(f"l{l}sf_psS1", [P, 512], F32))
        psS2 = ph.enter_context(nc.psum_tensor(f"l{l}sf_psS2", [P, 512], F32))
        psBC = ph.enter_context(nc.psum_tensor(f"l{l}sf_psBC", [P, 1024], F32))
        psA1 = [ph.enter_context(nc.psum_tensor(f"l{l}sf_psA{i}", [P, 512], F32))
                for i in range(2)]
        psG = [ph.enter_context(nc.psum_tensor(f"l{l}sf_psG{i}", [P, 512], F32))
               for i in range(2)]

        with _PhaseCtx(env) as blk:
            snap = env.snap
            _phase_barrier(env, blk, snap)

            @blk.sync
            def _(s):
                _barrier_waits(env, s, snap)
                s.dma_start(out=W1[:], in_=Z["w12_bf"][l].rearrange("(c p) n -> p c n", p=P)
                            ).then_inc(sD, 16)
                c.add(sD, 16)
                s.dma_start(out=W2[:, 0:21, :],
                            in_=Z["w22_bf"][l][0:2688].rearrange("(c p) n -> p c n", p=P)
                            ).then_inc(sD, 16)
                c.add(sD, 16)
                s.dma_start(out=W2[0:42, 21, :], in_=Z["w22_bf"][l][2688:2730, :]
                            ).then_inc(sD, 16)
                c.add(sD, 16)

            prev = None
            for (r0, nr) in groups:
                prev = _slF_group(env, blk, l, r0, nr, W1, W2, ysh, sq8, yn2, tshb,
                                  gelb, yinc, ystg, murow, v1row, varrow,
                                  psS1, psS2, psBC, psA1, psG, prev)

    env.snap = c.snap()


def _slF_group(env, blk, l, r0, nr, W1, W2, ysh, sq8, yn2, tshb, gelb, yinc,
               ystg, murow, v1row, varrow, psS1, psS2, psBC, psA1, psG, prev):
    nc, I, Z, pt, c = env.nc, env.I, env.Z, env.pt, env.c
    sD, sG, sT, sV, sA = env.sD, env.sG, env.sT, env.sV, env.sA

    if prev is None:
        tprev, vprev, aprev = c.cur(sT), c.cur(sV), c.cur(sA)
    else:
        tprev, vprev, aprev = prev["ysh_T"], prev["ysh_V"], prev["ysh_A"]

    @blk.sync
    def _(s):
        s.wait_ge(sT, tprev)
        s.wait_ge(sV, vprev)
        s.wait_ge(sA, aprev)
        s.dma_start(out=ysh[:, 0:4, 0:nr],
                    in_=Z["yA"][0:512, r0:r0 + nr].rearrange("(c p) n -> p c n", p=P)
                    ).then_inc(sD, 16)
        c.add(sD, 16)
        s.dma_start(out=ysh[:, 4:8, 0:nr],
                    in_=Z["yA"][512:1024, r0 - 1:r0 + nr - 1].rearrange("(c p) n -> p c n", p=P)
                    ).then_inc(sD, 16)
        c.add(sD, 16)
    ld_ok = c.cur(sD)

    @blk.scalar
    def _(sc, ld_ok=ld_ok, tw=(prev or {}).get("sq8_T", 0)):
        sc.wait_ge(sD, ld_ok)
        if tw:
            sc.wait_ge(sT, tw)
        for f in range(8):
            last = sc.activation(out=sq8[:, f, 0:nr], in_=ysh[:, f, 0:nr],
                                 func=AF.Square)
        last.then_inc(sA, 1)
        c.add(sA, 1)
    sq_ok = c.cur(sA)

    @blk.tensor
    def _(t, ld_ok=ld_ok, sq_ok=sq_ok, vw=(prev or {}).get("stats_V", 0)):
        t.wait_ge(sD, ld_ok)
        t.wait_ge(sA, sq_ok)
        if vw:
            t.wait_ge(sV, vw)
        for f in range(8):
            t.matmul(psS1[0:1, 0:nr], pt.onesc[:, 0:1], ysh[:, f, 0:nr],
                     start=(f == 0), stop=(f == 7))
        for f in range(8):
            last = t.matmul(psS2[0:1, 0:nr], pt.onesb[:, 0:1], sq8[:, f, 0:nr],
                            start=(f == 0), stop=(f == 7))
        last.then_inc(sT, 1)
        c.add(sT, 1)
    sums_ok = c.cur(sT)
    sq8_T = sums_ok

    @blk.vector
    def _(v, sums_ok=sums_ok, tw2=(prev or {}).get("rows_T", 0)):
        v.wait_ge(sT, sums_ok)
        if tw2:
            v.wait_ge(sT, tw2)
        v.tensor_scalar_mul(murow[0:1, 0:nr], psS1[0:1, 0:nr], 1.0 / D)
        last = v.tensor_scalar_mul(v1row[0:1, 0:nr], psS2[0:1, 0:nr], 1.0 / D)
        _fence(env, v, sV, last)
        last = v.tensor_tensor(varrow[0:1, 0:nr], murow[0:1, 0:nr], murow[0:1, 0:nr],
                               op=ALU.mult)
        _fence(env, v, sV, last)
        v.tensor_tensor(varrow[0:1, 0:nr], v1row[0:1, 0:nr], varrow[0:1, 0:nr],
                        op=ALU.subtract).then_inc(sV, 1)
        c.add(sV, 1)
    var_ok = c.cur(sV)
    stats_V = var_ok

    @blk.scalar
    def _(sc, var_ok=var_ok):
        sc.wait_ge(sV, var_ok)
        last = sc.activation(out=varrow[0:1, 0:nr], in_=varrow[0:1, 0:nr],
                             func=AF.Ln, bias=pt.eps[0:1, 0:1])
        _fence(env, sc, sA, last)
        sc.activation(out=varrow[0:1, 0:nr], in_=varrow[0:1, 0:nr],
                      func=AF.Exp, scale=-0.5).then_inc(sA, 1)
        c.add(sA, 1)
    rstd_ok = c.cur(sA)

    @blk.tensor
    def _(t, rstd_ok=rstd_ok, vw2=(prev or {}).get("bc_V", 0)):
        t.wait_ge(sA, rstd_ok)
        if vw2:
            t.wait_ge(sV, vw2)
        t.matmul(psBC[:, 0:nr], pt.ones1r[0:1, :], murow[0:1, 0:nr],
                 start=True, stop=True)
        t.matmul(psBC[:, 512:512 + nr], pt.ones1r[0:1, :], varrow[0:1, 0:nr],
                 start=True, stop=True).then_inc(sT, 1)
        c.add(sT, 1)
    bc_ok = c.cur(sT)
    rows_T = bc_ok

    @blk.vector
    def _(v, bc_ok=bc_ok, tw3=(prev or {}).get("yn2_T", 0)):
        v.wait_ge(sT, bc_ok)
        if tw3:
            v.wait_ge(sT, tw3)
        for f in range(8):
            last = v.tensor_tensor(ysh[:, f, 0:nr], ysh[:, f, 0:nr], psBC[:, 0:nr],
                                   op=ALU.subtract)
        _fence(env, v, sV, last)
        for f in range(8):
            last = v.scalar_tensor_tensor(out=yn2[:, f, 0:nr], in0=ysh[:, f, 0:nr],
                                          scalar=pt.lng[:, l, f:f + 1],
                                          in1=psBC[:, 512:512 + nr],
                                          op0=ALU.mult, op1=ALU.mult)
        _fence(env, v, sV, last)
        for f in range(8):
            last = v.tensor_scalar(yn2[:, f, 0:nr], yn2[:, f, 0:nr],
                                   pt.lnb[:, l, f:f + 1], None, op0=ALU.add)
        last.then_inc(sV, 1)
        c.add(sV, 1)
    yn_ok = c.cur(sV)
    bc_V = yn_ok
    ysh_V = yn_ok

    gel_done = [0, 0]
    gelu_done = [0, 0]
    for j in range(22):
        par = j % 2
        mj = 128 if j < 21 else 42
        ac0 = j * 128
        gc0 = HID + j * 128

        @blk.tensor
        def _(t, j=j, par=par, mj=mj, ac0=ac0, gc0=gc0, yn_ok=yn_ok,
              aw=gelu_done[par], vw3=gel_done[par]):
            t.wait_ge(sV, yn_ok)
            if aw:
                t.wait_ge(sA, aw)
            if vw3:
                t.wait_ge(sV, vw3)
            for f in range(8):
                t.matmul(psA1[par][0:mj, 0:nr], W1[:, f, ac0:ac0 + mj],
                         yn2[:, f, 0:nr], start=(f == 0), stop=(f == 7))
            for f in range(8):
                last = t.matmul(psG[par][0:mj, 0:nr], W1[:, f, gc0:gc0 + mj],
                                yn2[:, f, 0:nr], start=(f == 0), stop=(f == 7))
            last.then_inc(sT, 1)
            c.add(sT, 1)
        w1_ok = c.cur(sT)

        @blk.scalar
        def _(sc, j=j, par=par, mj=mj, w1_ok=w1_ok, vw4=gel_done[par]):
            sc.wait_ge(sT, w1_ok)
            if vw4:
                sc.wait_ge(sV, vw4)
            sc.activation(out=gelb[par][0:mj, 0:nr], in_=psG[par][0:mj, 0:nr],
                          func=AF.Gelu,
                          bias=pt.b1s[0:mj, l, 22 + j:23 + j]).then_inc(sA, 1)
            c.add(sA, 1)
        gelu_done[par] = c.cur(sA)
        gel_ok = gelu_done[par]

        @blk.vector
        def _(v, j=j, par=par, mj=mj, gel_ok=gel_ok):
            v.wait_ge(sA, gel_ok)
            v.scalar_tensor_tensor(out=tshb[0:mj, j, 0:nr], in0=psA1[par][0:mj, 0:nr],
                                   scalar=pt.b1s[0:mj, l, j:j + 1],
                                   in1=gelb[par][0:mj, 0:nr],
                                   op0=ALU.add, op1=ALU.mult).then_inc(sV, 1)
            c.add(sV, 1)
        gel_done[par] = c.cur(sV)
    t_ok = c.cur(sV)
    yn2_T = c.cur(sT)

    w2v = [gel_done[0], gel_done[1]]
    ystg_store = [0, 0]
    for oc in range(8):
        par = oc % 2

        @blk.sync
        def _(s, oc=oc, par=par, vw5=w2v[par]):
            if vw5:
                s.wait_ge(sV, vw5)
            s.dma_start(out=yinc[par][:, 0:nr],
                        in_=Z["yA"][oc * 128:(oc + 1) * 128, r0:r0 + nr]
                        ).then_inc(sD, 16)
            c.add(sD, 16)
        yi_ok = c.cur(sD)

        @blk.tensor
        def _(t, oc=oc, par=par, t_ok=t_ok, vw6=w2v[par]):
            t.wait_ge(sV, t_ok)
            if vw6:
                t.wait_ge(sV, vw6)
            for j in range(22):
                mj = 128 if j < 21 else 42
                last = t.matmul(psA1[par][:, 0:nr], W2[0:mj, j, oc * 128:(oc + 1) * 128],
                                tshb[0:mj, j, 0:nr], start=(j == 0), stop=(j == 21))
            last.then_inc(sT, 1)
            c.add(sT, 1)
        w2_ok = c.cur(sT)

        @blk.vector
        def _(v, oc=oc, par=par, w2_ok=w2_ok, yi_ok=yi_ok, sw=ystg_store[par]):
            v.wait_ge(sT, w2_ok)
            v.wait_ge(sD, max(yi_ok, sw))
            last = v.scalar_tensor_tensor(out=ystg[par][:, 0:nr], in0=psA1[par][:, 0:nr],
                                          scalar=pt.b2s[:, l, oc:oc + 1],
                                          in1=yinc[par][:, 0:nr],
                                          op0=ALU.add, op1=ALU.add)
            last.then_inc(sV, 1)
            c.add(sV, 1)
        w2v[par] = c.cur(sV)
        st_ok = w2v[par]

        @blk.sync
        def _(s, oc=oc, par=par, st_ok=st_ok):
            s.wait_ge(sV, st_ok)
            s.dma_start(out=Z["yB"][oc * 128:(oc + 1) * 128, r0:r0 + nr],
                        in_=ystg[par][:, 0:nr]).then_inc(sD, 16)
            c.add(sD, 16)
        ystg_store[par] = c.cur(sD)

    return {
        "ysh_T": sums_ok, "ysh_V": ysh_V, "ysh_A": sq_ok,
        "sq8_T": sq8_T, "stats_V": stats_V, "rows_T": rows_T,
        "bc_V": bc_V, "yn2_T": yn2_T,
    }


def _emit_final(env):
    nc, I, Z, pt, c = env.nc, env.I, env.Z, env.pt, env.c
    sD, sG, sT, sV, sA = env.sD, env.sG, env.sT, env.sV, env.sA

    with contextlib.ExitStack() as ph:
        sb = lambda n, s, d: ph.enter_context(nc.sbuf_tensor(n, s, d))
        yb = sb("fn_y", [P, 8, 512], F32)
        sfb = sb("fn_s", [P, 8, 512], F32)
        gch = sb("fn_g", [P, 8, 512], F32)
        xtk = sb("fn_x", [P, 4, 1024], F32)
        osb = sb("fn_o", [P, 1024], F32)
        psF = ph.enter_context(nc.psum_tensor("fn_psF", [P, 1024], F32))

        with _PhaseCtx(env) as blk:
            snap = env.snap
            _phase_barrier(env, blk, snap)
            first = [True]
            for g in range(8):
                c0 = H + g * 512
                tprev, vprev, aprev = c.cur(sT), c.cur(sV), c.cur(sA)

                @blk.sync
                def _(s, g=g, c0=c0, tprev=tprev, vprev=vprev, aprev=aprev,
                      fst=first[0]):
                    if fst:
                        _barrier_waits(env, s, snap)
                    s.wait_ge(sT, tprev)
                    s.wait_ge(sV, vprev)
                    s.wait_ge(sA, aprev)
                    s.dma_start(out=yb[:], in_=Z["yB"][:, c0:c0 + 512]
                                .rearrange("(c p) n -> p c n", p=P)).then_inc(sD, 16)
                    c.add(sD, 16)
                    s.dma_start(out=sfb[:], in_=Z["sF"][:, g * 512:(g + 1) * 512]
                                .rearrange("(c p) n -> p c n", p=P)).then_inc(sD, 16)
                    c.add(sD, 16)
                    s.dma_start(out=xtk[:], in_=I["x_oh"][c0:c0 + 512, :]
                                .rearrange("(a p) f -> p a f", p=P)).then_inc(sD, 16)
                    c.add(sD, 16)
                first[0] = False
                ld_ok = c.cur(sD)

                @blk.scalar
                def _(sc, ld_ok=ld_ok):
                    sc.wait_ge(sD, ld_ok)
                    for f in range(8):
                        last = sc.activation(out=gch[:, f, :], in_=yb[:, f, :],
                                             func=AF.Silu)
                    last.then_inc(sA, 1)
                    c.add(sA, 1)
                si_ok = c.cur(sA)

                @blk.vector
                def _(v, si_ok=si_ok):
                    v.wait_ge(sA, si_ok)
                    for f in range(8):
                        last = v.tensor_tensor(gch[:, f, :], gch[:, f, :],
                                               sfb[:, f, :], op=ALU.mult)
                    last.then_inc(sV, 1)
                    c.add(sV, 1)
                t_ok = c.cur(sV)

                for a in range(4):
                    vw = c.cur(sV)

                    @blk.tensor
                    def _(t, a=a, t_ok=t_ok, vw=vw):
                        t.wait_ge(sV, vw)
                        for f in range(8):
                            last = t.transpose(psF[:, f * 128:(f + 1) * 128],
                                               gch[:, f, a * 128:(a + 1) * 128],
                                               pt.ident3[:])
                        last.then_inc(sT, 1)
                        c.add(sT, 1)
                    tr_ok = c.cur(sT)
                    dw = c.cur(sD)

                    @blk.vector
                    def _(v, a=a, tr_ok=tr_ok, dw=dw):
                        v.wait_ge(sT, tr_ok)
                        v.wait_ge(sD, dw)
                        v.tensor_tensor(osb[:], psF[:, 0:1024], xtk[:, a, :],
                                        op=ALU.add).then_inc(sV, 1)
                        c.add(sV, 1)
                    ev_ok = c.cur(sV)

                    @blk.sync
                    def _(s, g=g, a=a, ev_ok=ev_ok):
                        s.wait_ge(sV, ev_ok)
                        s.dma_start(out=env.out_o[g * 512 + a * 128:
                                                  g * 512 + (a + 1) * 128, :],
                                    in_=osb[:]).then_inc(sD, 16)
                        c.add(sD, 16)

    env.snap = c.snap()


def _emit_tail(env):
    nc, c = env.nc, env.c
    with _PhaseCtx(env) as blk:
        @blk.sync
        def _(s):
            s.wait_ge(env.sD, c.cur(env.sD))
            s.wait_ge(env.sG, c.cur(env.sG))


def pk(vec, ncol):
    return np.ascontiguousarray(np.asarray(vec, np.float32).reshape(ncol, P).T)


def pk8(vec):
    return pk(vec, 8)


def pk2x8(arr):
    a = np.asarray(arr, np.float32).reshape(2, 8, P)
    return np.ascontiguousarray(a.transpose(2, 0, 1))


def pk_head(arr):
    a = np.asarray(arr, np.float32)
    return np.ascontiguousarray(np.concatenate([a, a], axis=1).T)


def pk_b1(arr):
    out = np.zeros((P, 2, 44), np.float32)
    a = np.asarray(arr, np.float32)
    for l in range(2):
        for half in range(2):
            seg = a[l, half * 2730:(half + 1) * 2730]
            full = seg[0:2688].reshape(21, P).T
            out[:, l, half * 22:half * 22 + 21] = full
            out[0:42, l, half * 22 + 21] = seg[2688:2730]
    return out


def _in_maps(inputs):
    t = _tables()
    x = np.ascontiguousarray(inputs["L0_attn_x"], dtype=np.float32)
    maps = []
    for core in range(NC):
        b, hf = core // 2, core % 2
        t0 = hf * TOWN
        xb = x[b]
        if hf == 0:
            x_oh = np.concatenate([np.zeros((H, D), np.float32), xb[0:TOWN]], axis=0)
            x_oth = xb[TOWN:]
            rc_own = t["rc"][:, 0:TOWN]
            rs_own = t["rs"][:, 0:TOWN]
            rc_oth = t["rc"][:, TOWN:]
            rs_oth = t["rs"][:, TOWN:]
            xslc = slice(H - H, H - H + OH)
            lm = np.concatenate([t["mask_first"], t["mask_plain"][None]], axis=0)
            gate = np.zeros((P, 1), np.float32)
        else:
            x_oh = xb[TOWN - H:]
            x_oth = xb[0:TOWN]
            rc_own = t["rc"][:, TOWN:]
            rs_own = t["rs"][:, TOWN:]
            rc_oth = t["rc"][:, 0:TOWN]
            rs_oth = t["rs"][:, 0:TOWN]
            xslc = slice(TOWN, TOWN + OH)
            lm = np.concatenate([np.broadcast_to(t["mask_plain"], (4, P, 640)),
                                 t["mask_plain"][None]], axis=0)
            gate = np.ones((P, 1), np.float32)
        xoff = t0
        m = {
            "x_oh": np.ascontiguousarray(x_oh),
            "x_oth": np.ascontiguousarray(x_oth),
            "gnorm": pk8(inputs["L0_attn_g_norm"]),
            "wqkvA": np.ascontiguousarray(inputs["L0_attn_w_qkv"], np.float32),
            "bqk_t": pk(np.asarray(inputs["L0_attn_b_qkv"], np.float32)[0:2048], 16),
            "bqv_row": np.ascontiguousarray(
                np.asarray(inputs["L0_attn_b_qkv"], np.float32)[None, 2048:3072]),
            "wprojA": np.ascontiguousarray(inputs["L0_attn_w_proj"], np.float32),
            "bprojA": pk8(inputs["L0_attn_b_proj"]),
            "rc_own": np.ascontiguousarray(rc_own),
            "rs_own": np.ascontiguousarray(rs_own),
            "rc_oth": np.ascontiguousarray(rc_oth),
            "rs_oth": np.ascontiguousarray(rs_oth),
            "xqc": np.ascontiguousarray(t["xqc"][:, xoff:xoff + OH]),
            "xqs": np.ascontiguousarray(t["xqs"][:, xoff:xoff + OH]),
            "xkc": np.ascontiguousarray(t["xkc"][:, xoff:xoff + OH]),
            "xks": np.ascontiguousarray(t["xks"][:, xoff:xoff + OH]),
            "lmask": np.ascontiguousarray(lm),
            "segbf": t["segbf"],
            "perm32": t["perm32"],
            "segT": t["segT"],
            "gate": gate,
            "wqkv2": np.ascontiguousarray(inputs["L0_ffn_wqkv"], np.float32),
            "qsc": pk_head(inputs["L0_ffn_q_scale"]),
            "ksc": pk_head(inputs["L0_ffn_k_scale"]),
            "wo2": np.ascontiguousarray(inputs["L0_ffn_wo"], np.float32),
            "lng": pk2x8(inputs["L0_ffn_ln_g"]),
            "lnb": pk2x8(inputs["L0_ffn_ln_b"]),
            "w12": np.ascontiguousarray(inputs["L0_ffn_w1"], np.float32),
            "b12": pk_b1(inputs["L0_ffn_b1"]),
            "w22": np.ascontiguousarray(inputs["L0_ffn_w2"], np.float32),
            "b22": pk2x8(inputs["L0_ffn_b2"]),
        }
        maps.append(m)
    return maps


def kernel(**inputs):
    maps = _in_maps(inputs)
    nc = build_program()
    res = run_bass_kernel_spmd(nc, maps, core_ids=list(range(NC)))
    out = np.empty((NB, S, D), np.float32)
    for core in range(NC):
        b, hf = core // 2, core % 2
        out[b, hf * TOWN:(hf + 1) * TOWN] = res.results[core]["out_o"]
    return out


# version 32
# speedup vs baseline: 1.0506x; 1.0003x over previous
import contextlib
import math
import numpy as np
import ml_dtypes
import concourse.bass as bass
import concourse.mybir as mybir
from concourse.bass_utils import run_bass_kernel_spmd

F32 = mybir.dt.float32
BF16 = mybir.dt.bfloat16
AF = mybir.ActivationFunctionType
ALU = mybir.AluOpType
AX = mybir.AxisListType

P = 128
D = 1024
NB = 4
S = 8192
TOWN = 4096
H = 1152
OH = H + TOWN
NC = 8
C_A = float(np.float32(1.0986122886681098) / np.float32(32.0))
HID = 2730
W1C = 5460
INNER = 512
DHEAD = 64
NHEAD = 8
WIN = 512

N_PHASES = 5


_TBL = {}


def _tables():
    if _TBL:
        return _TBL
    import jax
    import jax.numpy as jnp
    cpu = jax.devices("cpu")[0]
    with jax.default_device(cpu):
        _tables_jax(jnp)
    return _TBL


def _tables_jax(jnp):
    import jax
    half = D // 2
    inv_freq = 1.0 / 10000.0 ** (jnp.arange(half, dtype=jnp.float32) / half)
    ang = jnp.arange(S, dtype=jnp.float32)[:, None] * inv_freq[None, :]
    cos = jnp.round(jnp.cos(ang))
    sin = jnp.round(jnp.sin(ang))
    C = jnp.concatenate([cos, cos], axis=-1)
    Sg = jnp.concatenate([sin, sin], axis=-1)
    Sf = np.asarray(Sg, np.float32).copy()
    Sf[:, :half] = -Sf[:, :half]
    rc = np.asarray(C, np.float32).T.astype(ml_dtypes.bfloat16)
    rs = Sf.T.astype(ml_dtypes.bfloat16)
    _TBL["rc"] = np.ascontiguousarray(rc)
    _TBL["rs"] = np.ascontiguousarray(rs)

    n = S
    half_idx = jnp.arange(0, DHEAD, 2, dtype=jnp.float32)
    inv_f = 1.0 / 10000.0 ** (half_idx / DHEAD)
    t_ext = jnp.arange(-H, n, dtype=jnp.float32)
    freqs = t_ext[:, None] * inv_f[None, :]
    freqs = jnp.concatenate([freqs, freqs], axis=-1)
    base = (half_idx + 0.4 * DHEAD) / (1.4 * DHEAD)
    power = (t_ext - n // 2) / float(WIN // 2)
    scale = base[None, :] ** power[:, None]
    scale = jnp.concatenate([scale, scale], axis=-1)
    cosx = jnp.cos(freqs)
    sinx = jnp.sin(freqs)
    qc = np.array(cosx * scale, np.float32)
    qs = np.array(sinx * scale, np.float32)
    kc = np.array(cosx / scale, np.float32)
    ks = np.array(sinx / scale, np.float32)
    for t in (qs, ks):
        t[:, :32] = -t[:, :32]
    def fm(t):
        tt = t.T
        return np.ascontiguousarray(np.concatenate([tt, tt], axis=0).astype(np.float32))
    _TBL["xqc"] = fm(qc)
    _TBL["xqs"] = fm(qs)
    _TBL["xkc"] = fm(kc)
    _TBL["xks"] = fm(ks)

    def band_mask(tbase):
        m = np.zeros((128, 640), np.float32)
        pp = np.arange(128)[:, None]
        cc = np.arange(640)[None, :]
        valid = (cc >= pp) & (cc <= pp + 512)
        if tbase < 512:
            valid &= (tbase - 512 + cc) >= 0
        m[~valid] = -1e30
        return m
    _TBL["mask_plain"] = band_mask(10_000)
    _TBL["mask_first"] = np.stack([band_mask(i * 128) for i in range(4)])

    pm = np.zeros((128, 128), np.float32)
    for kk in range(128):
        pm[kk, kk ^ 32] = 1.0
    _TBL["perm32"] = pm.astype(ml_dtypes.bfloat16)

    seg = np.zeros((128, 2), np.float32)
    seg[:64, 0] = 1.0
    seg[64:, 1] = 1.0
    _TBL["segbf"] = seg.astype(ml_dtypes.bfloat16)
    _TBL["segT"] = np.ascontiguousarray(seg.T)


class Ctr:
    def __init__(self):
        self.v = {}

    def add(self, sem, n):
        self.v[sem.name] = self.v.get(sem.name, 0) + n
        return self.v[sem.name]

    def cur(self, sem):
        return self.v.get(sem.name, 0)

    def snap(self):
        return dict(self.v)


class Env:
    pass


def _mk_env(nc, es):
    env = Env()
    env.nc = nc
    env.es = es
    sem = lambda n: es.enter_context(nc.semaphore(n))
    env.sD = sem("sD")
    env.sG = sem("sG")
    env.sT = sem("sT")
    env.sV = sem("sV")
    env.sA = sem("sA")
    env.c = Ctr()
    return env


def _fence(env, eng, sem, instr):
    v = env.c.add(sem, 1)
    instr.then_inc(sem, 1)
    eng.wait_ge(sem, v)


def _phase_barrier(env, blk, snap):
    for e in ("sync", "gpsimd", "tensor", "vector", "scalar"):
        rec = blk.recs[e]
        for s in (env.sD, env.sG, env.sT, env.sV, env.sA):
            v = snap.get(s.name, 0)
            if v:
                rec.calls.append((False, None, "wait_ge", (s, v), {}))


def _barrier_waits(env, eng, snap):
    for s in (env.sD, env.sG, env.sT, env.sV, env.sA):
        v = snap.get(s.name, 0)
        if v:
            eng.wait_ge(s, v)


class _ChainProxy:
    def __init__(self, rec, idx):
        self.rec = rec
        self.idx = idx

    def then_inc(self, sem, n):
        self.rec.calls.append((True, self.idx, "then_inc", (sem, n), {}))
        return self


class _Rec:
    def __init__(self):
        self.calls = []

    def __getattr__(self, name):
        rec = self

        def m(*a, **k):
            rec.calls.append((False, None, name, a, k))
            return _ChainProxy(rec, len(rec.calls) - 1)
        return m

    def replay(self, eng):
        res = []
        for chain, idx, name, a, k in self.calls:
            if chain:
                res.append(res[idx].then_inc(*a))
            else:
                res.append(getattr(eng, name)(*a, **k))


class DeferBlk:
    _ENGS = ("sync", "gpsimd", "tensor", "vector", "scalar")

    def __init__(self):
        self.recs = {e: _Rec() for e in self._ENGS}

    def _dec(self, e):
        def deco(f):
            f(self.recs[e])
        return deco

    @property
    def sync(self):
        return self._dec("sync")

    @property
    def gpsimd(self):
        return self._dec("gpsimd")

    @property
    def tensor(self):
        return self._dec("tensor")

    @property
    def vector(self):
        return self._dec("vector")

    @property
    def scalar(self):
        return self._dec("scalar")

    def flush(self, blk):
        for e in self._ENGS:
            rec = self.recs[e]
            if rec.calls:
                getattr(blk, e)(lambda eng, r=rec: r.replay(eng))


class _PhaseCtx:
    def __init__(self, env):
        self.env = env

    def __enter__(self):
        return self.env.dblk

    def __exit__(self, et, ev, tb):
        return False


class _DeferCtx:
    def __init__(self, nc, name, **kw):
        self.nc = nc
        self.name = name
        self.kw = kw
        self.d = DeferBlk()

    def __enter__(self):
        return self.d

    def __exit__(self, et, ev, tb):
        if et is None:
            with self.nc.Block(self.name, **self.kw) as b:
                self.d.flush(b)
        return False


def build_program(taps=()):
    nc = bass.Bass("TRN2", target_bir_lowering=False)
    es = contextlib.ExitStack()
    env = _mk_env(nc, es)
    env.dblk = DeferBlk()
    env.taps = set(taps)

    di = lambda n, s, d=F32: nc.dram_tensor(n, s, d, kind="ExternalInput").ap()
    I = {}
    I["x_oh"] = di("x_oh", [OH, D])
    I["x_oth"] = di("x_oth", [TOWN, D])
    I["gnorm"] = di("gnorm", [P, 8])
    I["wqkvA"] = di("wqkvA", [D, 3 * D])
    I["bqk_t"] = di("bqk_t", [P, 16])
    I["bqv_row"] = di("bqv_row", [1, D])
    I["wprojA"] = di("wprojA", [D, D])
    I["bprojA"] = di("bprojA", [P, 8])
    I["rc_own"] = di("rc_own", [D, TOWN], BF16)
    I["rs_own"] = di("rs_own", [D, TOWN], BF16)
    I["rc_oth"] = di("rc_oth", [D, TOWN], BF16)
    I["rs_oth"] = di("rs_oth", [D, TOWN], BF16)
    I["xqc"] = di("xqc", [P, OH])
    I["xqs"] = di("xqs", [P, OH])
    I["xkc"] = di("xkc", [P, OH])
    I["xks"] = di("xks", [P, OH])
    I["lmask"] = di("lmask", [5, P, 640])
    I["segbf"] = di("segbf", [P, 2], BF16)
    I["perm32"] = di("perm32", [P, P], BF16)
    I["segT"] = di("segT", [2, P])
    I["gate"] = di("gate", [P, 1])
    I["wqkv2"] = di("wqkv2", [2, D, 3 * INNER])
    I["qsc"] = di("qsc", [P, 2])
    I["ksc"] = di("ksc", [P, 2])
    I["wo2"] = di("wo2", [2, INNER, D])
    I["lng"] = di("lng", [P, 2, 8])
    I["lnb"] = di("lnb", [P, 2, 8])
    I["w12"] = di("w12", [2, D, W1C])
    I["b12"] = di("b12", [P, 2, 44])
    I["w22"] = di("w22", [2, HID, D])
    I["b22"] = di("b22", [P, 2, 8])
    env.I = I

    out_o = nc.dram_tensor("out_o", [TOWN, D], F32, kind="ExternalOutput").ap()

    def scratch(name, shape, dt):
        kind = "ExternalOutput" if name in env.taps else "Internal"
        return nc.dram_tensor(name, shape, dt, kind=kind).ap()

    Z = {}
    Z["xnF"] = scratch("xnF", [D, OH], F32)
    Z["yA"] = scratch("yA", [D, OH], F32)
    Z["yB"] = scratch("yB", [D, OH], F32)
    Z["qfA"] = scratch("qfA", [D, TOWN], BF16)
    Z["kfA"] = scratch("kfA", [D, S], BF16)
    Z["vtA"] = scratch("vtA", [S, D], BF16)
    Z["sF"] = scratch("sF", [D, TOWN], F32)
    Z["wqkvA_bf"] = scratch("wqkvA_bf", [D, 3 * D], BF16)
    Z["wprojA_bf"] = scratch("wprojA_bf", [D, D], BF16)
    Z["wqkv2_bf"] = scratch("wqkv2_bf", [2, D, 3 * INNER], BF16)
    Z["wo2_bf"] = scratch("wo2_bf", [2, INNER, D], BF16)
    Z["w12_bf"] = scratch("w12_bf", [2, D, W1C], BF16)
    Z["w22_bf"] = scratch("w22_bf", [2, HID, D], BF16)
    env.Z = Z
    env.out_o = out_o

    sb = lambda n, s, d: es.enter_context(nc.sbuf_tensor(n, s, d))
    pt = Env()
    pt.identb = sb("identb", [P, P], BF16)
    pt.ident3 = sb("ident3", [P, P], F32)
    pt.ones1r = sb("ones1r", [1, P], F32)
    pt.onesc = sb("onesc", [P, 1], F32)
    pt.onesb = sb("onesb", [P, 1], BF16)
    pt.gn = sb("gn", [P, 8], F32)
    pt.bqkv = sb("bqkv", [P, 16], F32)
    pt.bvb = sb("bvb", [P, D], F32)
    pt.bproj = sb("bproj", [P, 8], F32)
    pt.segb = sb("segb_sb", [P, 2], BF16)
    pt.perm32 = sb("perm32_sb", [P, P], BF16)
    pt.segT = sb("segT_sb", [2, P], F32)
    pt.gate = sb("gate_sb", [P, 1], F32)
    pt.qs8 = sb("qs8", [P, 2], F32)
    pt.ksc = sb("kscs", [P, 2], F32)
    pt.lng = sb("lngs", [P, 2, 8], F32)
    pt.lnb = sb("lnbs", [P, 2, 8], F32)
    pt.b2s = sb("b2s", [P, 2, 8], F32)
    pt.b1s = sb("b1s", [P, 2, 44], F32)
    pt.eps = sb("eps", [P, 1], F32)
    pt.eps6 = sb("eps6", [P, 1], F32)
    pt.eps24 = sb("eps24", [P, 1], F32)
    env.pt = pt

    _emit_init(env)
    with contextlib.ExitStack() as wces:
        _emit_wcast(env, wces)
        if N_PHASES >= 1:
            _emit_ph0(env)
    if N_PHASES >= 2:
        _emit_attnA(env)
    if N_PHASES >= 3:
        _emit_zeroinit(env)
        _emit_slA(env, 0)
        _emit_slF(env, 0)
        _emit_gate(env, "yB")
    if N_PHASES >= 4:
        _emit_slA(env, 1)
        _emit_gate(env, "yA")
        _emit_slF(env, 1)
    if N_PHASES >= 5:
        _emit_final(env)
    _emit_tail(env)

    with nc.Block("main") as _mb:
        env.dblk.flush(_mb)
    es.close()
    return nc


def _emit_init(env):
    nc, I, pt, c = env.nc, env.I, env.pt, env.c
    sD, sG = env.sD, env.sG

    with _PhaseCtx(env) as blk:
        @blk.gpsimd
        def _(g):
            last = g.memset(pt.identb[:], 0.0)
            _fence(env, g, sG, last)
            g.affine_select(out=pt.identb[:], in_=pt.identb[:], compare_op=ALU.not_equal,
                            fill=1.0, base=0, pattern=[[-1, P]],
                            channel_multiplier=1).then_inc(env.sG, 1)
            c.add(sG, 1)
            last = g.memset(pt.ident3[:], 0.0)
            _fence(env, g, sG, last)
            g.affine_select(out=pt.ident3[:], in_=pt.ident3[:], compare_op=ALU.not_equal,
                            fill=1.0, base=0, pattern=[[-1, P]],
                            channel_multiplier=1).then_inc(env.sG, 1)
            c.add(sG, 1)
            g.memset(pt.ones1r[:], 1.0).then_inc(env.sG, 1)
            g.memset(pt.onesc[:], 1.0).then_inc(env.sG, 1)
            g.memset(pt.onesb[:], 1.0).then_inc(env.sG, 1)
            g.memset(pt.eps[:], 1e-5).then_inc(env.sG, 1)
            g.memset(pt.eps6[:], 1e-6).then_inc(env.sG, 1)
            g.memset(pt.eps24[:], 1e-24).then_inc(env.sG, 1)
            c.add(sG, 6)

        @blk.sync
        def _(s):
            def ld(dst, src):
                s.dma_start(out=dst, in_=src).then_inc(sD, 16)
                c.add(sD, 16)
            ld(pt.gn[:], I["gnorm"][:])
            ld(pt.bqkv[:], I["bqk_t"][:])
            ld(pt.bproj[:], I["bprojA"][:])
            ld(pt.segb[:], I["segbf"][:])
            ld(pt.perm32[:], I["perm32"][:])
            ld(pt.segT[:], I["segT"][:])
            ld(pt.gate[:], I["gate"][:])
            ld(pt.qs8[:], I["qsc"][:])
            ld(pt.ksc[:], I["ksc"][:])
            ld(pt.lng[:], I["lng"][:])
            ld(pt.lnb[:], I["lnb"][:])
            ld(pt.b2s[:], I["b22"][:])
            ld(pt.b1s[:], I["b12"][:])

    snap0 = c.snap()

    with (nc.psum_tensor("ps_init", [P, 512], F32) as ps,
          nc.sbuf_tensor("bvrow", [1, D], F32) as bvrow,
          _PhaseCtx(env) as blk):
        @blk.sync
        def _(s):
            _barrier_waits(env, s, snap0)
            s.dma_start(out=bvrow[:], in_=I["bqv_row"][:]).then_inc(sD, 16)
            c.add(sD, 16)

        vready = c.cur(env.sD)

        @blk.tensor
        def _(t):
            t.wait_ge(sD, vready)
            t.matmul(ps[:, 0:512], env.pt.ones1r[0:1, :], bvrow[0:1, 0:512],
                     start=True, stop=True).then_inc(env.sT, 1)
            c.add(env.sT, 1)

        @blk.vector
        def _(v):
            v.wait_ge(env.sT, c.cur(env.sT))
            v.tensor_copy(pt.bvb[:, 0:512], ps[:, 0:512]).then_inc(env.sV, 1)
            c.add(env.sV, 1)

        t2 = c.cur(env.sV)

        @blk.tensor
        def _(t):
            t.wait_ge(env.sV, t2)
            t.matmul(ps[:, 0:512], env.pt.ones1r[0:1, :], bvrow[0:1, 512:1024],
                     start=True, stop=True).then_inc(env.sT, 1)
            c.add(env.sT, 1)

        @blk.vector
        def _(v):
            v.wait_ge(env.sT, c.cur(env.sT))
            v.tensor_copy(pt.bvb[:, 512:1024], ps[:, 0:512]).then_inc(env.sV, 1)
            c.add(env.sV, 1)
            v.wait_ge(env.sG, 1)
            v.tensor_scalar_mul(pt.qs8[:], pt.qs8[:], 8.0).then_inc(env.sV, 1)
            c.add(env.sV, 1)

    env.snap = c.snap()


def _emit_wcast(env, wces):
    nc, I, Z, c = env.nc, env.I, env.Z, env.c
    sG = env.sG

    jobs = []
    def add2d(srcap, dstap, rows, ncols):
        for r in range(0, rows, 128):
            rr = min(128, rows - r)
            for c0 in range(0, ncols, HID):
                cw = min(HID, ncols - c0)
                jobs.append((srcap[r:r + rr, c0:c0 + cw],
                             dstap[r:r + rr, c0:c0 + cw], rr, cw))
    add2d(I["wqkvA"], Z["wqkvA_bf"], D, 3 * D)
    add2d(I["wprojA"], Z["wprojA_bf"], D, D)
    for l in range(2):
        add2d(I["wqkv2"][l], Z["wqkv2_bf"][l], D, 3 * INNER)
        add2d(I["wo2"][l], Z["wo2_bf"][l], INNER, D)
        add2d(I["w12"][l], Z["w12_bf"][l], D, W1C)
        add2d(I["w22"][l], Z["w22_bf"][l], HID, D)
    wq_jobs = 16

    b32 = [wces.enter_context(nc.sbuf_tensor("wc_f0", [P, HID], F32))]
    b16 = [wces.enter_context(nc.sbuf_tensor("wc_b0", [P, HID], BF16))]

    snap = env.snap
    wq_mark = [0]

    with _PhaseCtx(env) as blk:
        @blk.gpsimd
        def _(g):
            _barrier_waits(env, g, snap)
            store_done = [0]
            for i, (sap, dap, rr, ncols) in enumerate(jobs):
                sl = 0
                if store_done[sl]:
                    g.wait_ge(sG, store_done[sl])
                g.dma_start(out=b32[sl][0:rr, 0:ncols], in_=sap).then_inc(sG, 16)
                ld = c.add(sG, 16)
                g.wait_ge(sG, ld)
                g.tensor_copy(b16[sl][0:rr, 0:ncols], b32[sl][0:rr, 0:ncols]).then_inc(sG, 1)
                cv = c.add(sG, 1)
                g.wait_ge(sG, cv)
                g.dma_start(out=dap, in_=b16[sl][0:rr, 0:ncols]).then_inc(sG, 16)
                store_done[sl] = c.add(sG, 16)
                if i == wq_jobs - 1:
                    wq_mark[0] = max(store_done)
    env.wq_ready = wq_mark[0]


def _emit_ph0(env):
    nc, I, Z, pt, c = env.nc, env.I, env.Z, env.pt, env.c
    sD, sG, sT, sV, sA = env.sD, env.sG, env.sT, env.sV, env.sA

    groups = []
    groups.append(("oh", 0, 512, "halo"))
    groups.append(("oh", 512, 512, "halo"))
    groups.append(("oh", 1024, 128, "halo"))
    for g in range(8):
        groups.append(("oh", 1152 + 512 * g, 512, "own"))
    for g in range(8):
        groups.append(("oth", 512 * g, 512, "oth"))

    with contextlib.ExitStack() as ph:
        sb = lambda n, s, d: ph.enter_context(nc.sbuf_tensor(n, s, d))
        wq_bf = sb("ph0_wq", [P, 8, 3 * D], BF16)
        xin = [sb(f"ph0_xin{i}", [P, 4, D], F32) for i in range(2)]
        stat = [sb(f"ph0_stat{i}", [P, 4 * 6], F32) for i in range(2)]
        xc = [sb(f"ph0_xc{i}", [P, 4, D], F32) for i in range(2)]
        ha_t = sb("ph0_hat", [P, 4, D], BF16)
        ha_f = sb("ph0_haf", [P, 8, 512], BF16)
        stg = sb("ph0_stg", [P, 512], F32)
        qtmp = sb("ph0_qtmp", [P, 8, 512], F32)
        rstg = sb("ph0_rstg", [P, 512], BF16)
        rc_s = [sb("ph0_rc0", [P, 8, 512], BF16)]
        rs_s = [sb("ph0_rs0", [P, 8, 512], BF16)]
        vstg = sb("ph0_vstg", [P, 1024], BF16)
        rstg2 = sb("ph0_rstg2", [P, 512], F32)
        ps1 = ph.enter_context(nc.psum_tensor("ph0_ps1", [P, 512], F32))
        ps2 = ph.enter_context(nc.psum_tensor("ph0_ps2", [P, 512], F32))
        psb = ph.enter_context(nc.psum_tensor("ph0_psb", [P, 1024], BF16))
        psv = ph.enter_context(nc.psum_tensor("ph0_psv", [P, 1024], F32))

        with _PhaseCtx(env) as blk:
            snap = env.snap
            _phase_barrier(env, blk, snap)

            @blk.sync
            def _(s):
                _barrier_waits(env, s, snap)
                s.wait_ge(env.sG, env.wq_ready)
                s.dma_start(out=wq_bf[:], in_=Z["wqkvA_bf"].rearrange("(c p) n -> p c n", p=P)
                            ).then_inc(sD, 16)
                c.add(sD, 16)

            bufs = dict(wq_bf=wq_bf, xin=xin, stat=stat, xc=xc, ha_t=ha_t,
                        ha_f=ha_f, stg=stg, qtmp=qtmp, rstg=rstg, rc_s=rc_s,
                        rs_s=rs_s, vstg=vstg, rstg2=rstg2, ps1=ps1, ps2=ps2,
                        psb=psb, psv=psv)
            prev = {}
            pend = None
            for gi, (src, r0, nr, kind) in enumerate(groups):
                if pend is not None:
                    _ph0_back(env, blk, bufs, pend)
                fr = _ph0_front(env, blk, bufs, gi, src, r0, nr, kind, prev)
                prev = fr["prev2"]
                pend = fr
            if pend is not None:
                _ph0_back(env, blk, bufs, pend)

    env.snap = c.snap()


def _ph0_front(env, blk, bufs, gi, src, r0, nr, kind, prev):
    nc, I, Z, pt, c = env.nc, env.I, env.Z, env.pt, env.c
    sD, sG, sT, sV, sA = env.sD, env.sG, env.sT, env.sV, env.sA
    nt = nr // 128
    par = gi % 2
    xin, stat, xc = bufs["xin"][par], bufs["stat"][par], bufs["xc"][par]
    ha_t, stg, psv = bufs["ha_t"], bufs["stg"], bufs["psv"]
    xsrc = I["x_oh"] if src == "oh" else I["x_oth"]
    do_xn = kind in ("halo", "own")
    do_qkv = kind in ("own", "oth")
    own_q = kind == "own"
    xn_c0 = r0 if src == "oh" else None
    key_c0 = (r0 - H) if src == "oh" else (TOWN + r0)
    pf = prev.get(par, {})
    out = {}

    @blk.sync
    def _(s, pf=pf):
        if pf.get("xin_T"):
            s.wait_ge(sT, pf["xin_T"])
        if pf.get("xin_V"):
            s.wait_ge(sV, pf["xin_V"])
        s.dma_start(out=xin[:, 0:nt, :],
                    in_=xsrc[r0:r0 + nr, :].rearrange("(a p) f -> p a f", p=P)
                    ).then_inc(sD, 16)
        c.add(sD, 16)
    ld_done = c.cur(sD)

    @blk.vector
    def _(v, pf=pf):
        v.wait_ge(sD, ld_done)
        if pf.get("stat_A"):
            v.wait_ge(sA, pf["stat_A"])
        for a in range(nt):
            last = v.reduce_sum(stat[:, a * 6:a * 6 + 1], xin[:, a, :], axis=AX.X)
        _fence(env, v, sV, last)
        for a in range(nt):
            last = v.tensor_scalar_mul(stat[:, a * 6 + 1:a * 6 + 2], stat[:, a * 6:a * 6 + 1], 1.0 / D)
        _fence(env, v, sV, last)
        for a in range(nt):
            last = v.tensor_scalar(xc[:, a, :], xin[:, a, :], stat[:, a * 6 + 1:a * 6 + 2],
                                   None, op0=ALU.subtract)
        last.then_inc(sV, 1)
        c.add(sV, 1)
    xc_done = c.cur(sV)

    @blk.scalar
    def _(sc, tw=c.cur(sT)):
        sc.wait_ge(sV, xc_done)
        sc.wait_ge(sT, tw)
        for a in range(nt):
            last = sc.activation(out=ha_t[:, a, :], in_=xc[:, a, :], func=AF.Square)
        last.then_inc(sA, 1)
        c.add(sA, 1)
    sq_done = c.cur(sA)

    @blk.vector
    def _(v):
        v.wait_ge(sA, sq_done)
        for a in range(nt):
            last = v.reduce_sum(stat[:, a * 6 + 2:a * 6 + 3], ha_t[:, a, :], axis=AX.X)
        last.then_inc(sV, 1)
        c.add(sV, 1)
    ss_done = c.cur(sV)

    @blk.scalar
    def _(sc):
        sc.wait_ge(sV, ss_done)
        for a in range(nt):
            last = sc.activation(out=stat[:, a * 6 + 3:a * 6 + 4], in_=stat[:, a * 6 + 2:a * 6 + 3],
                                 func=AF.Ln, bias=pt.eps[:, 0:1], scale=1.0 / D)
        _fence(env, sc, sA, last)
        for a in range(nt):
            last = sc.activation(out=stat[:, a * 6 + 2:a * 6 + 3], in_=stat[:, a * 6 + 3:a * 6 + 4],
                                 func=AF.Exp, scale=-0.5)
        if do_qkv:
            for a in range(nt):
                last = sc.activation(out=stat[:, a * 6 + 4:a * 6 + 5], in_=stat[:, a * 6 + 3:a * 6 + 4],
                                     func=AF.Exp, scale=-1.0)
        last.then_inc(sA, 1)
        c.add(sA, 1)
    r1_done = c.cur(sA)

    if do_qkv:
        @blk.vector
        def _(v):
            v.wait_ge(sA, r1_done)
            for a in range(nt):
                last = v.tensor_scalar(stat[:, a * 6 + 4:a * 6 + 5], stat[:, a * 6 + 4:a * 6 + 5],
                                       -1e-5, 1.0, op0=ALU.mult, op1=ALU.add)
            last.then_inc(sV, 1)
            c.add(sV, 1)
        ms_done = c.cur(sV)

        @blk.scalar
        def _(sc):
            sc.wait_ge(sV, ms_done)
            for a in range(nt):
                last = sc.activation(out=stat[:, a * 6 + 5:a * 6 + 6], in_=stat[:, a * 6 + 4:a * 6 + 5],
                                     func=AF.Ln, bias=pt.eps6[:, 0:1])
            last.then_inc(sA, 1)
            c.add(sA, 1)
        w_done = c.cur(sA)

        @blk.vector
        def _(v):
            v.wait_ge(sA, w_done)
            for a in range(nt):
                last = v.tensor_add(stat[:, a * 6 + 5:a * 6 + 6], stat[:, a * 6 + 5:a * 6 + 6],
                                    stat[:, a * 6 + 3:a * 6 + 4])
            last.then_inc(sV, 1)
            c.add(sV, 1)
        uw_done = c.cur(sV)

        @blk.scalar
        def _(sc):
            sc.wait_ge(sV, uw_done)
            for a in range(nt):
                last = sc.activation(out=stat[:, a * 6 + 5:a * 6 + 6], in_=stat[:, a * 6 + 5:a * 6 + 6],
                                     func=AF.Exp, scale=-0.5)
            last.then_inc(sA, 1)
            c.add(sA, 1)

    stats_done = c.cur(sA)
    out["stat_A"] = stats_done

    @blk.vector
    def _(v):
        v.wait_ge(sA, stats_done)
        last = None
        if do_qkv:
            for a in range(nt):
                last = v.tensor_scalar(ha_t[:, a, :], xc[:, a, :], stat[:, a * 6 + 5:a * 6 + 6],
                                       None, op0=ALU.mult)
        if do_xn:
            for a in range(nt):
                last = v.tensor_scalar(xin[:, a, :], xc[:, a, :], stat[:, a * 6 + 2:a * 6 + 3],
                                       None, op0=ALU.mult)
        last.then_inc(sV, 1)
        c.add(sV, 1)
    norm_done = c.cur(sV)

    if do_xn:
        for cc in range(8):
            tw = c.cur(sV)
            aw3 = c.cur(sA)

            @blk.tensor
            def _(t, cc=cc, tw=tw, aw3=aw3):
                t.wait_ge(sV, tw)
                t.wait_ge(sA, aw3)
                for a in range(nt):
                    last = t.transpose(psv[:, a * 128:(a + 1) * 128],
                                       xin[:, a, cc * 128:(cc + 1) * 128], pt.ident3[:])
                last.then_inc(sT, 1)
                c.add(sT, 1)
            tp_done = c.cur(sT)
            dprev = c.cur(sD)

            @blk.scalar
            def _(sc, cc=cc, tp_done=tp_done, dprev=dprev):
                sc.wait_ge(sT, tp_done)
                sc.wait_ge(sD, dprev)
                sc.activation(out=stg[:, 0:nr], in_=psv[:, 0:nr],
                              func=AF.Copy).then_inc(sA, 1)
                c.add(sA, 1)
            ev_done = c.cur(sA)

            @blk.sync
            def _(s, cc=cc, ev_done=ev_done):
                s.wait_ge(sA, ev_done)
                s.dma_start(out=Z["xnF"][cc * 128:(cc + 1) * 128, xn_c0:xn_c0 + nr],
                            in_=stg[:, 0:nr]).then_inc(sD, 16)
                c.add(sD, 16)
    out["xin_T"] = c.cur(sT)
    out["xin_V"] = norm_done

    prev2 = dict(prev)
    prev2[par] = out
    return dict(prev2=prev2, gi=gi, src=src, r0=r0, nr=nr, kind=kind, par=par,
                out=out, norm_done=norm_done, key_c0=key_c0)


def _ph0_back(env, blk, bufs, fr):
    nc, I, Z, pt, c = env.nc, env.I, env.Z, env.pt, env.c
    sD, sG, sT, sV, sA = env.sD, env.sG, env.sT, env.sV, env.sA
    kind, r0, nr, par = fr["kind"], fr["r0"], fr["nr"], fr["par"]
    do_qkv = kind in ("own", "oth")
    if not do_qkv:
        return
    own_q = kind == "own"
    key_c0 = fr["key_c0"]
    nt = nr // 128
    norm_done = fr["norm_done"]
    wq_bf, ha_t, ha_f = bufs["wq_bf"], bufs["ha_t"], bufs["ha_f"]
    qtmp, rstg, rstg2 = bufs["qtmp"], bufs["rstg"], bufs["rstg2"]
    rc_s, rs_s = bufs["rc_s"][0], bufs["rs_s"][0]
    vstg, ps1, psb, psv = bufs["vstg"], bufs["ps1"], bufs["psb"], bufs["psv"]

    @blk.sync
    def _(s, vw0=c.cur(sV)):
        s.wait_ge(sV, vw0)
        rc = I["rc_own"] if own_q else I["rc_oth"]
        rs = I["rs_own"] if own_q else I["rs_oth"]
        col = (r0 - H) if own_q else r0
        s.dma_start(out=rc_s[:],
                    in_=rc[:, col:col + 512].rearrange("(c p) n -> p c n", p=P)
                    ).then_inc(sD, 16)
        c.add(sD, 16)
        s.dma_start(out=rs_s[:],
                    in_=rs[:, col:col + 512].rearrange("(c p) n -> p c n", p=P)
                    ).then_inc(sD, 16)
        c.add(sD, 16)
    tbl_ok = c.cur(sD)

    for cc in range(8):
        tw = c.cur(sV)
        aw2 = c.cur(sA)

        @blk.tensor
        def _(t, cc=cc, tw=tw, aw2=aw2):
            t.wait_ge(sV, max(tw, norm_done))
            t.wait_ge(sA, aw2)
            for a in range(nt):
                last = t.transpose(psb[:, a * 128:(a + 1) * 128],
                                   ha_t[:, a, cc * 128:(cc + 1) * 128], pt.identb[:])
            last.then_inc(sT, 1)
            c.add(sT, 1)
        tp2 = c.cur(sT)

        @blk.scalar
        def _(sc, cc=cc, tp2=tp2):
            sc.wait_ge(sT, tp2)
            sc.activation(out=ha_f[:, cc, 0:nr], in_=psb[:, 0:nr], func=AF.Copy,
                          scale=pt.gn[:, cc:cc + 1]).then_inc(sA, 1)
            c.add(sA, 1)
    haf_done = c.cur(sA)

    col_sets = []
    if own_q:
        col_sets.append(("q", 0))
    col_sets.append(("k", 8))
    for nmk, cbase in col_sets:
        for oc in range(8):
            aw = c.cur(sA)
            vw = c.cur(sV)

            @blk.tensor
            def _(t, oc=oc, cbase=cbase, aw=aw):
                t.wait_ge(sA, max(haf_done, aw))
                for f in range(8):
                    last = t.matmul(ps1[:, 0:nr],
                                    wq_bf[:, f, (cbase + oc) * 128:(cbase + oc + 1) * 128],
                                    ha_f[:, f, 0:nr], start=(f == 0), stop=(f == 7))
                last.then_inc(sT, 1)
                c.add(sT, 1)
            mm_done = c.cur(sT)

            @blk.scalar
            def _(sc, oc=oc, cbase=cbase, mm_done=mm_done, vw=vw):
                sc.wait_ge(sT, mm_done)
                sc.wait_ge(sV, vw)
                sc.activation(out=qtmp[:, oc, 0:nr], in_=ps1[:, 0:nr], func=AF.Identity,
                              bias=pt.bqkv[:, cbase + oc:cbase + oc + 1]).then_inc(sA, 1)
                c.add(sA, 1)
        qk_done = c.cur(sA)

        for oc in range(8):
            dw = c.cur(sD)

            @blk.vector
            def _(v, oc=oc, dw=dw, qk_done=qk_done):
                v.wait_ge(sA, qk_done)
                v.wait_ge(sD, max(dw, tbl_ok))
                v.tensor_tensor(rstg2[:, 0:nr], qtmp[:, oc, 0:nr], rc_s[:, oc, 0:nr],
                                op=ALU.mult)
                last = v.tensor_tensor(bufs["xc"][par][:, 0, 0:nr], qtmp[:, oc ^ 4, 0:nr],
                                       rs_s[:, oc, 0:nr], op=ALU.mult)
                _fence(env, v, sV, last)
                v.tensor_tensor(rstg[:, 0:nr], rstg2[:, 0:nr],
                                bufs["xc"][par][:, 0, 0:nr], op=ALU.add).then_inc(sV, 1)
                c.add(sV, 1)
            rp_done = c.cur(sV)

            @blk.sync
            def _(s, oc=oc, nmk=nmk, rp_done=rp_done):
                s.wait_ge(sV, rp_done)
                dst = Z["qfA"] if nmk == "q" else Z["kfA"]
                col0 = (r0 - H) if nmk == "q" else key_c0
                s.dma_start(out=dst[oc * 128:(oc + 1) * 128, col0:col0 + nr],
                            in_=rstg[:, 0:nr]).then_inc(sD, 16)
                c.add(sD, 16)

    for a in range(nt):
        vw = c.cur(sV)
        aw4 = c.cur(sA)

        @blk.tensor
        def _(t, a=a, vw=vw, aw4=aw4):
            t.wait_ge(sA, max(haf_done, aw4))
            t.wait_ge(sV, vw)
            for f in range(8):
                for vh in range(2):
                    last = t.matmul(psv[:, vh * 512:(vh + 1) * 512],
                                    ha_f[:, f, a * 128:(a + 1) * 128],
                                    wq_bf[:, f, 2048 + vh * 512:2048 + (vh + 1) * 512],
                                    start=(f == 0), stop=(f == 7))
            last.then_inc(sT, 1)
            c.add(sT, 1)
        vm_done = c.cur(sT)
        dw = c.cur(sD)

        @blk.vector
        def _(v, a=a, vm_done=vm_done, dw=dw):
            v.wait_ge(sT, vm_done)
            v.wait_ge(sD, dw)
            v.tensor_tensor(vstg[:, 0:1024], psv[:, 0:1024], pt.bvb[:, 0:1024],
                            op=ALU.add).then_inc(sV, 1)
            c.add(sV, 1)
        ve_done = c.cur(sV)

        @blk.sync
        def _(s, a=a, ve_done=ve_done):
            s.wait_ge(sV, ve_done)
            s.dma_start(out=Z["vtA"][key_c0 + a * 128:key_c0 + (a + 1) * 128, :],
                        in_=vstg[:, 0:1024]).then_inc(sD, 16)
            c.add(sD, 16)


def _emit_attnA(env):
    nc, I, Z, pt, c = env.nc, env.I, env.Z, env.pt, env.c
    sD, sG, sT, sV, sA = env.sD, env.sG, env.sT, env.sV, env.sA

    with contextlib.ExitStack() as ph:
        sb = lambda n, s, d: ph.enter_context(nc.sbuf_tensor(n, s, d))
        QSB = sb("aA_q", [P, 8, 1024], BF16)
        KSB = [sb(f"aA_k{i}", [P, 8, 1024], BF16) for i in range(2)]
        VSB = [sb(f"aA_v{i}", [P, 8, 1024], BF16) for i in range(2)]
        OACC = sb("aA_oacc", [P, 8, 1024], F32)
        PSB = [sb(f"aA_p{i}", [P, 1024], BF16) for i in range(2)]
        PTS = [sb(f"aA_pt{i}", [P, 1024], BF16) for i in range(2)]
        OFSB = sb("aA_of", [P, 8, 1024], BF16)
        WP = sb("aA_wp", [P, 8, 1024], BF16)
        mcur = sb("aA_m", [P, 8], F32)
        lcur = sb("aA_l", [P, 8], F32)
        mx = [sb(f"aA_mx{i}", [P, 1], F32) for i in range(2)]
        mnew = [sb(f"aA_mn{i}", [P, 1], F32) for i in range(2)]
        dd = [sb(f"aA_dd{i}", [P, 1], F32) for i in range(2)]
        af = [sb(f"aA_af{i}", [P, 1], F32) for i in range(2)]
        bias = [sb(f"aA_bias{i}", [P, 1], F32) for i in range(2)]
        lch = [sb(f"aA_lch{i}", [P, 1], F32) for i in range(2)]
        rl = sb("aA_rl", [P, 1], F32)
        obf = sb("aA_obf", [P, 1024], BF16)
        stg = sb("aA_stg", [P, 512], F32)
        xns = sb("aA_xns", [P, 512], F32)
        psS = [ph.enter_context(nc.psum_tensor(f"aA_psS{i}", [P, 1024], F32))
               for i in range(2)]
        psT = ph.enter_context(nc.psum_tensor("aA_psT", [P, 1024], BF16))
        psO = ph.enter_context(nc.psum_tensor("aA_psO", [P, 1024], F32))
        ps1 = ph.enter_context(nc.psum_tensor("aA_ps1", [P, 512], F32))

        with _PhaseCtx(env) as blk:
            snap = env.snap
            _phase_barrier(env, blk, snap)

            @blk.sync
            def _(s):
                _barrier_waits(env, s, snap)
                s.dma_start(out=WP[:], in_=Z["wprojA_bf"].rearrange("(c p) n -> p c n", p=P)
                            ).then_inc(sD, 16)
                c.add(sD, 16)

            for qpass in range(4):
                _attnA_pass(env, blk, qpass, QSB, KSB, VSB, OACC, PSB, PTS, OFSB,
                            WP, mcur, lcur, mx, mnew, dd, af, bias, lch, rl, obf,
                            stg, xns, psS, psT, psO, ps1)

    env.snap = c.snap()


def _attnA_pass(env, blk, qpass, QSB, KSB, VSB, OACC, PSB, PTS, OFSB, WP,
                mcur, lcur, mx, mnew, dd, af, bias, lch, rl, obf, stg, xns,
                psS, psT, psO, ps1):
    nc, I, Z, pt, c = env.nc, env.I, env.Z, env.pt, env.c
    sD, sG, sT, sV, sA = env.sD, env.sG, env.sT, env.sV, env.sA
    q0 = qpass * 1024

    exp_done = [0, 0]
    trans_done = [0, 0]
    pv_done = [0, 0]
    mch_done = [0, 0]
    lo_done = [0, 0]

    tprev, vprev, aprev = c.cur(sT), c.cur(sV), c.cur(sA)

    @blk.sync
    def _(s):
        s.wait_ge(sT, tprev)
        s.wait_ge(sV, vprev)
        s.wait_ge(sA, aprev)
        for f in range(8):
            s.dma_start(out=QSB[:, f, :],
                        in_=Z["qfA"][f * 128:(f + 1) * 128, q0:q0 + 1024]).then_inc(sD, 16)
            c.add(sD, 16)

    kv_users = [tprev, tprev]
    kv_ok = [0] * 9

    def load_kv(kc):
        sl = kc % 2

        @blk.sync
        def _(s, kc=kc, sl=sl, tw=kv_users[sl]):
            s.wait_ge(sT, tw)
            for f in range(8):
                s.dma_start(out=KSB[sl][:, f, :],
                            in_=Z["kfA"][f * 128:(f + 1) * 128, kc * 1024:(kc + 1) * 1024]
                            ).then_inc(sD, 16)
                c.add(sD, 16)
            for u in range(8):
                s.dma_start(out=VSB[sl][:, u, :],
                            in_=Z["vtA"][kc * 1024 + u * 128:kc * 1024 + (u + 1) * 128, :]
                            ).then_inc(sD, 16)
                c.add(sD, 16)
        kv_ok[kc] = c.cur(sD)

    load_kv(0)
    for kc in range(8):
        sl = kc % 2
        if kc + 1 < 8:
            load_kv(kc + 1)

        def emit_S(qt):
            par = qt % 2

            @blk.tensor
            def _(t, qt=qt, par=par, sl=sl, ko=kv_ok[kc], ew=exp_done[par]):
                t.wait_ge(sD, ko)
                if ew:
                    t.wait_ge(sA, ew)
                for f in range(8):
                    for kk in range(2):
                        last = t.matmul(psS[par][:, kk * 512:(kk + 1) * 512],
                                        QSB[:, f, qt * 128:(qt + 1) * 128],
                                        KSB[sl][:, f, kk * 512:(kk + 1) * 512],
                                        start=(f == 0), stop=(f == 7))
                last.then_inc(sT, 1)
                c.add(sT, 1)
            s_done = c.cur(sT)

            @blk.vector
            def _(v, qt=qt, kc=kc, par=par, s_done=s_done, aw2=exp_done[par]):
                v.wait_ge(sT, s_done)
                if aw2:
                    v.wait_ge(sA, aw2)
                last = v.reduce_max(mx[par][:], psS[par][:, 0:1024], axis=AX.X)
                _fence(env, v, sV, last)
                if kc == 0:
                    v.tensor_copy(mcur[:, qt:qt + 1], mx[par][:])
                    v.tensor_scalar_mul(bias[par][:], mx[par][:], -C_A).then_inc(sV, 1)
                else:
                    last = v.tensor_max(mnew[par][:], mcur[:, qt:qt + 1], mx[par][:])
                    _fence(env, v, sV, last)
                    v.tensor_scalar_mul(bias[par][:], mnew[par][:], -C_A)
                    v.tensor_tensor(dd[par][:], mcur[:, qt:qt + 1], mnew[par][:],
                                    op=ALU.subtract)
                    v.tensor_copy(mcur[:, qt:qt + 1], mnew[par][:]).then_inc(sV, 1)
                c.add(sV, 1)
            mch_done[par] = c.cur(sV)
            m_done = mch_done[par]

            @blk.scalar
            def _(sc, qt=qt, kc=kc, par=par, m_done=m_done, tw=trans_done[par],
                  lw=lo_done[par]):
                sc.wait_ge(sV, m_done)
                if tw:
                    sc.wait_ge(sT, tw)
                if lw:
                    sc.wait_ge(sV, lw)
                last = sc.activation(out=PSB[par][:], in_=psS[par][:, 0:1024],
                                     func=AF.Exp, bias=bias[par][:, 0:1], scale=C_A,
                                     accum_out=lch[par][:, 0:1])
                if kc > 0:
                    last = sc.activation(out=af[par][:], in_=dd[par][:, 0:1],
                                         func=AF.Exp, scale=C_A)
                last.then_inc(sA, 1)
                c.add(sA, 1)
            exp_done[par] = c.cur(sA)

        def emit_tail(qt):
            par = qt % 2
            e_done = exp_done[par]

            @blk.tensor
            def _(t, qt=qt, par=par, e_done=e_done, vw=c.cur(sV)):
                t.wait_ge(sA, e_done)
                t.wait_ge(sV, vw)
                for f in range(8):
                    last = t.transpose(psT[:, f * 128:(f + 1) * 128],
                                       PSB[par][:, f * 128:(f + 1) * 128], pt.identb[:])
                last.then_inc(sT, 1)
                c.add(sT, 1)
            trans_done[par] = c.cur(sT)
            tr_done = trans_done[par]

            @blk.vector
            def _(v, par=par, tr_done=tr_done, pw=pv_done[par]):
                v.wait_ge(sT, tr_done)
                if pw:
                    v.wait_ge(sT, pw)
                v.tensor_copy(PTS[par][:], psT[:, 0:1024]).then_inc(sV, 1)
                c.add(sV, 1)
            pt_ok = c.cur(sV)

            @blk.tensor
            def _(t, par=par, sl=sl, pt_ok=pt_ok):
                t.wait_ge(sV, pt_ok)
                for f in range(8):
                    for vh in range(2):
                        last = t.matmul(psO[:, vh * 512:(vh + 1) * 512],
                                        PTS[par][:, f * 128:(f + 1) * 128],
                                        VSB[sl][:, f, vh * 512:(vh + 1) * 512],
                                        start=(f == 0), stop=(f == 7))
                last.then_inc(sT, 1)
                c.add(sT, 1)
            pv_done[par] = c.cur(sT)
            pv_ok = pv_done[par]

            @blk.vector
            def _(v, qt=qt, kc=kc, par=par, pv_ok=pv_ok):
                v.wait_ge(sT, pv_ok)
                if kc == 0:
                    v.tensor_copy(lcur[:, qt:qt + 1], lch[par][:])
                    v.tensor_copy(OACC[:, qt, :], psO[:, 0:1024]).then_inc(sV, 1)
                else:
                    v.scalar_tensor_tensor(out=lcur[:, qt:qt + 1], in0=lcur[:, qt:qt + 1],
                                           scalar=af[par][:, 0:1], in1=lch[par][:],
                                           op0=ALU.mult, op1=ALU.add)
                    v.scalar_tensor_tensor(out=OACC[:, qt, :], in0=OACC[:, qt, :],
                                           scalar=af[par][:, 0:1], in1=psO[:, 0:1024],
                                           op0=ALU.mult, op1=ALU.add).then_inc(sV, 1)
                c.add(sV, 1)
            lo_done[par] = c.cur(sV)

        for qt in range(8):
            emit_S(qt)
            if qt >= 1:
                emit_tail(qt - 1)
        emit_tail(7)
        kv_users[sl] = c.cur(sT)

    for qt in range(8):
        @blk.vector
        def _(v, qt=qt, tw=c.cur(sT)):
            v.wait_ge(sT, tw)
            last = v.reciprocal(rl[:], lcur[:, qt:qt + 1])
            _fence(env, v, sV, last)
            v.tensor_scalar(obf[:], OACC[:, qt, :], rl[:, 0:1], None,
                            op0=ALU.mult).then_inc(sV, 1)
            c.add(sV, 1)
        n_done = c.cur(sV)

        @blk.tensor
        def _(t, qt=qt, n_done=n_done):
            t.wait_ge(sV, n_done)
            for cc in range(8):
                last = t.transpose(psT[:, cc * 128:(cc + 1) * 128],
                                   obf[:, cc * 128:(cc + 1) * 128], pt.identb[:])
            last.then_inc(sT, 1)
            c.add(sT, 1)
        ot_done = c.cur(sT)

        @blk.vector
        def _(v, qt=qt, ot_done=ot_done):
            v.wait_ge(sT, ot_done)
            for cc in range(8):
                last = v.tensor_copy(OFSB[:, cc, qt * 128:(qt + 1) * 128],
                                     psT[:, cc * 128:(cc + 1) * 128])
            last.then_inc(sV, 1)
            c.add(sV, 1)

    of_done = c.cur(sV)
    gw = c.cur(sG)

    for pc in range(8):
        for ntile in range(2):
            aw = c.cur(sA)

            @blk.sync
            def _(s, pc=pc, ntile=ntile, dw=c.cur(sD), vw2=c.cur(sV)):
                s.wait_ge(sD, dw)
                s.wait_ge(sV, vw2)
                s.dma_start(out=xns[:],
                            in_=Z["xnF"][pc * 128:(pc + 1) * 128,
                                         H + q0 + ntile * 512:H + q0 + (ntile + 1) * 512]
                            ).then_inc(sD, 16)
                c.add(sD, 16)
            xn_ok = c.cur(sD)

            @blk.tensor
            def _(t, pc=pc, ntile=ntile, of_done=of_done, aw=aw):
                t.wait_ge(sV, of_done)
                t.wait_ge(sA, aw)
                for f in range(8):
                    last = t.matmul(ps1[:, 0:512],
                                    WP[:, f, pc * 128:(pc + 1) * 128],
                                    OFSB[:, f, ntile * 512:(ntile + 1) * 512],
                                    start=(f == 0), stop=(f == 7))
                last.then_inc(sT, 1)
                c.add(sT, 1)
            pj_done = c.cur(sT)

            @blk.scalar
            def _(sc, pc=pc, pj_done=pj_done, dw2=c.cur(sD)):
                sc.wait_ge(sT, pj_done)
                sc.wait_ge(sD, dw2)
                sc.activation(out=stg[:], in_=ps1[:, 0:512], func=AF.Identity,
                              bias=pt.bproj[:, pc:pc + 1]).then_inc(sA, 1)
                c.add(sA, 1)
            ev_done = c.cur(sA)

            @blk.vector
            def _(v, xn_ok=xn_ok, ev_done=ev_done):
                v.wait_ge(sD, xn_ok)
                v.wait_ge(sA, ev_done)
                v.tensor_add(stg[:], stg[:], xns[:]).then_inc(sV, 1)
                c.add(sV, 1)
            add_done = c.cur(sV)

            @blk.sync
            def _(s, pc=pc, ntile=ntile, add_done=add_done):
                s.wait_ge(sV, add_done)
                s.dma_start(out=Z["sF"][pc * 128:(pc + 1) * 128,
                                        q0 + ntile * 512:q0 + (ntile + 1) * 512],
                            in_=stg[:]).then_inc(sD, 16)
                c.add(sD, 16)


def _slA_groups():
    gs = [(0, 512), (512, 512), (1024, 128)]
    for g in range(8):
        gs.append((1152 + 512 * g, 512))
    return gs


def _emit_zeroinit(env):
    nc, Z, c = env.nc, env.Z, env.c
    sD, sV = env.sD, env.sV
    with contextlib.ExitStack() as ph:
        zt = ph.enter_context(nc.sbuf_tensor("zi_z", [P, 512], F32))
        with _PhaseCtx(env) as blk:
            snap = env.snap
            _phase_barrier(env, blk, snap)

            @blk.vector
            def _(v):
                _barrier_waits(env, v, snap)
                v.memset(zt[:], 0.0).then_inc(sV, 1)
                c.add(sV, 1)
            zok = c.cur(sV)

            @blk.sync
            def _(s):
                s.wait_ge(sV, zok)
                for buf in ("yA", "yB"):
                    for cc in range(8):
                        s.dma_start(out=env.Z[buf][cc * 128:(cc + 1) * 128, 0:512],
                                    in_=zt[:]).then_inc(sD, 16)
                        c.add(sD, 16)
    env.snap = c.snap()


def _emit_gate(env, bufname):
    nc, Z, pt, c = env.nc, env.Z, env.pt, env.c
    sD, sV = env.sD, env.sV
    with contextlib.ExitStack() as ph:
        colt = ph.enter_context(nc.sbuf_tensor("gt_col_" + bufname, [P, 8], F32))
        with _PhaseCtx(env) as blk:
            snap = env.snap
            _phase_barrier(env, blk, snap)

            @blk.sync
            def _(s):
                _barrier_waits(env, s, snap)
                s.dma_start(out=colt[:], in_=Z[bufname][:, H - 1:H].rearrange("(c p) one -> p c one", p=P),
                            allow_slow_non_contiguous=True).then_inc(sD, 16)
                c.add(sD, 16)
            lok = c.cur(sD)

            @blk.vector
            def _(v):
                v.wait_ge(sD, lok)
                v.tensor_scalar(colt[:], colt[:], pt.gate[:, 0:1], None,
                                op0=ALU.mult).then_inc(sV, 1)
                c.add(sV, 1)
            gok = c.cur(sV)

            @blk.sync
            def _(s):
                s.wait_ge(sV, gok)
                s.dma_start(out=Z[bufname][:, H - 1:H].rearrange("(c p) one -> p c one", p=P),
                            in_=colt[:], allow_slow_non_contiguous=True).then_inc(sD, 16)
                c.add(sD, 16)
    env.snap = c.snap()


def _emit_slA(env, l):
    nc, I, Z, pt, c = env.nc, env.I, env.Z, env.pt, env.c
    sD, sG, sT, sV, sA = env.sD, env.sG, env.sT, env.sV, env.sA
    ysrc = Z["xnF"] if l == 0 else Z["yB"]
    qstart = 512 if l == 0 else 1024

    with contextlib.ExitStack() as ph:
        sb = lambda n, s, d: ph.enter_context(nc.sbuf_tensor(f"l{l}{n}", s, d))
        WQ2 = sb("sa_wq", [P, 8, 1536], BF16)
        lmk = sb("sa_lmk", [P, 5, 640], F32)
        WO2 = sb("sa_wo", [P, 4, 1024], BF16)
        KF2 = sb("sa_k", [P, 4, OH], BF16)
        VT2 = sb("sa_v", [P, 41, 512], BF16)
        QF2 = sb("sa_q", [P, 4, 512], BF16)
        ysh = sb("sa_ysh", [P, 8, 512], BF16)
        yf32 = sb("sa_yf32", [P, 8, 513], F32)
        qraw = sb("sa_qraw", [P, 4, 512], F32)
        qnrm = sb("sa_qnrm", [P, 512], BF16)
        sqb = sb("sa_sqb", [P, 512], BF16)
        rr = sb("sa_rr", [P, 512], F32)
        xc1 = sb("sa_xc1", [P, 512], F32)
        xc2 = sb("sa_xc2", [P, 512], F32)
        xc3 = sb("sa_xc3", [P, 512], F32)
        xc4 = sb("sa_xc4", [P, 512], F32)
        Ssb0 = sb("sa_S0", [P, 2, 640], F32)
        Ssb = [Ssb0, Ssb0]
        gx1 = Ssb0[:, 0, 0:512]
        gx2 = Ssb0[:, 1, 0:512]
        Pb0 = sb("sa_P0", [P, 2, 640], BF16)
        Pb = [Pb0, Pb0]
        PTS = [sb(f"sa_PT{i}", [P, 2, 640], BF16) for i in range(2)]
        mxp = [sb(f"sa_mx{i}", [P, 2], F32) for i in range(2)]
        nm2 = [sb(f"sa_nm{i}", [P, 2], F32) for i in range(2)]
        ll2 = [sb(f"sa_ll{i}", [P, 2], F32) for i in range(2)]
        rli2 = [sb(f"sa_rl{i}", [P, 2], F32) for i in range(2)]
        o_loc = sb("sa_ol", [P, 4, 512], BF16)
        oF = sb("sa_of", [P, 4, 512], BF16)
        ystg = sb("sa_ystg", [P, 512], F32)
        PS_BIG = ph.enter_context(nc.psum_tensor(f"l{l}sa_big", [P, 2048], F32))
        PS_T2 = ph.enter_context(nc.psum_tensor(f"l{l}sa_t2", [P, 2048], BF16))
        PS_O = [ph.enter_context(nc.psum_tensor(f"l{l}sa_o{i}", [P, 512], F32))
                for i in range(2)]

        with _PhaseCtx(env) as blk:
            snap = env.snap
            _phase_barrier(env, blk, snap)

            @blk.sync
            def _(s):
                _barrier_waits(env, s, snap)
                s.dma_start(out=WQ2[:], in_=Z["wqkv2_bf"][l].rearrange("(c p) n -> p c n", p=P)
                            ).then_inc(sD, 16)
                c.add(sD, 16)
                s.dma_start(out=WO2[:], in_=Z["wo2_bf"][l].rearrange("(c p) n -> p c n", p=P)
                            ).then_inc(sD, 16)
                c.add(sD, 16)
                s.dma_start(out=lmk[:], in_=I["lmask"].rearrange("m p n -> p m n")
                            ).then_inc(sD, 16)
                c.add(sD, 16)

            buf = Env()
            buf.WQ2, buf.WO2, buf.KF2, buf.VT2, buf.QF2 = WQ2, WO2, KF2, VT2, QF2
            buf.ysh, buf.yf32, buf.qraw, buf.qnrm = ysh, yf32, qraw, qnrm
            buf.gx1, buf.gx2, buf.sqb, buf.rr = gx1, gx2, sqb, rr
            buf.xc = (xc1, xc2, xc3, xc4)
            buf.Ssb, buf.Pb, buf.PTS = Ssb, Pb, PTS
            buf.mxp, buf.nm2, buf.ll2, buf.rli2 = mxp, nm2, ll2, rli2
            buf.o_loc, buf.oF, buf.ystg, buf.lmk = o_loc, oF, ystg, lmk
            buf.PS_BIG, buf.PS_T2, buf.PS_O = PS_BIG, PS_T2, PS_O

            for (r0, nr) in _slA_groups():
                _slA_group(env, blk, l, ysrc, qstart, r0, nr, buf)

    env.snap = c.snap()


def _slA_group(env, blk, l, ysrc, qstart, r0, nr, buf):
    nc, I, Z, pt, c = env.nc, env.I, env.Z, env.pt, env.c
    sD, sG, sT, sV, sA = env.sD, env.sG, env.sT, env.sV, env.sA
    nt = nr // 128
    do_attn = (r0 + nr) > qstart
    B = buf

    tprev, vprev, aprev = c.cur(sT), c.cur(sV), c.cur(sA)

    @blk.sync
    def _(s):
        s.wait_ge(sT, tprev)
        s.wait_ge(sV, vprev)
        s.wait_ge(sA, aprev)
        if r0 == 0:
            s.dma_start(out=B.yf32[:, :, 1:513],
                        in_=ysrc[:, 0:512].rearrange("(c p) n -> p c n", p=P)
                        ).then_inc(sD, 16)
            c.add(sD, 16)
        else:
            s.dma_start(out=B.yf32[:, :, 0:nr + 1],
                        in_=ysrc[:, r0 - 1:r0 + nr].rearrange("(c p) n -> p c n", p=P)
                        ).then_inc(sD, 16)
            c.add(sD, 16)
        for name, bufx in (("xqc", B.xc[0]), ("xqs", B.xc[1]),
                           ("xkc", B.xc[2]), ("xks", B.xc[3])):
            s.dma_start(out=bufx[:, 0:nr], in_=I[name][:, r0:r0 + nr]).then_inc(sD, 16)
            c.add(sD, 16)
    yld = c.cur(sD)

    @blk.vector
    def _(v, yld=yld):
        v.wait_ge(sD, yld)
        if r0 == 0:
            last = v.memset(B.yf32[:, 4:8, 0:1], 0.0)
            _fence(env, v, sV, last)
        v.tensor_copy(B.ysh[:, 0:4, 0:nr], B.yf32[:, 0:4, 1:nr + 1])
        v.tensor_copy(B.ysh[:, 4:8, 0:nr], B.yf32[:, 4:8, 0:nr]).then_inc(sV, 1)
        c.add(sV, 1)
    ysh_ok = c.cur(sV)
    ld_ok = (c.cur(sD), c.cur(sG))

    for cc in range(8):
        aw = c.cur(sA)
        par = cc % 2
        psq = B.PS_BIG[:, par * 512:(par + 1) * 512]

        @blk.tensor
        def _(t, cc=cc, aw=aw, psq=psq):
            t.wait_ge(sV, ysh_ok)
            t.wait_ge(sA, aw)
            for f in range(8):
                last = t.matmul(psq[:, 0:nr], B.WQ2[:, f, cc * 128:(cc + 1) * 128],
                                B.ysh[:, f, 0:nr], start=(f == 0), stop=(f == 7))
            last.then_inc(sT, 1)
            c.add(sT, 1)
        mm_ok = c.cur(sT)

        @blk.scalar
        def _(sc, cc=cc, mm_ok=mm_ok, vw=c.cur(sV), psq=psq):
            sc.wait_ge(sT, mm_ok)
            sc.wait_ge(sV, vw)
            sc.activation(out=B.qraw[:, cc % 4, 0:nr], in_=psq[:, 0:nr],
                          func=AF.Identity).then_inc(sA, 1)
            c.add(sA, 1)

        if cc % 4 == 3:
            isq = cc < 4
            for u in range(4):
                _slA_norm_chunk(env, blk, l, isq, u, r0, nr, B, ld_ok)

    for a in range(nt):
        vw = c.cur(sV)

        @blk.tensor
        def _(t, a=a, vw=vw):
            t.wait_ge(sV, max(vw, ysh_ok))
            for f in range(8):
                last = t.matmul(B.PS_O[0][:, 0:512], B.ysh[:, f, a * 128:(a + 1) * 128],
                                B.WQ2[:, f, 1024:1536], start=(f == 0), stop=(f == 7))
            last.then_inc(sT, 1)
            c.add(sT, 1)
        vm_ok = c.cur(sT)

        @blk.vector
        def _(v, a=a, vm_ok=vm_ok):
            v.wait_ge(sT, vm_ok)
            v.tensor_copy(B.VT2[:, (r0 // 128) + a, :], B.PS_O[0][:, 0:512]).then_inc(sV, 1)
            c.add(sV, 1)

    if not do_attn:
        return

    ttr_done = [0, 0]
    exp_done = [0, 0]
    trans_done = [0, 0]
    evac_done = [0, 0]
    pv_done = [0, 0]
    o_done = [0, 0]

    pairs = []
    for a in range(nt):
        r_tile = r0 + a * 128
        if r_tile + 128 <= qstart:
            continue
        if r_tile >= H:
            j = (r_tile - H) // 128
            midx = j if j < 4 else 4
        else:
            midx = 4
        for hp2 in range(4):
            pairs.append((a, hp2, r_tile, midx))

    qkv_done = (c.cur(sV), c.cur(sA), c.cur(sT))

    def emit_S(i):
        a, hp2, r_tile, midx = pairs[i]
        par = i % 2
        k0 = r_tile - 512

        @blk.tensor
        def _(t, a=a, hp2=hp2, k0=k0, tw=max(ttr_done)):
            if i == 0:
                t.wait_ge(sV, qkv_done[0])
                t.wait_ge(sA, qkv_done[1])
            if tw:
                t.wait_ge(sV, tw)
            qA = B.QF2[0:64, hp2, a * 128:(a + 1) * 128]
            qB = B.QF2[64:128, hp2, a * 128:(a + 1) * 128]
            kA = B.KF2[0:64, hp2, :]
            kB = B.KF2[64:128, hp2, :]
            t.matmul(B.PS_BIG[:, 0:512], qA, kA[:, k0:k0 + 512], start=True, stop=True)
            t.matmul(B.PS_BIG[:, 512:640], qA, kA[:, k0 + 512:k0 + 640], start=True, stop=True)
            t.matmul(B.PS_BIG[:, 1024:1536], qB, kB[:, k0:k0 + 512], start=True, stop=True)
            t.matmul(B.PS_BIG[:, 1536:1664], qB, kB[:, k0 + 512:k0 + 640],
                     start=True, stop=True).then_inc(sT, 1)
            c.add(sT, 1)
        s_ok = c.cur(sT)

        @blk.vector
        def _(v, midx=midx, par=par, s_ok=s_ok, aw=c.cur(sA)):
            v.wait_ge(sT, s_ok)
            if aw:
                v.wait_ge(sA, aw)
            pview = B.PS_BIG.rearrange("p (h n) -> p h n", n=1024)
            last = v.tensor_tensor(B.Ssb[par][:, :, :], pview[:, :, 0:640],
                                   B.lmk[:, midx, :].unsqueeze(1).broadcast_to([P, 2, 640]),
                                   op=ALU.add)
            _fence(env, v, sV, last)
            v.reduce_max(B.nm2[par][:, 0:2], B.Ssb[par][:, :, :], axis=AX.X,
                         negate=True).then_inc(sV, 1)
            c.add(sV, 1)
        ttr_done[par] = c.cur(sV)

    def emit_exp(i):
        par = i % 2
        m_ok = ttr_done[par]

        @blk.scalar
        def _(sc, par=par, m_ok=m_ok, tw=max(trans_done), ow=o_done[par]):
            sc.wait_ge(sV, m_ok)
            if tw:
                sc.wait_ge(sT, tw)
            if ow:
                sc.wait_ge(sV, ow)
            sc.activation(out=B.Pb[par][:, 0, :], in_=B.Ssb[par][:, 0, :], func=AF.Exp,
                          bias=B.nm2[par][:, 0:1], accum_out=B.ll2[par][:, 0:1])
            sc.activation(out=B.Pb[par][:, 1, :], in_=B.Ssb[par][:, 1, :], func=AF.Exp,
                          bias=B.nm2[par][:, 1:2], accum_out=B.ll2[par][:, 1:2]
                          ).then_inc(sA, 1)
            c.add(sA, 1)
        exp_done[par] = c.cur(sA)

    def emit_tail(i):
        a, hp2, r_tile, midx = pairs[i]
        par = i % 2
        e_ok = exp_done[par]

        @blk.tensor
        def _(t, par=par, e_ok=e_ok, vw=c.cur(sV)):
            t.wait_ge(sA, e_ok)
            t.wait_ge(sV, vw)
            for i5 in range(5):
                t.transpose(B.PS_T2[:, i5 * 128:i5 * 128 + 128],
                            B.Pb[par][:, 0, i5 * 128:(i5 + 1) * 128], pt.identb[:])
            for i5 in range(5):
                last = t.transpose(B.PS_T2[:, 1024 + i5 * 128:1024 + i5 * 128 + 128],
                                   B.Pb[par][:, 1, i5 * 128:(i5 + 1) * 128], pt.identb[:])
            last.then_inc(sT, 1)
            c.add(sT, 1)
        trans_done[par] = c.cur(sT)
        tr_ok = trans_done[par]

        @blk.vector
        def _(v, par=par, tr_ok=tr_ok, pw=pv_done[par]):
            v.wait_ge(sT, tr_ok)
            if pw:
                v.wait_ge(sT, pw)
            tv = B.PS_T2.rearrange("p (h n) -> p h n", n=1024)
            v.tensor_copy(B.PTS[par][:, 0, :], tv[:, 0, 0:640])
            last = v.tensor_copy(B.PTS[par][:, 1, :], tv[:, 1, 0:640])
            _fence(env, v, sV, last)
            last = v.reciprocal(rli_ap(B, par), B.ll2[par][:])
            last.then_inc(sV, 1)
            c.add(sV, 1)
        pt_ok = c.cur(sV)

        @blk.tensor
        def _(t, a=a, hp2=hp2, r_tile=r_tile, par=par, pt_ok=pt_ok):
            t.wait_ge(sV, pt_ok)
            sub0 = (r_tile - 512) // 128
            for h2 in range(2):
                h = hp2 * 2 + h2
                for i5 in range(5):
                    last = t.matmul(B.PS_O[h2][:, 0:64],
                                    B.PTS[par][:, h2, i5 * 128:(i5 + 1) * 128],
                                    B.VT2[:, sub0 + i5, h * 64:(h + 1) * 64],
                                    start=(i5 == 0), stop=(i5 == 4))
            last.then_inc(sT, 1)
            c.add(sT, 1)
        pv_done[par] = c.cur(sT)
        pv_ok = pv_done[par]

        @blk.vector
        def _(v, a=a, hp2=hp2, par=par, pv_ok=pv_ok):
            v.wait_ge(sT, pv_ok)
            h0 = hp2 * 2
            v.tensor_scalar(B.o_loc[:, a, h0 * 64:(h0 + 1) * 64], B.PS_O[0][:, 0:64],
                            B.rli2[par][:, 0:1], None, op0=ALU.mult)
            v.tensor_scalar(B.o_loc[:, a, (h0 + 1) * 64:(h0 + 2) * 64], B.PS_O[1][:, 0:64],
                            B.rli2[par][:, 1:2], None, op0=ALU.mult).then_inc(sV, 1)
            c.add(sV, 1)
        o_done[par] = c.cur(sV)

    for i in range(len(pairs)):
        emit_S(i)
        if i >= 1:
            emit_tail(i - 1)
        emit_exp(i)
    if pairs:
        emit_tail(len(pairs) - 1)

    a0 = max(0, (qstart - r0) // 128)
    for a in range(a0, nt):
        ow = c.cur(sV)

        @blk.tensor
        def _(t, a=a, ow=ow):
            t.wait_ge(sV, ow)
            for oc4 in range(4):
                last = t.transpose(B.PS_T2[:, oc4 * 128:(oc4 + 1) * 128],
                                   B.o_loc[:, a, oc4 * 128:(oc4 + 1) * 128], pt.identb[:])
            last.then_inc(sT, 1)
            c.add(sT, 1)
        ot_ok = c.cur(sT)

        @blk.vector
        def _(v, a=a, ot_ok=ot_ok):
            v.wait_ge(sT, ot_ok)
            for oc4 in range(4):
                last = v.tensor_copy(B.oF[:, oc4, a * 128:(a + 1) * 128],
                                     B.PS_T2[:, oc4 * 128:(oc4 + 1) * 128])
            last.then_inc(sV, 1)
            c.add(sV, 1)
    of_ok = c.cur(sV)

    wbase = qstart - r0 if qstart > r0 else 0
    wn = nr - wbase
    if wn <= 0:
        return
    for oc in range(8):
        par = oc % 2
        psw = B.PS_BIG[:, par * 512:(par + 1) * 512]

        @blk.tensor
        def _(t, oc=oc, psw=psw, of_ok=of_ok, vw4=c.cur(sV)):
            t.wait_ge(sV, of_ok)
            t.wait_ge(sV, vw4)
            for kk in range(4):
                last = t.matmul(psw[:, 0:wn], B.WO2[:, kk, oc * 128:(oc + 1) * 128],
                                B.oF[:, kk, wbase:nr], start=(kk == 0), stop=(kk == 3))
            last.then_inc(sT, 1)
            c.add(sT, 1)
        wm_ok = c.cur(sT)

        @blk.vector
        def _(v, oc=oc, psw=psw, wm_ok=wm_ok, dw2=c.cur(sD)):
            v.wait_ge(sT, wm_ok)
            v.wait_ge(sD, dw2)
            v.tensor_add(B.ystg[:, 0:wn], psw[:, 0:wn],
                         B.yf32[:, oc, 1 + wbase:1 + nr]).then_inc(sV, 1)
            c.add(sV, 1)
        ad_ok = c.cur(sV)

        @blk.sync
        def _(s, oc=oc, ad_ok=ad_ok):
            s.wait_ge(sV, ad_ok)
            s.dma_start(out=Z["yA"][oc * 128:(oc + 1) * 128, r0 + wbase:r0 + nr],
                        in_=B.ystg[:, 0:wn]).then_inc(sD, 16)
            c.add(sD, 16)


def rli_ap(B, par):
    return B.rli2[par][:]


def _slA_norm_chunk(env, blk, l, isq, u, r0, nr, B, ld_ok):
    nc, I, Z, pt, c = env.nc, env.I, env.Z, env.pt, env.c
    sD, sG, sT, sV, sA = env.sD, env.sG, env.sT, env.sV, env.sA
    tct = B.xc[0] if isq else B.xc[2]
    tst = B.xc[1] if isq else B.xc[3]
    psSS = B.PS_BIG[:, 1024:1536]
    psBC = B.PS_BIG[:, 1536:2048]
    psROT = B.PS_O[1]

    aw0 = c.cur(sA)

    @blk.scalar
    def _(sc, u=u, aw0=aw0, tw0=c.cur(sT)):
        sc.wait_ge(sA, aw0)
        sc.wait_ge(sT, tw0)
        sc.activation(out=B.sqb[:, 0:nr], in_=B.qraw[:, u, 0:nr],
                      func=AF.Square).then_inc(sA, 1)
        c.add(sA, 1)
    sq_ok = c.cur(sA)

    @blk.tensor
    def _(t, u=u, sq_ok=sq_ok, aw1=c.cur(sA)):
        t.wait_ge(sA, max(sq_ok, aw1))
        t.matmul(psSS[0:2, 0:nr], pt.segb[:, 0:2], B.sqb[:, 0:nr],
                 start=True, stop=True).then_inc(sT, 1)
        c.add(sT, 1)
    ssm_ok = c.cur(sT)

    @blk.scalar
    def _(sc, ssm_ok=ssm_ok, tw1=c.cur(sT)):
        sc.wait_ge(sT, max(ssm_ok, tw1))
        last = sc.activation(out=B.rr[0:2, 0:nr], in_=psSS[0:2, 0:nr], func=AF.Ln,
                             bias=pt.eps24[0:2, 0:1])
        _fence(env, sc, sA, last)
        sc.activation(out=B.rr[0:2, 0:nr], in_=B.rr[0:2, 0:nr], func=AF.Exp,
                      scale=-0.5).then_inc(sA, 1)
        c.add(sA, 1)
    rr_ok = c.cur(sA)

    @blk.tensor
    def _(t, rr_ok=rr_ok, vw2=c.cur(sV)):
        t.wait_ge(sA, rr_ok)
        t.wait_ge(sV, vw2)
        t.matmul(psBC[:, 0:nr], pt.segT[0:2, :], B.rr[0:2, 0:nr],
                 start=True, stop=True).then_inc(sT, 1)
        c.add(sT, 1)
    bc_ok = c.cur(sT)

    @blk.vector
    def _(v, u=u, isq=isq, bc_ok=bc_ok, ld2=ld_ok[0], tw2=c.cur(sT)):
        v.wait_ge(sT, bc_ok)
        v.wait_ge(sD, ld2)
        sc_ap = pt.qs8[:, l:l + 1] if isq else pt.ksc[:, l:l + 1]
        v.scalar_tensor_tensor(out=B.qnrm[:, 0:nr], in0=B.qraw[:, u, 0:nr],
                               scalar=sc_ap, in1=psBC[:, 0:nr],
                               op0=ALU.mult, op1=ALU.mult).then_inc(sV, 1)
        c.add(sV, 1)
    qn_ok = c.cur(sV)

    @blk.tensor
    def _(t, qn_ok=qn_ok):
        t.wait_ge(sV, qn_ok)
        t.matmul(psROT[:, 0:nr], pt.perm32[:], B.qnrm[:, 0:nr],
                 start=True, stop=True).then_inc(sT, 1)
        c.add(sT, 1)
    rot_ok = c.cur(sT)

    @blk.vector
    def _(v, u=u, isq=isq, r0=r0, rot_ok=rot_ok, tct=tct, tst=tst):
        v.wait_ge(sT, rot_ok)
        v.tensor_tensor(B.gx1[:, 0:nr], B.qnrm[:, 0:nr], tct[:, 0:nr], op=ALU.mult)
        last = v.tensor_tensor(B.gx2[:, 0:nr], psROT[:, 0:nr], tst[:, 0:nr], op=ALU.mult)
        _fence(env, v, sV, last)
        dst = B.QF2[:, u, 0:nr] if isq else B.KF2[:, u, r0:r0 + nr]
        v.tensor_add(dst, B.gx1[:, 0:nr], B.gx2[:, 0:nr]).then_inc(sV, 1)
        c.add(sV, 1)


def _emit_slF(env, l):
    nc, I, Z, pt, c = env.nc, env.I, env.Z, env.pt, env.c
    sD, sG, sT, sV, sA = env.sD, env.sG, env.sT, env.sV, env.sA

    if l == 0:
        groups = [(512, 512), (1024, 128)] + [(1152 + 512 * g, 512) for g in range(8)]
    else:
        groups = [(1152 + 512 * g, 512) for g in range(8)]

    with contextlib.ExitStack() as ph:
        sb = lambda n, s, d: ph.enter_context(nc.sbuf_tensor(f"l{l}{n}", s, d))
        W1 = sb("sf_w1", [P, 8, W1C], BF16)
        W2 = sb("sf_w2", [P, 22, D], BF16)
        ysh = sb("sf_ysh", [P, 8, 512], F32)
        sq8 = sb("sf_sq8", [P, 8, 512], BF16)
        yn2 = sb("sf_yn2", [P, 8, 512], BF16)
        tshb = sb("sf_tsh", [P, 22, 512], BF16)
        gelb = [sb(f"sf_gel{i}", [P, 512], BF16) for i in range(2)]
        yinc = [sb(f"sf_yin{i}", [P, 512], F32) for i in range(2)]
        ystg = [sb(f"sf_ystg{i}", [P, 512], F32) for i in range(2)]
        murow = sb("sf_mu", [1, 512], F32)
        v1row = sb("sf_v1", [1, 512], F32)
        varrow = sb("sf_var", [1, 512], F32)
        psS1 = ph.enter_context(nc.psum_tensor(f"l{l}sf_psS1", [P, 512], F32))
        psS2 = ph.enter_context(nc.psum_tensor(f"l{l}sf_psS2", [P, 512], F32))
        psBC = ph.enter_context(nc.psum_tensor(f"l{l}sf_psBC", [P, 1024], F32))
        psA1 = [ph.enter_context(nc.psum_tensor(f"l{l}sf_psA{i}", [P, 512], F32))
                for i in range(2)]
        psG = [ph.enter_context(nc.psum_tensor(f"l{l}sf_psG{i}", [P, 512], F32))
               for i in range(2)]

        with _PhaseCtx(env) as blk:
            snap = env.snap
            _phase_barrier(env, blk, snap)

            @blk.sync
            def _(s):
                _barrier_waits(env, s, snap)
                s.dma_start(out=W1[:], in_=Z["w12_bf"][l].rearrange("(c p) n -> p c n", p=P)
                            ).then_inc(sD, 16)
                c.add(sD, 16)
                s.dma_start(out=W2[:, 0:21, :],
                            in_=Z["w22_bf"][l][0:2688].rearrange("(c p) n -> p c n", p=P)
                            ).then_inc(sD, 16)
                c.add(sD, 16)
                s.dma_start(out=W2[0:42, 21, :], in_=Z["w22_bf"][l][2688:2730, :]
                            ).then_inc(sD, 16)
                c.add(sD, 16)

            prev = None
            for (r0, nr) in groups:
                prev = _slF_group(env, blk, l, r0, nr, W1, W2, ysh, sq8, yn2, tshb,
                                  gelb, yinc, ystg, murow, v1row, varrow,
                                  psS1, psS2, psBC, psA1, psG, prev)

    env.snap = c.snap()


def _slF_group(env, blk, l, r0, nr, W1, W2, ysh, sq8, yn2, tshb, gelb, yinc,
               ystg, murow, v1row, varrow, psS1, psS2, psBC, psA1, psG, prev):
    nc, I, Z, pt, c = env.nc, env.I, env.Z, env.pt, env.c
    sD, sG, sT, sV, sA = env.sD, env.sG, env.sT, env.sV, env.sA

    if prev is None:
        tprev, vprev, aprev = c.cur(sT), c.cur(sV), c.cur(sA)
    else:
        tprev, vprev, aprev = prev["ysh_T"], prev["ysh_V"], prev["ysh_A"]

    @blk.sync
    def _(s):
        s.wait_ge(sT, tprev)
        s.wait_ge(sV, vprev)
        s.wait_ge(sA, aprev)
        s.dma_start(out=ysh[:, 0:4, 0:nr],
                    in_=Z["yA"][0:512, r0:r0 + nr].rearrange("(c p) n -> p c n", p=P)
                    ).then_inc(sD, 16)
        c.add(sD, 16)
        s.dma_start(out=ysh[:, 4:8, 0:nr],
                    in_=Z["yA"][512:1024, r0 - 1:r0 + nr - 1].rearrange("(c p) n -> p c n", p=P)
                    ).then_inc(sD, 16)
        c.add(sD, 16)
    ld_ok = c.cur(sD)

    @blk.scalar
    def _(sc, ld_ok=ld_ok, tw=(prev or {}).get("sq8_T", 0)):
        sc.wait_ge(sD, ld_ok)
        if tw:
            sc.wait_ge(sT, tw)
        for f in range(8):
            last = sc.activation(out=sq8[:, f, 0:nr], in_=ysh[:, f, 0:nr],
                                 func=AF.Square)
        last.then_inc(sA, 1)
        c.add(sA, 1)
    sq_ok = c.cur(sA)

    @blk.tensor
    def _(t, ld_ok=ld_ok, sq_ok=sq_ok, vw=(prev or {}).get("stats_V", 0)):
        t.wait_ge(sD, ld_ok)
        t.wait_ge(sA, sq_ok)
        if vw:
            t.wait_ge(sV, vw)
        for f in range(8):
            t.matmul(psS1[0:1, 0:nr], pt.onesc[:, 0:1], ysh[:, f, 0:nr],
                     start=(f == 0), stop=(f == 7))
        for f in range(8):
            last = t.matmul(psS2[0:1, 0:nr], pt.onesb[:, 0:1], sq8[:, f, 0:nr],
                            start=(f == 0), stop=(f == 7))
        last.then_inc(sT, 1)
        c.add(sT, 1)
    sums_ok = c.cur(sT)
    sq8_T = sums_ok

    @blk.vector
    def _(v, sums_ok=sums_ok, tw2=(prev or {}).get("rows_T", 0)):
        v.wait_ge(sT, sums_ok)
        if tw2:
            v.wait_ge(sT, tw2)
        v.tensor_scalar_mul(murow[0:1, 0:nr], psS1[0:1, 0:nr], 1.0 / D)
        last = v.tensor_scalar_mul(v1row[0:1, 0:nr], psS2[0:1, 0:nr], 1.0 / D)
        _fence(env, v, sV, last)
        last = v.tensor_tensor(varrow[0:1, 0:nr], murow[0:1, 0:nr], murow[0:1, 0:nr],
                               op=ALU.mult)
        _fence(env, v, sV, last)
        v.tensor_tensor(varrow[0:1, 0:nr], v1row[0:1, 0:nr], varrow[0:1, 0:nr],
                        op=ALU.subtract).then_inc(sV, 1)
        c.add(sV, 1)
    var_ok = c.cur(sV)
    stats_V = var_ok

    @blk.scalar
    def _(sc, var_ok=var_ok):
        sc.wait_ge(sV, var_ok)
        last = sc.activation(out=varrow[0:1, 0:nr], in_=varrow[0:1, 0:nr],
                             func=AF.Ln, bias=pt.eps[0:1, 0:1])
        _fence(env, sc, sA, last)
        sc.activation(out=varrow[0:1, 0:nr], in_=varrow[0:1, 0:nr],
                      func=AF.Exp, scale=-0.5).then_inc(sA, 1)
        c.add(sA, 1)
    rstd_ok = c.cur(sA)

    @blk.tensor
    def _(t, rstd_ok=rstd_ok, vw2=(prev or {}).get("bc_V", 0)):
        t.wait_ge(sA, rstd_ok)
        if vw2:
            t.wait_ge(sV, vw2)
        t.matmul(psBC[:, 0:nr], pt.ones1r[0:1, :], murow[0:1, 0:nr],
                 start=True, stop=True)
        t.matmul(psBC[:, 512:512 + nr], pt.ones1r[0:1, :], varrow[0:1, 0:nr],
                 start=True, stop=True).then_inc(sT, 1)
        c.add(sT, 1)
    bc_ok = c.cur(sT)
    rows_T = bc_ok

    @blk.vector
    def _(v, bc_ok=bc_ok, tw3=(prev or {}).get("yn2_T", 0)):
        v.wait_ge(sT, bc_ok)
        if tw3:
            v.wait_ge(sT, tw3)
        for f in range(8):
            last = v.tensor_tensor(ysh[:, f, 0:nr], ysh[:, f, 0:nr], psBC[:, 0:nr],
                                   op=ALU.subtract)
        _fence(env, v, sV, last)
        for f in range(8):
            last = v.scalar_tensor_tensor(out=yn2[:, f, 0:nr], in0=ysh[:, f, 0:nr],
                                          scalar=pt.lng[:, l, f:f + 1],
                                          in1=psBC[:, 512:512 + nr],
                                          op0=ALU.mult, op1=ALU.mult)
        _fence(env, v, sV, last)
        for f in range(8):
            last = v.tensor_scalar(yn2[:, f, 0:nr], yn2[:, f, 0:nr],
                                   pt.lnb[:, l, f:f + 1], None, op0=ALU.add)
        last.then_inc(sV, 1)
        c.add(sV, 1)
    yn_ok = c.cur(sV)
    bc_V = yn_ok
    ysh_V = yn_ok

    gel_done = [0, 0]
    gelu_done = [0, 0]
    for j in range(22):
        par = j % 2
        mj = 128 if j < 21 else 42
        ac0 = j * 128
        gc0 = HID + j * 128

        @blk.tensor
        def _(t, j=j, par=par, mj=mj, ac0=ac0, gc0=gc0, yn_ok=yn_ok,
              aw=gelu_done[par], vw3=gel_done[par]):
            t.wait_ge(sV, yn_ok)
            if aw:
                t.wait_ge(sA, aw)
            if vw3:
                t.wait_ge(sV, vw3)
            for f in range(8):
                t.matmul(psA1[par][0:mj, 0:nr], W1[:, f, ac0:ac0 + mj],
                         yn2[:, f, 0:nr], start=(f == 0), stop=(f == 7))
            for f in range(8):
                last = t.matmul(psG[par][0:mj, 0:nr], W1[:, f, gc0:gc0 + mj],
                                yn2[:, f, 0:nr], start=(f == 0), stop=(f == 7))
            last.then_inc(sT, 1)
            c.add(sT, 1)
        w1_ok = c.cur(sT)

        @blk.scalar
        def _(sc, j=j, par=par, mj=mj, w1_ok=w1_ok, vw4=gel_done[par]):
            sc.wait_ge(sT, w1_ok)
            if vw4:
                sc.wait_ge(sV, vw4)
            sc.activation(out=gelb[par][0:mj, 0:nr], in_=psG[par][0:mj, 0:nr],
                          func=AF.Gelu,
                          bias=pt.b1s[0:mj, l, 22 + j:23 + j]).then_inc(sA, 1)
            c.add(sA, 1)
        gelu_done[par] = c.cur(sA)
        gel_ok = gelu_done[par]

        @blk.vector
        def _(v, j=j, par=par, mj=mj, gel_ok=gel_ok):
            v.wait_ge(sA, gel_ok)
            v.scalar_tensor_tensor(out=tshb[0:mj, j, 0:nr], in0=psA1[par][0:mj, 0:nr],
                                   scalar=pt.b1s[0:mj, l, j:j + 1],
                                   in1=gelb[par][0:mj, 0:nr],
                                   op0=ALU.add, op1=ALU.mult).then_inc(sV, 1)
            c.add(sV, 1)
        gel_done[par] = c.cur(sV)
    t_ok = c.cur(sV)
    yn2_T = c.cur(sT)

    w2v = [gel_done[0], gel_done[1]]
    ystg_store = [0, 0]
    for oc in range(8):
        par = oc % 2

        @blk.sync
        def _(s, oc=oc, par=par, vw5=w2v[par]):
            if vw5:
                s.wait_ge(sV, vw5)
            s.dma_start(out=yinc[par][:, 0:nr],
                        in_=Z["yA"][oc * 128:(oc + 1) * 128, r0:r0 + nr]
                        ).then_inc(sD, 16)
            c.add(sD, 16)
        yi_ok = c.cur(sD)

        @blk.tensor
        def _(t, oc=oc, par=par, t_ok=t_ok, vw6=w2v[par]):
            t.wait_ge(sV, t_ok)
            if vw6:
                t.wait_ge(sV, vw6)
            for j in range(22):
                mj = 128 if j < 21 else 42
                last = t.matmul(psA1[par][:, 0:nr], W2[0:mj, j, oc * 128:(oc + 1) * 128],
                                tshb[0:mj, j, 0:nr], start=(j == 0), stop=(j == 21))
            last.then_inc(sT, 1)
            c.add(sT, 1)
        w2_ok = c.cur(sT)

        @blk.vector
        def _(v, oc=oc, par=par, w2_ok=w2_ok, yi_ok=yi_ok, sw=ystg_store[par]):
            v.wait_ge(sT, w2_ok)
            v.wait_ge(sD, max(yi_ok, sw))
            last = v.scalar_tensor_tensor(out=ystg[par][:, 0:nr], in0=psA1[par][:, 0:nr],
                                          scalar=pt.b2s[:, l, oc:oc + 1],
                                          in1=yinc[par][:, 0:nr],
                                          op0=ALU.add, op1=ALU.add)
            last.then_inc(sV, 1)
            c.add(sV, 1)
        w2v[par] = c.cur(sV)
        st_ok = w2v[par]

        @blk.sync
        def _(s, oc=oc, par=par, st_ok=st_ok):
            s.wait_ge(sV, st_ok)
            s.dma_start(out=Z["yB"][oc * 128:(oc + 1) * 128, r0:r0 + nr],
                        in_=ystg[par][:, 0:nr]).then_inc(sD, 16)
            c.add(sD, 16)
        ystg_store[par] = c.cur(sD)

    return {
        "ysh_T": sums_ok, "ysh_V": ysh_V, "ysh_A": sq_ok,
        "sq8_T": sq8_T, "stats_V": stats_V, "rows_T": rows_T,
        "bc_V": bc_V, "yn2_T": yn2_T,
    }


def _emit_final(env):
    nc, I, Z, pt, c = env.nc, env.I, env.Z, env.pt, env.c
    sD, sG, sT, sV, sA = env.sD, env.sG, env.sT, env.sV, env.sA

    with contextlib.ExitStack() as ph:
        sb = lambda n, s, d: ph.enter_context(nc.sbuf_tensor(n, s, d))
        yb = sb("fn_y", [P, 8, 512], F32)
        sfb = sb("fn_s", [P, 8, 512], F32)
        gch = sb("fn_g", [P, 8, 512], F32)
        xtk = sb("fn_x", [P, 4, 1024], F32)
        osb = sb("fn_o", [P, 1024], F32)
        psF = ph.enter_context(nc.psum_tensor("fn_psF", [P, 1024], F32))

        with _PhaseCtx(env) as blk:
            snap = env.snap
            _phase_barrier(env, blk, snap)
            first = [True]
            for g in range(8):
                c0 = H + g * 512
                tprev, vprev, aprev = c.cur(sT), c.cur(sV), c.cur(sA)

                @blk.sync
                def _(s, g=g, c0=c0, tprev=tprev, vprev=vprev, aprev=aprev,
                      fst=first[0]):
                    if fst:
                        _barrier_waits(env, s, snap)
                    s.wait_ge(sT, tprev)
                    s.wait_ge(sV, vprev)
                    s.wait_ge(sA, aprev)
                    s.dma_start(out=yb[:], in_=Z["yB"][:, c0:c0 + 512]
                                .rearrange("(c p) n -> p c n", p=P)).then_inc(sD, 16)
                    c.add(sD, 16)
                    s.dma_start(out=sfb[:], in_=Z["sF"][:, g * 512:(g + 1) * 512]
                                .rearrange("(c p) n -> p c n", p=P)).then_inc(sD, 16)
                    c.add(sD, 16)
                    s.dma_start(out=xtk[:], in_=I["x_oh"][c0:c0 + 512, :]
                                .rearrange("(a p) f -> p a f", p=P)).then_inc(sD, 16)
                    c.add(sD, 16)
                first[0] = False
                ld_ok = c.cur(sD)

                @blk.scalar
                def _(sc, ld_ok=ld_ok):
                    sc.wait_ge(sD, ld_ok)
                    for f in range(8):
                        last = sc.activation(out=gch[:, f, :], in_=yb[:, f, :],
                                             func=AF.Silu)
                    last.then_inc(sA, 1)
                    c.add(sA, 1)
                si_ok = c.cur(sA)

                @blk.vector
                def _(v, si_ok=si_ok):
                    v.wait_ge(sA, si_ok)
                    for f in range(8):
                        last = v.tensor_tensor(gch[:, f, :], gch[:, f, :],
                                               sfb[:, f, :], op=ALU.mult)
                    last.then_inc(sV, 1)
                    c.add(sV, 1)
                t_ok = c.cur(sV)

                for a in range(4):
                    vw = c.cur(sV)

                    @blk.tensor
                    def _(t, a=a, t_ok=t_ok, vw=vw):
                        t.wait_ge(sV, vw)
                        for f in range(8):
                            last = t.transpose(psF[:, f * 128:(f + 1) * 128],
                                               gch[:, f, a * 128:(a + 1) * 128],
                                               pt.ident3[:])
                        last.then_inc(sT, 1)
                        c.add(sT, 1)
                    tr_ok = c.cur(sT)
                    dw = c.cur(sD)

                    @blk.vector
                    def _(v, a=a, tr_ok=tr_ok, dw=dw):
                        v.wait_ge(sT, tr_ok)
                        v.wait_ge(sD, dw)
                        v.tensor_tensor(osb[:], psF[:, 0:1024], xtk[:, a, :],
                                        op=ALU.add).then_inc(sV, 1)
                        c.add(sV, 1)
                    ev_ok = c.cur(sV)

                    @blk.sync
                    def _(s, g=g, a=a, ev_ok=ev_ok):
                        s.wait_ge(sV, ev_ok)
                        s.dma_start(out=env.out_o[g * 512 + a * 128:
                                                  g * 512 + (a + 1) * 128, :],
                                    in_=osb[:]).then_inc(sD, 16)
                        c.add(sD, 16)

    env.snap = c.snap()


def _emit_tail(env):
    nc, c = env.nc, env.c
    with _PhaseCtx(env) as blk:
        @blk.sync
        def _(s):
            s.wait_ge(env.sD, c.cur(env.sD))
            s.wait_ge(env.sG, c.cur(env.sG))


def pk(vec, ncol):
    return np.ascontiguousarray(np.asarray(vec, np.float32).reshape(ncol, P).T)


def pk8(vec):
    return pk(vec, 8)


def pk2x8(arr):
    a = np.asarray(arr, np.float32).reshape(2, 8, P)
    return np.ascontiguousarray(a.transpose(2, 0, 1))


def pk_head(arr):
    a = np.asarray(arr, np.float32)
    return np.ascontiguousarray(np.concatenate([a, a], axis=1).T)


def pk_b1(arr):
    out = np.zeros((P, 2, 44), np.float32)
    a = np.asarray(arr, np.float32)
    for l in range(2):
        for half in range(2):
            seg = a[l, half * 2730:(half + 1) * 2730]
            full = seg[0:2688].reshape(21, P).T
            out[:, l, half * 22:half * 22 + 21] = full
            out[0:42, l, half * 22 + 21] = seg[2688:2730]
    return out


def _in_maps(inputs):
    t = _tables()
    x = np.ascontiguousarray(inputs["L0_attn_x"], dtype=np.float32)
    maps = []
    for core in range(NC):
        b, hf = core // 2, core % 2
        t0 = hf * TOWN
        xb = x[b]
        if hf == 0:
            x_oh = np.concatenate([np.zeros((H, D), np.float32), xb[0:TOWN]], axis=0)
            x_oth = xb[TOWN:]
            rc_own = t["rc"][:, 0:TOWN]
            rs_own = t["rs"][:, 0:TOWN]
            rc_oth = t["rc"][:, TOWN:]
            rs_oth = t["rs"][:, TOWN:]
            xslc = slice(H - H, H - H + OH)
            lm = np.concatenate([t["mask_first"], t["mask_plain"][None]], axis=0)
            gate = np.zeros((P, 1), np.float32)
        else:
            x_oh = xb[TOWN - H:]
            x_oth = xb[0:TOWN]
            rc_own = t["rc"][:, TOWN:]
            rs_own = t["rs"][:, TOWN:]
            rc_oth = t["rc"][:, 0:TOWN]
            rs_oth = t["rs"][:, 0:TOWN]
            xslc = slice(TOWN, TOWN + OH)
            lm = np.concatenate([np.broadcast_to(t["mask_plain"], (4, P, 640)),
                                 t["mask_plain"][None]], axis=0)
            gate = np.ones((P, 1), np.float32)
        xoff = t0
        m = {
            "x_oh": np.ascontiguousarray(x_oh),
            "x_oth": np.ascontiguousarray(x_oth),
            "gnorm": pk8(inputs["L0_attn_g_norm"]),
            "wqkvA": np.ascontiguousarray(inputs["L0_attn_w_qkv"], np.float32),
            "bqk_t": pk(np.asarray(inputs["L0_attn_b_qkv"], np.float32)[0:2048], 16),
            "bqv_row": np.ascontiguousarray(
                np.asarray(inputs["L0_attn_b_qkv"], np.float32)[None, 2048:3072]),
            "wprojA": np.ascontiguousarray(inputs["L0_attn_w_proj"], np.float32),
            "bprojA": pk8(inputs["L0_attn_b_proj"]),
            "rc_own": np.ascontiguousarray(rc_own),
            "rs_own": np.ascontiguousarray(rs_own),
            "rc_oth": np.ascontiguousarray(rc_oth),
            "rs_oth": np.ascontiguousarray(rs_oth),
            "xqc": np.ascontiguousarray(t["xqc"][:, xoff:xoff + OH]),
            "xqs": np.ascontiguousarray(t["xqs"][:, xoff:xoff + OH]),
            "xkc": np.ascontiguousarray(t["xkc"][:, xoff:xoff + OH]),
            "xks": np.ascontiguousarray(t["xks"][:, xoff:xoff + OH]),
            "lmask": np.ascontiguousarray(lm),
            "segbf": t["segbf"],
            "perm32": t["perm32"],
            "segT": t["segT"],
            "gate": gate,
            "wqkv2": np.ascontiguousarray(inputs["L0_ffn_wqkv"], np.float32),
            "qsc": pk_head(inputs["L0_ffn_q_scale"]),
            "ksc": pk_head(inputs["L0_ffn_k_scale"]),
            "wo2": np.ascontiguousarray(inputs["L0_ffn_wo"], np.float32),
            "lng": pk2x8(inputs["L0_ffn_ln_g"]),
            "lnb": pk2x8(inputs["L0_ffn_ln_b"]),
            "w12": np.ascontiguousarray(inputs["L0_ffn_w1"], np.float32),
            "b12": pk_b1(inputs["L0_ffn_b1"]),
            "w22": np.ascontiguousarray(inputs["L0_ffn_w2"], np.float32),
            "b22": pk2x8(inputs["L0_ffn_b2"]),
        }
        maps.append(m)
    return maps


def kernel(**inputs):
    maps = _in_maps(inputs)
    nc = build_program()
    res = run_bass_kernel_spmd(nc, maps, core_ids=list(range(NC)))
    out = np.empty((NB, S, D), np.float32)
    for core in range(NC):
        b, hf = core // 2, core % 2
        out[b, hf * TOWN:(hf + 1) * TOWN] = res.results[core]["out_o"]
    return out
